# version 38
# baseline (speedup 1.0000x reference)
import math
import contextlib
import numpy as np
import ml_dtypes

import concourse.bass as bass
import concourse.mybir as mybir
from concourse.bass_utils import run_bass_kernel_spmd
from concourse.bass_core_rust import IndirectOffsetOnAxis

F32 = mybir.dt.float32
BF16 = mybir.dt.bfloat16
I32 = mybir.dt.int32
AF = mybir.ActivationFunctionType
ALU = mybir.AluOpType
AXX = mybir.AxisListType.X

D = 1024
NH = 4
DK = 128
DV = 256
DKQ = 512
R = 16
IMLP = 2816
E = 8
HMOE = 4096
C = 128
S16 = 1.0 / 16.0
SCL = DK ** -0.5
EPS_LN = 1e-5
EPS_RMS = 1e-6
NCORE = 8

ENG = ("pe", "dve", "act", "pool", "sp")
QUEUE_ENG = {"spq": "sp", "actq": "act", "poolq": "pool"}
ENG_ATTR = {"pe": "tensor", "dve": "vector", "act": "scalar", "pool": "gpsimd", "sp": "sync"}


class KB:

    def __init__(self, nc, es):
        self.nc = nc
        self.es = es
        self.e = {k: getattr(nc, v) for k, v in ENG_ATTR.items()}
        self.sem = {}
        for name in ENG + ("cc",):
            self.sem[name] = es.enter_context(nc.semaphore(f"s_{name}"))
        self.cnt = {name: 0 for name in self.sem}
        self.w = {}
        self.r = {}
        self.waited = {en: {s: 0 for s in self.sem} for en in ENG}
        self.dma_sems = set()
        self.stream_eng = {}

    def stream(self, name, eng, npar=2):
        for i in range(npar):
            sn = f"{name}{i}"
            if sn in self.sem:
                continue
            self.sem[sn] = self.es.enter_context(self.nc.semaphore(f"s_{sn}"))
            self.cnt[sn] = 0
            self.dma_sems.add(sn)
            self.stream_eng[sn] = eng
            for en in ENG:
                self.waited[en][sn] = 0
        return lambda idx: f"{name}{idx % npar}"

    def _wait(self, eng, sem, val):
        if self.waited[eng][sem] >= val:
            return
        self.e[eng].wait_ge(self.sem[sem], val)
        self.waited[eng][sem] = val

    def _edge(self, eng, sem, val):
        if sem in self.dma_sems:
            val = self.cnt[sem]
        self._wait(eng, sem, val)

    def _pre(self, eng, reads, writes):
        for rg in reads:
            lw = self.w.get(rg)
            if lw is not None:
                self._edge(eng, *lw)
        for rg in writes:
            lw = self.w.get(rg)
            if lw is not None:
                self._edge(eng, *lw)
            for rr in self.r.get(rg, ()):
                self._edge(eng, *rr)

    def _post(self, sem, reads, writes):
        cn = self.cnt[sem]
        for rg in reads:
            self.r.setdefault(rg, []).append((sem, cn))
        for rg in writes:
            self.w[rg] = (sem, cn)
            self.r[rg] = []

    def op(self, eng, fn, reads=(), writes=()):
        self._pre(eng, reads, writes)
        inst = fn(self.e[eng])
        self.cnt[eng] += 1
        inst.then_inc(self.sem[eng], 1)
        self._post(eng, reads, writes)
        return inst

    def mmg(self, fn_list, reads=(), writes=(), first_start=True, last_stop=True):
        self._pre("pe", reads, writes)
        keys = [(ps.tensor.name, ps.offset) for ps, _, _ in fn_list]
        last = {k: i for i, k in enumerate(keys)}
        seen = set()
        for i, (ps, lh, rh) in enumerate(fn_list):
            k = keys[i]
            st = first_start and (k not in seen)
            seen.add(k)
            inst = self.e["pe"].matmul(
                ps, lh, rh, start=st, stop=(last_stop and last[k] == i))
        self.cnt["pe"] += 1
        inst.then_inc(self.sem["pe"], 1)
        self._post("pe", reads, writes)

    def transpose(self, out_ps, in_ap, ident_ap, reads=(), writes=()):
        self._pre("pe", reads, writes)
        inst = self.e["pe"].transpose(out_ps, in_ap, ident_ap)
        self.cnt["pe"] += 1
        inst.then_inc(self.sem["pe"], 1)
        self._post("pe", reads, writes)

    def dma(self, sem, dst, src, reads=(), writes=(), **kw):
        eng = self.stream_eng[sem]
        self._pre(eng, reads, writes)
        inst = self.e[eng].dma_start(dst, src, **kw)
        self.cnt[sem] += 16
        inst.then_inc(self.sem[sem], 16)
        self._post(sem, reads, writes)

    def idma(self, sem, dst, dst_off, src, src_off, reads=(), writes=(), **kw):
        assert self.stream_eng[sem] == "pool"
        self._pre("pool", reads, writes)
        inst = self.e["pool"].indirect_dma_start(dst, dst_off, src, src_off, **kw)
        self.cnt[sem] += 16
        inst.then_inc(self.sem[sem], 16)
        self._post(sem, reads, writes)

    def collective(self, kind, op, groups, ins, outs, reads=(), writes=()):
        self._pre("pool", reads, writes)
        inst = self.e["pool"].collective_compute(
            kind, op, replica_groups=groups, ins=ins, outs=outs)
        self.cnt["cc"] += 1
        inst.then_inc(self.sem["cc"])
        self._post("cc", reads, writes)

    def barrier(self):
        for sn in self.dma_sems:
            self._wait(self.stream_eng[sn], sn, self.cnt[sn])
        self.nc.all_engine_barrier()
        for en in ENG:
            for s in self.sem:
                if s != "cc":
                    self.waited[en][s] = self.cnt[s]

    def finish(self):
        self._wait("pool", "cc", self.cnt["cc"])
        self.barrier()


def _ap_bcast_free(t, part, grp, inner):
    base = t[:].ap
    pitch = base[0][0]
    return bass.AP(t, 0, [[pitch, part], [0, grp], [1, inner]])


def build_nc(T=8192, ST=512, CAP=2432, NBLK=512, GCAP=16384, n_cores=8, sim=False):
    assert T % C == 0 and T % ST == 0 and ST % C == 0
    NC = T // C
    NST = T // ST
    CST = ST // C
    NCHIP = n_cores
    SLOTS = E * CAP

    nc = bass.Bass("TRN2", target_bir_lowering=False)
    es = contextlib.ExitStack()

    din = {}
    def inp(name, shape, dt=F32):
        din[name] = nc.dram_tensor(name, list(shape), dt, kind="ExternalInput")
        return din[name]

    x_h = inp("x", (T, D))
    wq_h = inp("wq", (D, DKQ)); wk_h = inp("wk", (D, DKQ))
    wv_h = inp("wv", (D, D)); wgo_h = inp("wgo", (D, D))
    wgk1_h = inp("wgk1", (D, R)); wgk2_h = inp("wgk2", (R, DKQ))
    wo_h = inp("wo", (D, D))
    wgate_h = inp("wgate", (D, IMLP)); wup_h = inp("wup", (D, IMLP))
    wdown_h = inp("wdown", (IMLP, D))
    mwg_h = inp("mwg", (D, E))
    w1_h = inp("w1", (E, D, HMOE)); w2_h = inp("w2", (E, HMOE, D))
    anw_h = inp("anw_l", (128, 8)); mnw_h = inp("mnw_l", (128, 8))
    gnw_h = inp("gnw_l", (128, 8)); bgk2_h = inp("bgk2_l", (128, 4))
    b1_h = inp("b1_l", (E, 128, HMOE // 128)); b2_h = inp("b2", (E, D))
    ident_h = inp("ident", (128, 128), BF16)
    maskA_h = inp("maskA", (128, 128), BF16)
    tril128s_h = inp("tril128s", (128, 128), F32)
    ones1x128_h = inp("ones1x128", (1, 128), F32)
    onescol_h = inp("onescol", (128, 1), F32)
    iotaE_h = inp("iotaE", (128, 8))
    iota_col_h = inp("iota_col", (128, 1))
    ones8_h = inp("ones8", (128, 8))
    scanmask_h = inp("scanmask", (128, NH * ST), BF16)
    rowmask_h = inp("rowmask", (1, 8 * (T // C)), BF16)
    identF_h = inp("identF", (8, 8))
    core_oh_h = inp("core_oh", (1, 8))
    fconst_h = inp("fconst", (128, 8))
    core_pm_h = inp("core_pm", (8, 1))
    ones8col_h = inp("ones8col", (8, 1))

    out_h = nc.dram_tensor("out", [T, D], F32, kind="ExternalOutput")

    hbm_xn = nc.dram_tensor("hbm_xn", [T, D], BF16)
    hbm_hpre = nc.dram_tensor("hbm_hpre", [T, D], BF16)
    hbm_x2 = nc.dram_tensor("hbm_x2", [T, D], BF16)
    hbm_npre = nc.dram_tensor("hbm_npre", [T, D], BF16)
    hbm_s = nc.dram_tensor("hbm_s", [T, D], BF16)
    hbm_xe = nc.dram_tensor("hbm_xe", [SLOTS, D], BF16)
    hbm_ye = nc.dram_tensor("hbm_ye", [SLOTS, D], BF16)
    hbm_tok = nc.dram_tensor("hbm_tok", [SLOTS, 1], I32)
    hbm_ccin = nc.dram_tensor("hbm_ccin", [8, 16], F32)
    hbm_ccout = nc.dram_tensor("hbm_ccout", [8, 16], F32)

    kb = KB(nc, es)
    sb = lambda name, shape, dt: es.enter_context(nc.sbuf_tensor(name, list(shape), dt))
    psum = lambda name, shape, dt=F32: es.enter_context(nc.psum_tensor(name, list(shape), dt))
    def mk_scope():
        st_ = contextlib.ExitStack()
        return st_, (lambda name, shape, dt: st_.enter_context(
            nc.sbuf_tensor(name, list(shape), dt)))

    s_const = kb.stream("qconst", "sp", 1)
    s_ldx = kb.stream("qldx", "pool", 2)
    s_stxn = kb.stream("qstxn", "act", 2)
    s_sthp = kb.stream("qsthp", "act", 2)

    ident = sb("k_ident", (128, 128), BF16)
    maskA = sb("k_maskA", (128, 128), BF16)
    tril128s = sb("k_tril128s", (128, 128), F32)
    ones1x128 = sb("k_ones1x128", (1, 128), F32)
    onescol = sb("k_onescol", (128, 1), F32)
    iotaE = sb("k_iotaE", (128, 8), F32)
    iota_col = sb("k_iota_col", (128, 1), F32)
    ones8 = sb("k_ones8", (128, 8), F32)
    rowmask = sb("k_rowmask", (1, 8 * NC), BF16)
    identF = sb("k_identF", (8, 8), F32)
    core_oh = sb("k_core_oh", (1, 8), F32)
    core_pm = sb("k_core_pm", (8, 1), F32)
    ones8col = sb("k_ones8col", (8, 1), F32)
    anw = sb("k_anw", (128, 8), F32)
    mnw = sb("k_mnw", (128, 8), F32)
    gnw = sb("k_gnw", (128, 8), F32)
    bgk2 = sb("k_bgk2", (128, 4), F32)
    fconst = sb("k_fconst", (128, 8), F32)
    B_EPSLN = fconst[:, 0:1]; B_EPSRMS = fconst[:, 1:2]; B_ONE = fconst[:, 2:3]

    NCT = NC
    sig_h = sb("sig_h", (128, NCT), F32)
    g0t = sb("g0t", (128, NCT), F32)
    g1t = sb("g1t", (128, NCT), F32)
    e0t = sb("e0t", (128, NCT), F32)
    e1t = sb("e1t", (128, NCT), F32)
    oh0s = sb("oh0s", (128, NCT * 8), F32)
    oh1s = sb("oh1s", (128, NCT * 8), F32)
    tabC0 = sb("tabC0", (1, 8 * NC), F32)
    tabC1 = sb("tabC1", (1, 8 * NC), F32)
    tabC01 = sb("tabC01", (1, 8 * NC), F32)
    slot0t = sb("slot0t", (128, NCT), F32)
    slot1t = sb("slot1t", (128, NCT), F32)
    pre0t = sb("pre0t", (128, NCT), F32)
    pre1t = sb("pre1t", (128, NCT), F32)
    slotg0i = sb("slotg0i", (128, NCT), I32)
    slotg1i = sb("slotg1i", (128, NCT), I32)
    g0vt = sb("g0vt", (128, NCT), F32)
    g1vt = sb("g1vt", (128, NCT), F32)
    cnt_loc = sb("cnt_loc", (1, 16), F32)
    gb0 = sb("gb0", (128, 8), F32)
    gb1 = sb("gb1", (128, 8), F32)
    jt8 = sb("jt8", (128, 8), F32)
    lgtall = sb("lgtall", (128, NCT * 8), F32)
    rc0 = sb("rc0", (128, 1), F32)
    rc1 = sb("rc1", (128, 1), F32)
    rc2 = sb("rc2", (128, 1), F32)
    ccin_sb = sb("ccin_sb", (8, 16), F32)
    ccout_sb = sb("ccout_sb", (8, 16), F32)
    csc = sb("csc", (1, 16), F32)

    for t, h in ((ident, ident_h), (maskA, maskA_h), (tril128s, tril128s_h),
                 (ones1x128, ones1x128_h), (onescol, onescol_h),
                 (iotaE, iotaE_h), (iota_col, iota_col_h), (ones8, ones8_h),
                 (rowmask, rowmask_h), (identF, identF_h),
                 (core_oh, core_oh_h), (core_pm, core_pm_h),
                 (ones8col, ones8col_h), (fconst, fconst_h),
                 (anw, anw_h), (mnw, mnw_h), (gnw, gnw_h), (bgk2, bgk2_h)):
        kb.dma(s_const(0), t[:], h[:], writes=[f"c:{t.name}"])


    es2, sb2 = mk_scope()
    s_ldw = kb.stream("qldw", "pool", 2)
    wq_sb = sb2("wq_sb", (128, 8 * DKQ), BF16)
    wk_sb = sb2("wk_sb", (128, 8 * DKQ), BF16)
    wv_sb = sb2("wv_sb", (128, 8 * D), BF16)
    wgo_sb = sb2("wgo_sb", (128, 8 * D), BF16)
    wgk1_sb = sb2("wgk1_sb", (128, 8 * R), BF16)
    wgk2_sb = sb2("wgk2_sb", (16, DKQ), BF16)
    mwg_sb = sb2("mwg_sb", (128, 8 * E), BF16)
    wspec = ((wq_sb, wq_h, "wq", DKQ, True), (wk_sb, wk_h, "wk", DKQ, True),
             (wv_sb, wv_h, "wv", D, True), (wgo_sb, wgo_h, "wgo", D, True),
             (wgk1_sb, wgk1_h, "wgk1", R, True), (mwg_sb, mwg_h, "mwg", E, False))
    for wi, (wsb, wh, wid, cols, scaled) in enumerate(wspec):
        kb.dma(s_ldw(wi), wsb[:, :],
               bass.AP(wh, 0, [[cols, 128], [128 * cols, 8], [1, cols]]),
               writes=[f"w:{wid}:{ci}" for ci in range(8)])
    kb.dma(s_ldw(0), wgk2_sb[:, :], wgk2_h[:, :], writes=["w:wgk2"])
    wo_sb = sb2("wo_sb", (128, 8 * D), BF16)
    kb.dma(s_ldw(6), wo_sb[:, :],
           bass.AP(wo_h, 0, [[D, 128], [128 * D, 8], [1, D]]),
           writes=[f"w:wo:{ci}" for ci in range(8)])
    for ci in range(8):
        for (wsb, wh, wid, cols, scaled) in wspec:
            if scaled:
                sl = wsb[:, ci * cols:(ci + 1) * cols]
                kb.op("dve", lambda e, sl=sl, ci=ci: e.tensor_scalar_mul(
                    sl, sl, anw[:, ci:ci + 1]),
                    reads=[f"w:{wid}:{ci}", "c:k_anw"], writes=[f"w:{wid}:{ci}"])
        kb.op("dve", lambda e, ci=ci: e.tensor_scalar_mul(
            wo_sb[:, ci * D:(ci + 1) * D], wo_sb[:, ci * D:(ci + 1) * D],
            gnw[:, ci:ci + 1]),
            reads=[f"w:wo:{ci}", "c:k_gnw"], writes=[f"w:wo:{ci}"])

    NB = 2
    xin = [sb2(f"p1x{i}", (128, D), BF16) for i in range(NB)]
    xnout = [sb2(f"p1xn{i}", (128, D), BF16) for i in range(NB)]
    hpout = [sb2(f"p1hp{i}", (128, D), BF16) for i in range(NB)]
    sqscr_one = sb2("p1sq", (128, D), BF16)
    st1_l = [sb2(f"p1st{i}", (128, 8), F32) for i in range(NB)]

    def emit_p1_tile(i):
        b = i % NB
        xt, xnt, hpt = xin[b], xnout[b], hpout[b]
        st1, sqscr = st1_l[b], sqscr_one
        rx, rxn, rhp = f"p1x{b}", f"p1xn{b}", f"p1hp{b}"
        rst, rsq = f"p1st{b}", "p1sq"
        tok = slice(i * 128, (i + 1) * 128)
        kb.dma(s_ldx(i), xt[:], x_h[tok, :], writes=[rx])
        kb.op("act", lambda e: e.activation(sqscr[:], xt[:], AF.Identity,
                                            accum_out=st1[:, 0:1]),
              reads=[rx], writes=[rsq, rst])
        kb.op("act", lambda e: e.activation(sqscr[:], xt[:], AF.Square,
                                            accum_out=st1[:, 1:2]),
              reads=[rx], writes=[rsq, rst])
        kb.op("dve", lambda e: e.tensor_scalar(st1[:, 2:3], st1[:, 0:1], 1.0 / D, None, ALU.mult),
              reads=[rst], writes=[rst])
        kb.op("dve", lambda e: e.tensor_tensor(st1[:, 3:4], st1[:, 2:3], st1[:, 2:3], ALU.mult),
              reads=[rst], writes=[rst])
        kb.op("dve", lambda e: e.scalar_tensor_tensor(st1[:, 3:4], st1[:, 1:2], 1.0 / D,
                                                      st1[:, 3:4], ALU.mult, ALU.subtract),
              reads=[rst], writes=[rst])
        kb.op("act", lambda e: e.activation(st1[:, 4:5], st1[:, 3:4], AF.Ln, bias=B_EPSLN),
              reads=[rst, "c:k_fconst"], writes=[rst])
        kb.op("act", lambda e: e.activation(st1[:, 4:5], st1[:, 4:5], AF.Exp, scale=-0.5),
              reads=[rst], writes=[rst])
        kb.op("dve", lambda e: e.tensor_tensor(st1[:, 5:6], st1[:, 4:5], st1[:, 4:5], ALU.mult),
              reads=[rst], writes=[rst])
        kb.op("dve", lambda e: e.tensor_tensor(st1[:, 5:6], st1[:, 5:6], st1[:, 3:4], ALU.mult),
              reads=[rst], writes=[rst])
        kb.op("act", lambda e: e.activation(st1[:, 5:6], st1[:, 5:6], AF.Ln, bias=B_EPSRMS),
              reads=[rst, "c:k_fconst"], writes=[rst])
        kb.op("act", lambda e: e.activation(st1[:, 6:7], st1[:, 5:6], AF.Exp, scale=0.5),
              reads=[rst], writes=[rst])
        kb.op("act", lambda e: e.activation(st1[:, 5:6], st1[:, 5:6], AF.Exp, scale=-0.5),
              reads=[rst], writes=[rst])
        kb.op("dve", lambda e: e.tensor_copy(sig_h[:, i:i + 1], st1[:, 6:7]),
              reads=[rst], writes=[f"m:sig_h:{i}"])
        kb.op("dve", lambda e: e.tensor_tensor(st1[:, 7:8], st1[:, 4:5], st1[:, 5:6], ALU.mult),
              reads=[rst], writes=[rst])
        kb.op("dve", lambda e: e.tensor_scalar(xnt[:], xt[:], st1[:, 2:3], st1[:, 4:5],
                                               ALU.subtract, ALU.mult),
              reads=[rx, rst], writes=[rxn])
        kb.op("dve", lambda e: e.tensor_scalar(hpt[:], xt[:], st1[:, 2:3], st1[:, 7:8],
                                               ALU.subtract, ALU.mult),
              reads=[rx, rst], writes=[rhp])
        kb.dma(s_stxn(i), hbm_xn[tok, :], xnt[:], reads=[rxn], writes=[f"h:xn:{i}"])
        kb.dma(s_sthp(i), hbm_hpre[tok, :], hpt[:], reads=[rhp], writes=[f"h:hp:{i}"])


    s_ldh = kb.stream("qldh", "sp", 2)
    s_const2 = kb.stream("qconst2", "sp", 1)
    scanmask = sb2("k_scanmask", (128, NH * ST), BF16)
    kb.dma(s_const2(0), scanmask[:], scanmask_h[:], writes=["c:k_scanmask"])
    hpreF_l = [sb2(f"hpreF{i}", (128, 8 * ST), BF16) for i in range(2)]
    Gt = sb2("Gt", (128, NH * ST), F32)
    bt = Gt
    vt = sb2("vt", (128, CST * D), BF16)
    gout = sb2("gout", (128, CST * D), BF16)
    og = gout
    qe_t = [sb2(f"qe_t{i}", (128, ST), BF16) for i in range(2)]
    kt_t = [sb2(f"kt_t{i}", (128, ST), BF16) for i in range(2)]
    kd_t = [sb2(f"kd_t{i}", (128, ST), BF16) for i in range(2)]
    kdT_t = [sb2(f"kdT_t{i}", (128, ST), BF16) for i in range(2)]
    sA = sb2("sA", (128, NH * ST), F32)
    sB = sb2("sB", (128, NH * ST), F32)
    gk1b = sb2("gk1b", (16, ST), BF16)
    Ut_l = [sb2(f"Ut{i}", (128, DV * CST), F32) for i in range(2)]
    Ssc_l = Ut_l
    d0m_one = sb2("d0m", (128, DV * CST), F32)
    d0m_l = [d0m_one, d0m_one]
    lamALL = sb2("lamALL", (128, NH * CST), F32)
    lamm_one = sb2("lamm", (128, CST), F32)
    lamm = [lamm_one, lamm_one]
    Sprev = sb2("Sprev", (128, NH * DV), F32)
    Sbc_one = sb2("Sbcp", (128, DV), BF16)
    Sbc = [Sbc_one, Sbc_one]
    Sbc0_l = [sb2(f"Sbc0_{i}", (128, DV), BF16) for i in range(2)]
    asb = [sb2(f"asb{i}", (128, 128), BF16) for i in range(2)]
    jt256 = sqscr_one
    sst = sb2("sst", (128, CST * NH), F32)
    rstd4 = [sb2(f"rstd4_{i}", (128, NH), F32) for i in range(2)]
    lg = sb2("lg", (8, ST), F32)

    pA = psum("pA", (128, 512))
    pB = psum("pB", (128, 512))
    pC = psum("pC", (128, 512))
    pD = psum("pD", (128, 512))
    pO0 = psum("pO0", (128, DV))
    pO1 = psum("pO1", (128, DV))
    pAA = psum("pAA", (128, 128))
    pAA2 = pAA
    pTR = psum("pTR", (128, 128), BF16)
    pTR2 = pTR

    U_pitch = DV * CST

    for _h in range(NH):
        kb.op("dve", lambda e, _h=_h: e.memset(Sprev[:, _h * DV:(_h + 1) * DV], 0.0),
              writes=[f"Sprev:{_h}"])
    assert ST <= 512

    def emit_p2_st(st):
        t0 = st * ST
        hpreF = hpreF_l[st % 2]
        hpk = f"hpreF{st % 2}"
        hp_r = [f"h:hp:{j}" for j in range(st * CST, (st + 1) * CST)]
        for ci in range(8):
            kb.dma(s_ldh(st), hpreF[:, ci * ST:(ci + 1) * ST],
                   hbm_hpre[t0:t0 + ST, ci * 128:(ci + 1) * 128],
                   reads=hp_r, writes=[hpk], transpose=True)

        for c in range(CST):
            lhs = lambda ci, c=c: hpreF[:, ci * ST + c * C: ci * ST + (c + 1) * C]
            pL_, pH_, kL, kH = (pC, pD, "pC", "pD") if c % 2 == 0 else (pA, pB, "pA", "pB")
            for (wsb, wid, dst, dk_) in ((wv_sb, "wv", vt, "vt"), (wgo_sb, "wgo", gout, "gout")):
                mml = []
                for half in range(2):
                    for ci in range(8):
                        mml.append((
                            (pL_ if half == 0 else pH_)[:, 0:512],
                            lhs(ci),
                            wsb[:, ci * D + half * 512: ci * D + half * 512 + 512]))
                kb.mmg(mml, reads=[hpk] + [f"w:{wid}:{ci}" for ci in range(8)],
                       writes=[kL, kH])
                for half, ps, pk in ((0, pL_, kL), (1, pH_, kH)):
                    if half == 0:
                        kb.op("dve", lambda e, ps=ps, dst=dst, c=c, half=half: e.tensor_copy(
                            dst[:, c * D + half * 512: c * D + half * 512 + 512], ps[:, 0:512]),
                            reads=[pk], writes=[f"{dk_}:{c}"])
                    else:
                        kb.op("act", lambda e, ps=ps, dst=dst, c=c, half=half: e.activation(
                            dst[:, c * D + half * 512: c * D + half * 512 + 512], ps[:, 0:512],
                            AF.Identity),
                            reads=[pk], writes=[f"{dk_}:{c}"])
            kb.op("act", lambda e, c=c: e.activation(
                gout[:, c * D:(c + 1) * D], gout[:, c * D:(c + 1) * D], AF.Silu),
                reads=[f"gout:{c}"], writes=[f"gout:{c}"])
            yield

        kb.mmg([(pA[0:16, 0:ST], wgk1_sb[:, ci * R:(ci + 1) * R],
                 hpreF[:, ci * ST:(ci + 1) * ST]) for ci in range(8)],
               reads=[hpk] + [f"w:wgk1:{ci}" for ci in range(8)], writes=["pA"])
        kb.op("dve", lambda e: e.tensor_copy(gk1b[:, :], pA[0:16, 0:ST]),
              reads=["pA"], writes=["gk1b"])
        yield
        GK = NH * ST
        GTW = [f"Gt:{h}" for h in range(NH)]
        for h in range(NH):
            hS = h * ST
            pgk, kgk = (pB, "pB") if h % 2 == 0 else (pD, "pD")
            kb.mmg([(pgk[:, 0:ST], wgk2_sb[:, h * 128:(h + 1) * 128], gk1b[:, :])],
                   reads=["gk1b", "w:wgk2"], writes=[kgk])
            kb.op("act", lambda e, h=h, hS=hS, pgk=pgk: e.activation(
                sA[:, hS:hS + ST], pgk[:, 0:ST], AF.Identity, bias=bgk2[:, h:h + 1]),
                reads=[kgk, "c:k_bgk2"], writes=["sA"])
            yield
        kb.op("act", lambda e: e.activation(sB[:, 0:GK], sA[:, 0:GK], AF.Abs),
              reads=["sA"], writes=["sB"])
        kb.op("act", lambda e: e.activation(sB[:, 0:GK], sB[:, 0:GK], AF.Exp, scale=-1.0),
              reads=["sB"], writes=["sB"])
        kb.op("act", lambda e: e.activation(sB[:, 0:GK], sB[:, 0:GK], AF.Ln, bias=B_ONE),
              reads=["sB", "c:k_fconst"], writes=["sB"])
        kb.op("dve", lambda e: e.scalar_tensor_tensor(
            Gt[:, 0:GK], sA[:, 0:GK], 0.0, sB[:, 0:GK], ALU.min, ALU.subtract),
            reads=["sA", "sB"], writes=GTW)
        yield
        kb.op("dve", lambda e: e.tensor_tensor_scan(
            bt[:, 0:GK], scanmask[:, 0:GK], Gt[:, 0:GK], 0.0, ALU.mult, ALU.add),
            reads=GTW + ["c:k_scanmask"], writes=GTW)
        bl_view = bass.AP(bt, C - 1, [[NH * ST, 128], [ST, NH], [C, CST]])
        kb.op("dve", lambda e, bl_view=bl_view: e.tensor_scalar_mul(
            lamALL[:, :], bl_view, S16), reads=GTW, writes=["lamALL"])
        kb.op("act", lambda e: e.activation(lamALL[:, :], lamALL[:, :], AF.Exp),
              reads=["lamALL"], writes=["lamALL"])
        yield
        kb.op("act", lambda e: e.activation(sA[:, 0:GK], bt[:, 0:GK], AF.Exp, scale=S16),
              reads=GTW, writes=["sA"])
        kb.op("act", lambda e: e.activation(sB[:, 0:GK], bt[:, 0:GK], AF.Exp, scale=-S16),
              reads=GTW, writes=["sB"])
        yield

        def gen_gla(h):
            hS = h * ST
            hp_ = h % 2
            qe_, kt_, kd_, kdT_ = qe_t[hp_], kt_t[hp_], kd_t[hp_], kdT_t[hp_]
            rqe, rkt, rkd, rkdT = (f"qe_t{hp_}", f"kt_t{hp_}", f"kd_t{hp_}", f"kdT_t{hp_}")
            lamm_ = lamm_one
            rlam, rlamm = "lamALL", "lamm"
            pq, kq = (pA, "pA") if hp_ == 0 else (pC, "pC")
            pk_, kk_ = (pB, "pB") if hp_ == 0 else (pD, "pD")
            Ut, Ssc, d0m = Ut_l[hp_], Ssc_l[hp_], d0m_l[hp_]
            Sbc0 = Sbc0_l[hp_]
            kSb0 = f"Sbc0_{hp_}"
            kUt = kSsc = f"Ut{hp_}"
            kd0 = "d0m"
            kb.mmg([(pq[:, 0:ST], wq_sb[:, ci * DKQ + h * 128: ci * DKQ + (h + 1) * 128],
                     hpreF[:, ci * ST:(ci + 1) * ST]) for ci in range(8)],
                   reads=[hpk] + [f"w:wq:{ci}" for ci in range(8)], writes=[kq])
            kb.op("dve", lambda e, hS=hS, qe_=qe_, pq=pq: e.scalar_tensor_tensor(
                qe_[:, 0:ST], pq[:, 0:ST], SCL, sA[:, hS:hS + ST], ALU.mult, ALU.mult),
                reads=[kq, "sA"], writes=[rqe])
            kb.mmg([(pk_[:, 0:ST], wk_sb[:, ci * DKQ + h * 128: ci * DKQ + (h + 1) * 128],
                     hpreF[:, ci * ST:(ci + 1) * ST]) for ci in range(8)],
                   reads=[hpk] + [f"w:wk:{ci}" for ci in range(8)], writes=[kk_])
            kb.op("dve", lambda e, hS=hS, kt_=kt_, pk_=pk_: e.tensor_mul(
                kt_[:, 0:ST], pk_[:, 0:ST], sB[:, hS:hS + ST]),
                reads=[kk_, "sB"], writes=[rkt])
            yield
            for c in range(CST):
                kb.op("dve", lambda e, c=c, h=h, kd_=kd_, kt_=kt_: e.tensor_scalar_mul(
                    kd_[:, c * C:(c + 1) * C], kt_[:, c * C:(c + 1) * C],
                    lamALL[:, h * CST + c: h * CST + c + 1]),
                    reads=[rkt, rlam], writes=[rkd])
                yield
            for c in range(CST):
                ptr, ktr = (pTR, "pTR")
                kb.transpose(ptr[:, :], kd_[:, c * C:(c + 1) * C], ident[:, :],
                             reads=[rkd, "c:k_ident"], writes=[ktr])
                kb.op("dve", lambda e, c=c, ptr=ptr, kdT_=kdT_: e.tensor_copy(
                    kdT_[:, c * C:(c + 1) * C], ptr[:, :]),
                    reads=[ktr], writes=[rkdT])
                yield
            kb.op("act", lambda e, h=h: e.activation(
                Sbc0[:, :], Sprev[:, h * DV:(h + 1) * DV], AF.Identity),
                reads=[f"Sprev:{h}"], writes=[kSb0])
            pu, ku = (pO0, "pO0") if hp_ == 0 else (pO1, "pO1")
            for c in range(CST):
                kb.mmg([(pu[:, 0:DV], kdT_[:, c * C:(c + 1) * C],
                         vt[:, c * D + h * DV: c * D + (h + 1) * DV])],
                       reads=[rkdT, f"vt:{c}"], writes=[ku])
                udst = bass.AP(Ut, c, [[U_pitch, 128], [CST, DV]])
                if c == 0:
                    kb.op("dve", lambda e, udst=udst, h=h, pu=pu: e.scalar_tensor_tensor(
                        udst, Sprev[:, h * DV:(h + 1) * DV],
                        lamALL[:, h * CST: h * CST + 1],
                        pu[:, 0:DV], ALU.mult, ALU.add),
                        reads=[ku, f"Sprev:{h}", rlam], writes=[kUt])
                else:
                    kb.op("act", lambda e, udst=udst, pu=pu: e.activation(
                        udst, pu[:, 0:DV], AF.Identity),
                        reads=[ku], writes=[kUt])
                yield
            kb.op("dve", lambda e, h=h, lamm_=lamm_: e.tensor_copy(
                lamm_[:, 0:CST], lamALL[:, h * CST:(h + 1) * CST]),
                reads=[rlam], writes=[rlamm])
            kb.op("dve", lambda e, lamm_=lamm_: e.memset(lamm_[:, 0:1], 0.0), writes=[rlamm])
            d0v = bass.AP(d0m, 0, [[U_pitch, 128], [CST, DV], [1, CST]])
            lmv = bass.AP(lamm_, 0, [[CST, 128], [0, DV], [1, CST]])
            kb.op("dve", lambda e, d0v=d0v, lmv=lmv: e.tensor_copy(d0v, lmv),
                  reads=[rlamm], writes=[kd0])
            kb.op("dve", lambda e, d0m=d0m, Ssc=Ssc, Ut=Ut: e.tensor_tensor_scan(
                Ssc[:, 0:U_pitch], d0m[:, 0:U_pitch], Ut[:, 0:U_pitch], 0.0,
                ALU.mult, ALU.add),
                reads=[kd0, kUt], writes=[kSsc])
            yield
            kb.op("act", lambda e, h=h: e.activation(
                Sprev[:, h * DV:(h + 1) * DV],
                bass.AP(Ssc, CST - 1, [[U_pitch, 128], [CST, DV]]), AF.Identity),
                reads=[kSsc], writes=[f"Sprev:{h}"])
            yield
            for c in range(CST):
                pa_, ka_ = (pAA, "pAA")
                ai = c % 2
                as_, kas = asb[ai], f"asb{ai}"
                kb.mmg([(pa_[:, :], kt_[:, c * C:(c + 1) * C],
                         qe_[:, c * C:(c + 1) * C])],
                       reads=[rkt, rqe], writes=[ka_])
                kb.op("dve", lambda e, pa_=pa_, as_=as_: e.tensor_mul(
                    as_[:, :], pa_[:, :], maskA[:, :]),
                    reads=[ka_, "c:k_maskA"], writes=[kas])
                if c == 0:
                    sb_reads = [kSb0]
                    sbv = Sbc0[:, :]
                else:
                    sbci = Sbc_one
                    kb.op("act", lambda e, c=c, sbci=sbci: e.activation(
                        sbci[:, :], bass.AP(Ssc, c - 1, [[U_pitch, 128], [CST, DV]]),
                        AF.Identity),
                        reads=[kSsc], writes=["Sbc"])
                    sb_reads = ["Sbc"]
                    sbv = sbci[:, :]
                po, ko = pu, ku
                kb.mmg([(po[:, 0:DV], as_[:, :],
                         vt[:, c * D + h * DV: c * D + (h + 1) * DV]),
                        (po[:, 0:DV], qe_[:, c * C:(c + 1) * C], sbv)],
                       reads=[kas, f"vt:{c}", rqe] + sb_reads, writes=[ko])
                kb.op("dve", lambda e, c=c, h=h, po=po: e.tensor_mul(
                    og[:, c * D + h * DV: c * D + (h + 1) * DV], po[:, 0:DV],
                    gout[:, c * D + h * DV: c * D + (h + 1) * DV]),
                    reads=[ko], writes=[f"gout:{c}", f"poR{hp_}"])
                yield
                kb.op("act", lambda e, c=c, h=h, po=po: e.activation(
                    jt256[:, 0:DV], po[:, 0:DV], AF.Square,
                    accum_out=sst[:, c * NH + h: c * NH + h + 1]),
                    reads=[ko, f"poR{hp_}"], writes=["jt256", f"sst:{c}:{h}"])


        for hp in range(NH // 2):
            gA, gB = gen_gla(2 * hp), gen_gla(2 * hp + 1)
            live = [gA, gB]
            while live:
                for g_ in list(live):
                    try:
                        next(g_)
                    except StopIteration:
                        live.remove(g_)
                yield

        for c in range(CST):
            rs_, krs = rstd4[c % 2], f"rstd4_{c % 2}"
            sst_r = [f"sst:{c}:{h}" for h in range(NH)]
            kb.op("act", lambda e, c=c, rs_=rs_: e.activation(
                rs_[:, 0:NH], sst[:, c * NH:(c + 1) * NH], AF.Ln,
                scale=1.0 / DV, bias=B_EPSRMS),
                reads=sst_r + ["c:k_fconst"], writes=[krs])
            kb.op("act", lambda e, rs_=rs_: e.activation(rs_[:, 0:NH], rs_[:, 0:NH],
                                                AF.Exp, scale=-0.5),
                reads=[krs], writes=[krs])
            for h in range(NH):
                kb.op("dve", lambda e, c=c, h=h, rs_=rs_: e.tensor_scalar_mul(
                    og[:, c * D + h * DV: c * D + (h + 1) * DV],
                    og[:, c * D + h * DV: c * D + (h + 1) * DV],
                    rs_[:, h:h + 1]),
                    reads=[krs], writes=[f"gout:{c}"])
            yield

        kb.mmg([(pA[0:8, 0:ST], mwg_sb[:, ci * E:(ci + 1) * E],
                 hpreF[:, ci * ST:(ci + 1) * ST]) for ci in range(8)],
               reads=[hpk] + [f"w:mwg:{ci}" for ci in range(8)], writes=["pA"])
        kb.op("dve", lambda e: e.tensor_copy(lg[:, 0:ST], pA[0:8, 0:ST]),
              reads=["pA"], writes=["lg"])
        for c in range(CST):
            ic = st * CST + c
            pa_, ka_ = (pAA, "pAA")
            kb.transpose(pa_[:, 0:8], lg[:, c * C:(c + 1) * C], identF[0:8, 0:8],
                         reads=["lg", "c:k_identF"], writes=[ka_])
            kb.op("dve", lambda e, ic=ic, pa_=pa_: e.tensor_scalar_mul(
                lgtall[:, ic * 8:(ic + 1) * 8], pa_[:, 0:8], sig_h[:, ic:ic + 1]),
                reads=[ka_, f"m:sig_h:{ic}"], writes=[f"lgtall:{ic}"])
        yield

    ogF = sb2("ogF", (128, 8 * ST), BF16)
    xn_tm = [sb2(f"xn_tm{i}", (128, D), BF16) for i in range(2)]
    x2b = [sb2(f"x2b{i}", (128, D), BF16) for i in range(2)]
    npb = [sb2(f"npb{i}", (128, D), BF16) for i in range(2)]
    st3 = [sb2(f"st3_{i}", (128, 4), F32) for i in range(2)]
    sq3a = sqscr_one
    r8c = [sb2(f"r8c{i}", (128, 8), F32) for i in range(2)]
    r8d = [sb2(f"r8d{i}", (128, 8), F32) for i in range(2)]
    rcc = [sb2(f"rcc{i}", (128, 4), F32) for i in range(2)]
    jtc = [sb2(f"jtc{i}", (128, 8), F32) for i in range(2)]
    s_ldxn = kb.stream("qldxn", "sp", 2)
    s_stx2 = kb.stream("qstx2", "act", 2)
    s_stnp = kb.stream("qstnp", "act", 2)
    s_cc = kb.stream("qcc", "pool", 1)

    def emit_top2_chunk(ic):
        if True:
            cp = ic % 2
            lga = lgtall[:, ic * 8:(ic + 1) * 8]
            ra_, rb_ = r8c[cp], r8d[cp]
            kra, krb = f"r8c{cp}", f"r8d{cp}"
            rc_, jt_ = rcc[cp], jtc[cp]
            krc, kjt = f"rcc{cp}", f"jtc{cp}"
            kb.op("dve", lambda e, lga=lga, rc_=rc_: e.reduce_max(
                rc_[:, 0:1], lga, axis=AXX),
                reads=[f"lgtall:{ic}"], writes=[krc])
            kb.op("dve", lambda e, lga=lga, ra_=ra_, rc_=rc_: e.tensor_scalar(
                ra_[:, :], lga, rc_[:, 0:1], None, ALU.is_ge),
                reads=[f"lgtall:{ic}", krc], writes=[kra])
            kb.op("dve", lambda e, ra_=ra_, rb_=rb_: e.tensor_tensor_scan(
                rb_[:, :], ones8[:, :], ra_[:, :], 0.0, ALU.mult, ALU.add),
                reads=[kra, "c:k_ones8"], writes=[krb])
            kb.op("dve", lambda e, rb_=rb_: e.tensor_single_scalar(
                rb_[:, :], rb_[:, :], 1.0, ALU.is_le),
                reads=[krb], writes=[krb])
            kb.op("dve", lambda e, ic=ic, ra_=ra_, rb_=rb_: e.tensor_mul(
                oh0s[:, ic * 8:(ic + 1) * 8], ra_[:, :], rb_[:, :]),
                reads=[kra, krb], writes=["oh0s"])
            kb.op("dve", lambda e, ic=ic, ra_=ra_, lga=lga: e.scalar_tensor_tensor(
                ra_[:, :], oh0s[:, ic * 8:(ic + 1) * 8], -1e30, lga,
                ALU.mult, ALU.add),
                reads=["oh0s", f"lgtall:{ic}"], writes=[kra])
            kb.op("dve", lambda e, ra_=ra_, rc_=rc_: e.reduce_max(
                rc_[:, 1:2], ra_[:, :], axis=AXX),
                reads=[kra], writes=[krc])
            kb.op("dve", lambda e, ra_=ra_, rb_=rb_, rc_=rc_: e.tensor_scalar(
                rb_[:, :], ra_[:, :], rc_[:, 1:2], None, ALU.is_ge),
                reads=[kra, krc], writes=[krb])
            kb.op("dve", lambda e, ra_=ra_, rb_=rb_: e.tensor_tensor_scan(
                ra_[:, :], ones8[:, :], rb_[:, :], 0.0, ALU.mult, ALU.add),
                reads=[krb, "c:k_ones8"], writes=[kra])
            kb.op("dve", lambda e, ra_=ra_: e.tensor_single_scalar(
                ra_[:, :], ra_[:, :], 1.0, ALU.is_le),
                reads=[kra], writes=[kra])
            kb.op("dve", lambda e, ic=ic, ra_=ra_, rb_=rb_: e.tensor_mul(
                oh1s[:, ic * 8:(ic + 1) * 8], rb_[:, :], ra_[:, :]),
                reads=[kra, krb], writes=["oh1s"])
            kb.op("dve", lambda e, ic=ic, jt_=jt_: e.scalar_tensor_tensor(
                jt_[:, :], oh0s[:, ic * 8:(ic + 1) * 8], 1.0, iotaE[:, :],
                ALU.mult, ALU.mult, accum_out=e0t[:, ic:ic + 1]),
                reads=["oh0s", "c:k_iotaE"], writes=[kjt, "e0t"])
            kb.op("dve", lambda e, ic=ic, jt_=jt_: e.scalar_tensor_tensor(
                jt_[:, :], oh1s[:, ic * 8:(ic + 1) * 8], 1.0, iotaE[:, :],
                ALU.mult, ALU.mult, accum_out=e1t[:, ic:ic + 1]),
                reads=["oh1s", "c:k_iotaE"], writes=[kjt, "e1t"])
            kb.op("dve", lambda e, ic=ic, jt_=jt_, lga=lga, rc_=rc_: e.scalar_tensor_tensor(
                jt_[:, :], oh0s[:, ic * 8:(ic + 1) * 8], 1.0, lga,
                ALU.mult, ALU.mult, accum_out=rc_[:, 0:1]),
                reads=["oh0s", f"lgtall:{ic}"], writes=[kjt, krc])
            kb.op("dve", lambda e, ic=ic, jt_=jt_, lga=lga, rc_=rc_: e.scalar_tensor_tensor(
                jt_[:, :], oh1s[:, ic * 8:(ic + 1) * 8], 1.0, lga,
                ALU.mult, ALU.mult, accum_out=rc_[:, 1:2]),
                reads=["oh1s", f"lgtall:{ic}"], writes=[kjt, krc])
            kb.op("dve", lambda e, rc_=rc_: e.tensor_tensor(
                rc_[:, 2:3], rc_[:, 1:2], rc_[:, 0:1], ALU.subtract),
                reads=[krc], writes=[krc])
            kb.op("act", lambda e, rc_=rc_: e.activation(rc_[:, 2:3], rc_[:, 2:3], AF.Exp),
                  reads=[krc], writes=[krc])
            kb.op("dve", lambda e, rc_=rc_: e.tensor_scalar_add(rc_[:, 2:3], rc_[:, 2:3], 1.0),
                  reads=[krc], writes=[krc])
            kb.op("dve", lambda e, rc_=rc_: e.reciprocal(rc_[:, 2:3], rc_[:, 2:3]),
                  reads=[krc], writes=[krc])
            kb.op("dve", lambda e, ic=ic, rc_=rc_: e.tensor_copy(
                g0t[:, ic:ic + 1], rc_[:, 2:3]),
                reads=[krc], writes=["g0t"])
            kb.op("dve", lambda e, ic=ic, rc_=rc_: e.tensor_scalar(
                g1t[:, ic:ic + 1], rc_[:, 2:3], -1.0, 1.0, ALU.mult, ALU.add),
                reads=[krc], writes=["g1t"])
            
    def emit_p3a_st(st):
        t0 = st * ST
        for c in range(CST):
            for fb in range(8):
                kb.transpose(pTR[:, :], og[:, c * D + fb * 128: c * D + (fb + 1) * 128],
                             ident[:, :],
                             reads=[f"gout:{c}", "c:k_ident"], writes=["pTR"])
                kb.op("dve", lambda e, c=c, fb=fb: e.tensor_copy(
                    ogF[:, fb * ST + c * C: fb * ST + (c + 1) * C], pTR[:, :]),
                    reads=["pTR"], writes=["ogF"])
        for c in range(CST):
            cp = (st * CST + c) % 2
            xnt_, x2_, np_, s3 = xn_tm[cp], x2b[cp], npb[cp], st3[cp]
            kxn, kx2, knp, ks3 = (f"xn_tm{cp}", f"x2b{cp}", f"npb{cp}", f"st3_{cp}")
            tok = slice(t0 + c * C, t0 + (c + 1) * C)
            kb.dma(s_ldxn(st * CST + c), xnt_[:, 0:D], hbm_xn[tok, :],
                   reads=[f"h:xn:{st * CST + c}"], writes=[kxn])
            pL_, pH_, kL, kH = (pC, pD, "pC", "pD") if cp == 0 else (pA, pB, "pA", "pB")
            mml = []
            for half, ps in ((0, pL_), (1, pH_)):
                for ci in range(8):
                    mml.append((ps[:, 0:512],
                                ogF[:, ci * ST + c * C: ci * ST + (c + 1) * C],
                                wo_sb[:, ci * D + half * 512: ci * D + half * 512 + 512]))
            kb.mmg(mml, reads=["ogF"] + [f"w:wo:{ci}" for ci in range(8)],
                   writes=[kL, kH])
            for half, ps, pk in ((0, pL_, kL), (1, pH_, kH)):
                kb.op("dve", lambda e, ps=ps, half=half, x2_=x2_, xnt_=xnt_: e.scalar_tensor_tensor(
                    x2_[:, half * 512:(half + 1) * 512], ps[:, 0:512], 1.0,
                    xnt_[:, half * 512:(half + 1) * 512], ALU.mult, ALU.add),
                    reads=[pk, kxn], writes=[kx2])
            kb.dma(s_stx2(st * CST + c), hbm_x2[tok, :], x2_[:, :],
                   reads=[kx2], writes=[f"h:x2:{st * CST + c}"])
            kb.op("act", lambda e, x2_=x2_, s3=s3: e.activation(
                sq3a[:, :], x2_[:, :], AF.Square, accum_out=s3[:, 0:1]),
                reads=[kx2], writes=["sq3a", ks3])
            kb.op("act", lambda e, s3=s3: e.activation(s3[:, 1:2], s3[:, 0:1], AF.Ln,
                                                scale=1.0 / D, bias=B_EPSRMS),
                  reads=[ks3, "c:k_fconst"], writes=[ks3])
            kb.op("act", lambda e, s3=s3: e.activation(s3[:, 1:2], s3[:, 1:2], AF.Exp,
                                                scale=-0.5),
                  reads=[ks3], writes=[ks3])
            kb.op("dve", lambda e, x2_=x2_, np_=np_, s3=s3: e.tensor_scalar_mul(
                np_[:, :], x2_[:, :], s3[:, 1:2]),
                reads=[kx2, ks3], writes=[knp])
            kb.dma(s_stnp(st * CST + c), hbm_npre[tok, :], np_[:, :],
                   reads=[knp], writes=[f"h:np:{st * CST + c}"])
            yield

    def gen_aux(stp):
        gp = emit_p3a_st(stp)
        for c in range(CST):
            next(gp)
            yield
            emit_top2_chunk(stp * CST + c)
            yield

    for i in range(min(2 * CST, NC)):
        emit_p1_tile(i)
    p1_next = min(2 * CST, NC)
    for st in range(NST + 1):
        gens = []
        if st >= 1:
            gens.append(gen_aux(st - 1))
        if st < NST:
            gens.append(emit_p2_st(st))
        p1_hi = min((st + 3) * CST, NC)
        live = list(gens)
        rnd = 0
        aux_started = False
        while live:
            rnd += 1
            for g_ in list(live):
                if (len(gens) > 1 and g_ is gens[0] and aux_started
                        and rnd % 3 != 0 and len(live) > 1):
                    continue
                try:
                    next(g_)
                    if len(gens) > 1 and g_ is gens[0]:
                        aux_started = True
                except StopIteration:
                    live.remove(g_)
            if p1_next < p1_hi:
                emit_p1_tile(p1_next)
                p1_next += 1

    def emit_colsums_cc():
     for ic in range(NC):
         for ti, (ohl, ohk, tabC, tk) in enumerate((
                 ([oh0s], ["oh0s"], tabC0, "tabC0"),
                 ([oh1s], ["oh1s"], tabC1, "tabC1"),
                 ([oh0s, oh1s], ["oh0s", "oh1s"], tabC01, "tabC01"))):
             pcs, kcs = (pO0, "pO0") if ti % 2 == 0 else (pO1, "pO1")
             kb.mmg([(pcs[0:1, 0:8], onescol[:, :], ohs[:, ic * 8:(ic + 1) * 8])
                     for ohs in ohl],
                    reads=ohk + ["c:k_onescol"], writes=[kcs])
             kb.op("dve", lambda e, tabC=tabC, ic=ic, pcs=pcs: e.tensor_copy(
                 bass.AP(tabC, ic, [[8 * NC, 1], [NC, 8]]), pcs[0:1, 0:8]),
                 reads=[kcs], writes=[tk])
     for tabC, tk in ((tabC0, "tabC0"), (tabC1, "tabC1"), (tabC01, "tabC01")):
         kb.op("dve", lambda e, tabC=tabC: e.tensor_tensor_scan(
             tabC[:, :], rowmask[:, :], tabC[:, :], 0.0, ALU.mult, ALU.add),
             reads=[tk, "c:k_rowmask"], writes=[tk])
     kb.op("dve", lambda e: e.tensor_copy(
         cnt_loc[:, 0:8], bass.AP(tabC0, NC - 1, [[8 * NC, 1], [NC, 8]])),
         reads=["tabC0"], writes=["cnt_loc"])
     kb.op("dve", lambda e: e.tensor_copy(
         cnt_loc[:, 8:16], bass.AP(tabC1, NC - 1, [[8 * NC, 1], [NC, 8]])),
         reads=["tabC1"], writes=["cnt_loc"])
     kb.mmg([(pA[0:8, 0:16], core_oh[:, :], cnt_loc[:, :])],
            reads=["cnt_loc", "c:k_core_oh"], writes=["pA"])
     kb.op("dve", lambda e: e.tensor_copy(ccin_sb[:, :], pA[0:8, 0:16]),
           reads=["pA"], writes=["ccin_sb"])
     kb.dma(s_cc(0), hbm_ccin[:, :], ccin_sb[:, :], reads=["ccin_sb"], writes=["h:ccin"])
     kb.collective("AllReduce", ALU.add, [list(range(NCHIP))],
                   [hbm_ccin.ap().opt()], [hbm_ccout.ap().opt()],
                   reads=["h:ccin"], writes=["h:ccout"])

    emit_colsums_cc()

    kb.barrier()
    es2.close()

    capok0 = sb("capok0", (128, NC), F32)
    capok1 = sb("capok1", (128, NC), F32)
    tokf = sb("tokf", (128, 1), F32)
    toki = sb("toki", (128, 1), I32)
    sgf = sb("sgf", (128, 2), F32)
    sgi = sb("sgi", (128, 2), I32)
    isc1 = sb("isc1", (128, 2), I32)
    isc2 = sb("isc2", (128, 2), I32)
    s_sc = kb.stream("qsc", "pool", 2)
    bc_slots = kb.e["pool"].to_reg(SLOTS - 1)
    bc_tok = kb.e["pool"].to_reg(T - 1)
    es3b, sb3b = mk_scope()
    NI = IMLP // 128
    wga_sb = sb3b("wga_sb", (128, 8 * IMLP), BF16)
    wup_sb = sb3b("wup_sb", (128, 8 * IMLP), BF16)
    wdn_sb = sb3b("wdn_sb", (128, NI * D), BF16)
    kb.dma(s_ldw(0), wga_sb[:, :],
           bass.AP(wgate_h, 0, [[IMLP, 128], [128 * IMLP, 8], [1, IMLP]]),
           writes=[f"w:wga:{ci}" for ci in range(8)])
    kb.dma(s_ldw(1), wup_sb[:, :],
           bass.AP(wup_h, 0, [[IMLP, 128], [128 * IMLP, 8], [1, IMLP]]),
           writes=[f"w:wup:{ci}" for ci in range(8)])
    kb.dma(s_ldw(0), wdn_sb[:, :],
           bass.AP(wdown_h, 0, [[D, 128], [128 * D, NI], [1, D]]),
           writes=[f"w:wdn:{ci}" for ci in range(NI)])
    for ci in range(8):
        kb.op("dve", lambda e, ci=ci: e.tensor_scalar_mul(
            wga_sb[:, ci * IMLP:(ci + 1) * IMLP], wga_sb[:, ci * IMLP:(ci + 1) * IMLP],
            mnw[:, ci:ci + 1]),
            reads=[f"w:wga:{ci}", "c:k_mnw"], writes=[f"w:wga:{ci}"])
        kb.op("dve", lambda e, ci=ci: e.tensor_scalar_mul(
            wup_sb[:, ci * IMLP:(ci + 1) * IMLP], wup_sb[:, ci * IMLP:(ci + 1) * IMLP],
            mnw[:, ci:ci + 1]),
            reads=[f"w:wup:{ci}", "c:k_mnw"], writes=[f"w:wup:{ci}"])
    tokt = sb3b("tokt", (128, SLOTS // 128), I32)
    kb.op("dve", lambda e: e.memset(tokt[:, :], 0), writes=["tokt"])
    kb.dma(s_sc(0), bass.AP(hbm_tok, 0, [[SLOTS // 128, 128], [1, SLOTS // 128]]),
           tokt[:, :], reads=["tokt"], writes=["h:tok:z"])

    def emit_p4b_chunk(ic):
        o8 = slice(ic * 8, (ic + 1) * 8)
        mml = [(pO0[:, 0:8], tril128s[:, :], oh0s[:, o8]),
               (pO0[:, 0:8], tril128s[:, :], oh1s[:, o8])]
        if ic > 0:
            mml.append((pO0[:, 0:8], ones1x128[:, :],
                        bass.AP(tabC01, ic - 1, [[8 * NC, 1], [NC, 8]])))
        kb.mmg(mml, reads=["oh0s", "oh1s", "tabC01", "c:k_tril128s", "c:k_ones1x128"],
               writes=["pO0"])
        kb.op("dve", lambda e, ic=ic, o8=o8: e.scalar_tensor_tensor(
            jt8[:, :], oh0s[:, o8], 1.0, pO0[:, 0:8], ALU.mult, ALU.mult,
            accum_out=slot0t[:, ic:ic + 1]),
            reads=["oh0s", "pO0"], writes=["jt8", "slot0t"])
        kb.op("dve", lambda e, ic=ic, o8=o8: e.scalar_tensor_tensor(
            jt8[:, :], oh1s[:, o8], 1.0, pO0[:, 0:8], ALU.mult, ALU.mult,
            accum_out=slot1t[:, ic:ic + 1]),
            reads=["oh1s", "pO0"], writes=["jt8", "slot1t"])
        mml = [(pO1[:, 0:8], tril128s[:, :], oh0s[:, o8])]
        if ic > 0:
            mml.append((pO1[:, 0:8], ones1x128[:, :],
                        bass.AP(tabC0, ic - 1, [[8 * NC, 1], [NC, 8]])))
        kb.mmg(mml, reads=["oh0s", "tabC0", "c:k_tril128s", "c:k_ones1x128"],
               writes=["pO1"])
        kb.op("dve", lambda e, ic=ic, o8=o8: e.scalar_tensor_tensor(
            jt8[:, :], oh0s[:, o8], 1.0, pO1[:, 0:8], ALU.mult, ALU.mult,
            accum_out=pre0t[:, ic:ic + 1]),
            reads=["oh0s", "pO1"], writes=["jt8", "pre0t"])
        mml = [(pO1[:, 0:8], tril128s[:, :], oh1s[:, o8])]
        if ic > 0:
            mml.append((pO1[:, 0:8], ones1x128[:, :],
                        bass.AP(tabC1, ic - 1, [[8 * NC, 1], [NC, 8]])))
        kb.mmg(mml, reads=["oh1s", "tabC1", "c:k_tril128s", "c:k_ones1x128"],
               writes=["pO1"])
        kb.op("dve", lambda e, ic=ic, o8=o8: e.scalar_tensor_tensor(
            jt8[:, :], oh1s[:, o8], 1.0, pO1[:, 0:8], ALU.mult, ALU.mult,
            accum_out=pre1t[:, ic:ic + 1]),
            reads=["oh1s", "pO1"], writes=["jt8", "pre1t"])
        kb.op("dve", lambda e, ic=ic: e.tensor_single_scalar(
            capok0[:, ic:ic + 1], slot0t[:, ic:ic + 1], float(CAP), ALU.is_lt),
            reads=["slot0t"], writes=["capok0"])
        kb.op("dve", lambda e, ic=ic: e.tensor_single_scalar(
            capok1[:, ic:ic + 1], slot1t[:, ic:ic + 1], float(CAP), ALU.is_lt),
            reads=["slot1t"], writes=["capok1"])
        kb.op("dve", lambda e, ic=ic: e.scalar_tensor_tensor(
            sgf[:, 0:1], e0t[:, ic:ic + 1], float(CAP), slot0t[:, ic:ic + 1],
            ALU.mult, ALU.add),
            reads=["e0t", "slot0t"], writes=["sgf"])
        kb.op("dve", lambda e, ic=ic: e.scalar_tensor_tensor(
            sgf[:, 1:2], e1t[:, ic:ic + 1], float(CAP), slot1t[:, ic:ic + 1],
            ALU.mult, ALU.add),
            reads=["e1t", "slot1t"], writes=["sgf"])
        for col, cap_t, cap_r in ((0, capok0, "capok0"), (1, capok1, "capok1")):
            kb.op("dve", lambda e, ic=ic, col=col, cap_t=cap_t: e.scalar_tensor_tensor(
                sgf[:, col:col + 1], cap_t[:, ic:ic + 1], 1e7, sgf[:, col:col + 1],
                ALU.mult, ALU.subtract),
                reads=[cap_r, "sgf"], writes=["sgf"])
            kb.op("dve", lambda e, col=col: e.tensor_scalar(
                sgf[:, col:col + 1], sgf[:, col:col + 1], -1.0, 1e7, ALU.mult, ALU.add),
                reads=["sgf"], writes=["sgf"])
        kb.op("dve", lambda e, ic=ic: e.tensor_copy(sgi[:, :], sgf[:, :]),
              reads=["sgf"], writes=["sgi"])
        kb.op("dve", lambda e, ic=ic: e.tensor_copy(slotg0i[:, ic:ic + 1], sgi[:, 0:1]),
              reads=["sgi"], writes=["slotg0i"])
        kb.op("dve", lambda e, ic=ic: e.tensor_copy(slotg1i[:, ic:ic + 1], sgi[:, 1:2]),
              reads=["sgi"], writes=["slotg1i"])
        kb.op("dve", lambda e, ic=ic: e.tensor_scalar_add(
            tokf[:, :], iota_col[:, :], float(ic * 128)),
            reads=["c:k_iota_col"], writes=["tokf"])
        kb.op("dve", lambda e: e.tensor_copy(toki[:, :], tokf[:, :]),
              reads=["tokf"], writes=["toki"])
        for kk, (sgcol, sgrg) in enumerate(((slotg0i, "slotg0i"), (slotg1i, "slotg1i"))):
            kb.op("dve", lambda e, sgcol=sgcol, ic=ic: e.tensor_single_scalar(
                isc1[:, 0:1], sgcol[:, ic:ic + 1], 127, ALU.bitwise_and),
                reads=[sgrg], writes=["isc1"])
            kb.op("dve", lambda e, sgcol=sgcol, ic=ic: e.tensor_single_scalar(
                isc2[:, 0:1], sgcol[:, ic:ic + 1], 7, ALU.arith_shift_right),
                reads=[sgrg], writes=["isc2"])
            kb.op("dve", lambda e: e.tensor_scalar(
                isc1[:, 0:1], isc1[:, 0:1], SLOTS // 128, None, ALU.mult),
                reads=["isc1"], writes=["isc1"])
            kb.op("dve", lambda e: e.tensor_add(isc1[:, 0:1], isc1[:, 0:1], isc2[:, 0:1]),
                  reads=["isc1", "isc2"], writes=["isc1"])
            kb.idma(s_sc(ic + kk), bass.AP(hbm_tok, 0, [[1, SLOTS], [1, 1]]),
                    IndirectOffsetOnAxis(isc1[:, 0:1], 0), toki[:, :], None,
                    bounds_check=bc_slots, oob_is_err=False,
                    reads=["isc1", "toki", "h:tok:z"], writes=[f"h:tok:{ic}:{kk}"])

    xg = [sb3b(f"xg{i}", (128, D), BF16) for i in range(2)]
    s_gx = kb.stream("qgx", "pool", 2)
    s_sx = kb.stream("qsx", "sp", 2)
    def emit_p5_head():
        kb.dma(s_sc(0), tokt[:, :],
               bass.AP(hbm_tok, 0, [[SLOTS // 128, 128], [1, SLOTS // 128]]),
               reads=["h:tok:z"] + [f"h:tok:{ic}:{kk}" for ic in range(NC) for kk in range(2)],
               writes=["tokt"])

    def emit_p5_blk(blk):
        bb = blk % 2
        kb.idma(s_gx(blk), xg[bb][:, :], None, hbm_xn[:, :],
                IndirectOffsetOnAxis(tokt[:, blk:blk + 1], 0),
                bounds_check=bc_tok, oob_is_err=False,
                reads=["tokt", "h:xn"], writes=[f"xg{bb}"])
        kb.dma(s_sx(blk), hbm_xe[blk * 128:(blk + 1) * 128, :], xg[bb][:, :],
               reads=[f"xg{bb}"], writes=[f"h:xe:{blk}"])

    npF_l = [sb3b(f"npF{i}", (128, 8 * ST), BF16) for i in range(2)]
    mt = sb3b("mt", (128, NI * ST), BF16)
    sgl_l = [sb3b(f"sgl{i}", (128, 512), F32) for i in range(2)]
    sbuf_s_l = [sb3b(f"sbuf_s{i}", (128, D), BF16) for i in range(2)]
    x2_tm_one = sb3b("x2_tm", (128, D), BF16)
    x2_tm = [x2_tm_one, x2_tm_one]
    s_ldnp = kb.stream("qldnp", "sp", 2)
    s_ldx2 = kb.stream("qldx2", "sp", 2)
    s_sts = kb.stream("qsts", "act", 2)
    def emit_p3b_st(st):
        t0 = st * ST
        npF = npF_l[st % 2]
        npk = f"npF{st % 2}"
        np_r = [f"h:np:{j}" for j in range(st * CST, (st + 1) * CST)]
        for ci in range(8):
            kb.dma(s_ldnp(st), npF[:, ci * ST:(ci + 1) * ST],
                   hbm_npre[t0:t0 + ST, ci * 128:(ci + 1) * 128],
                   reads=np_r, writes=[npk], transpose=True)
        for ii in range(NI):
            pg, kg, pu_, ku_ = (pA, "pA", pB, "pB") if ii % 2 == 0 else (pC, "pC", pD, "pD")
            sg_, ksg = sgl_l[ii % 2], f"sgl{ii % 2}"
            kb.mmg([(pg[:, 0:ST], wga_sb[:, ci * IMLP + ii * 128: ci * IMLP + (ii + 1) * 128],
                     npF[:, ci * ST:(ci + 1) * ST]) for ci in range(8)],
                   reads=[npk] + [f"w:wga:{ci}" for ci in range(8)], writes=[kg])
            kb.mmg([(pu_[:, 0:ST], wup_sb[:, ci * IMLP + ii * 128: ci * IMLP + (ii + 1) * 128],
                     npF[:, ci * ST:(ci + 1) * ST]) for ci in range(8)],
                   reads=[npk] + [f"w:wup:{ci}" for ci in range(8)], writes=[ku_])
            kb.op("act", lambda e, pg=pg, sg_=sg_: e.activation(
                sg_[:, 0:ST], pg[:, 0:ST], AF.Silu),
                reads=[kg], writes=[ksg])
            kb.op("dve", lambda e, ii=ii, sg_=sg_, pu_=pu_: e.tensor_mul(
                mt[:, ii * ST:(ii + 1) * ST], sg_[:, 0:ST], pu_[:, 0:ST]),
                reads=[ksg, ku_], writes=["mt"])
            yield
        for c in range(CST):
            x2t, kx2t = x2_tm_one, "x2_tm"
            sbuf_s, ksbs = sbuf_s_l[c % 2], f"sbuf_s{c % 2}"
            tok = slice(t0 + c * C, t0 + (c + 1) * C)
            kb.dma(s_ldx2(st * CST + c), x2t[:, :], hbm_x2[tok, :],
                   reads=[f"h:x2:{st * CST + c}"], writes=[kx2t])
            pd0, pd1 = (pA, pB) if c % 2 == 0 else (pC, pD)
            kpd0, kpd1 = ("pA", "pB") if c % 2 == 0 else ("pC", "pD")
            mml = []
            for half, ps in ((0, pd0), (1, pd1)):
                for ii in range(NI):
                    mml.append((ps[:, 0:512],
                                mt[:, ii * ST + c * C: ii * ST + (c + 1) * C],
                                wdn_sb[:, ii * D + half * 512: ii * D + half * 512 + 512]))
            kb.mmg(mml, reads=["mt"] + [f"w:wdn:{ci}" for ci in range(NI)],
                   writes=[kpd0, kpd1])
            for half, ps, pk in ((0, pd0, kpd0), (1, pd1, kpd1)):
                kb.op("dve", lambda e, ps=ps, half=half, x2t=x2t, sbuf_s=sbuf_s: e.scalar_tensor_tensor(
                    sbuf_s[:, half * 512:(half + 1) * 512], ps[:, 0:512], 1.0,
                    x2t[:, half * 512:(half + 1) * 512], ALU.mult, ALU.add),
                    reads=[pk, kx2t], writes=[ksbs])
            kb.dma(s_sts(st * CST + c), hbm_s[tok, :], sbuf_s[:, :],
                   reads=[ksbs], writes=[f"h:s:{st * CST + c}"])
            yield

    def gen_p4b5():
        for ic in range(NC):
            emit_p4b_chunk(ic)
            yield
        emit_p5_head()
        for blk in range(SLOTS // 128):
            emit_p5_blk(blk)
            if blk % 2:
                yield

    def gen_p3b_all():
        for st in range(NST):
            yield from emit_p3b_st(st)

    gA_, gB_ = gen_p4b5(), gen_p3b_all()
    live = [gA_, gB_]
    turn = 0
    while live:
        turn += 1
        for g in list(live):
            if g is gA_ and turn % 3 != 0:
                continue
            try:
                next(g)
            except StopIteration:
                live.remove(g)

    kb.barrier()
    es3b.close()

    kb.dma(s_cc(0), ccout_sb[:, :], hbm_ccout[:, :], reads=["h:ccout"], writes=["ccout_sb"])
    kb.mmg([(pA[0:1, 0:8], core_pm[:, :], ccout_sb[:, 0:8])],
           reads=["ccout_sb", "c:k_core_pm"], writes=["pA"])
    kb.op("dve", lambda e: e.tensor_copy(csc[:, 0:8], pA[0:1, 0:8]),
          reads=["pA"], writes=["csc"])
    kb.mmg([(pA[0:1, 0:8], ones8col[:, :], ccout_sb[:, 0:8]),
            (pA[0:1, 0:8], core_pm[:, :], ccout_sb[:, 8:16])],
           reads=["ccout_sb", "c:k_ones8col", "c:k_core_pm"], writes=["pA"])
    kb.op("dve", lambda e: e.tensor_copy(csc[:, 8:16], pA[0:1, 0:8]),
          reads=["pA"], writes=["csc"])
    kb.mmg([(pA[:, 0:8], ones1x128[:, :], csc[:, 0:8])],
           reads=["csc", "c:k_ones1x128"], writes=["pA"])
    kb.op("dve", lambda e: e.tensor_copy(gb0[:, :], pA[:, 0:8]),
          reads=["pA"], writes=["gb0"])
    kb.mmg([(pA[:, 0:8], ones1x128[:, :], csc[:, 8:16])],
           reads=["csc", "c:k_ones1x128"], writes=["pA"])
    kb.op("dve", lambda e: e.tensor_copy(gb1[:, :], pA[:, 0:8]),
          reads=["pA"], writes=["gb1"])
    for ic in range(NC):
        o8 = slice(ic * 8, (ic + 1) * 8)
        kb.op("dve", lambda e, o8=o8: e.scalar_tensor_tensor(
            jt8[:, :], oh0s[:, o8], 1.0, gb0[:, :], ALU.mult, ALU.mult,
            accum_out=rc0[:, :]),
            reads=["oh0s", "gb0"], writes=["jt8", "rc0"])
        kb.op("dve", lambda e, ic=ic: e.tensor_add(rc0[:, :], rc0[:, :],
                                                   pre0t[:, ic:ic + 1]),
              reads=["rc0", "pre0t"], writes=["rc0"])
        kb.op("dve", lambda e: e.tensor_single_scalar(rc0[:, :], rc0[:, :],
                                                      float(GCAP), ALU.is_lt),
              reads=["rc0"], writes=["rc0"])
        kb.op("dve", lambda e, ic=ic: e.tensor_mul(rc0[:, :], rc0[:, :],
                                                   capok0[:, ic:ic + 1]),
              reads=["rc0", "capok0"], writes=["rc0"])
        kb.op("dve", lambda e, ic=ic: e.tensor_mul(g0vt[:, ic:ic + 1],
                                                   rc0[:, :], g0t[:, ic:ic + 1]),
              reads=["rc0", "g0t"], writes=["g0vt"])
        kb.op("dve", lambda e, o8=o8: e.scalar_tensor_tensor(
            jt8[:, :], oh1s[:, o8], 1.0, gb1[:, :], ALU.mult, ALU.mult,
            accum_out=rc1[:, :]),
            reads=["oh1s", "gb1"], writes=["jt8", "rc1"])
        kb.op("dve", lambda e, ic=ic: e.tensor_add(rc1[:, :], rc1[:, :],
                                                   pre1t[:, ic:ic + 1]),
              reads=["rc1", "pre1t"], writes=["rc1"])
        kb.op("dve", lambda e: e.tensor_single_scalar(rc1[:, :], rc1[:, :],
                                                      float(GCAP), ALU.is_lt),
              reads=["rc1"], writes=["rc1"])
        kb.op("dve", lambda e, ic=ic: e.tensor_mul(rc1[:, :], rc1[:, :],
                                                   capok1[:, ic:ic + 1]),
              reads=["rc1", "capok1"], writes=["rc1"])
        kb.op("dve", lambda e, ic=ic: e.tensor_mul(g1vt[:, ic:ic + 1],
                                                   rc1[:, :], g1t[:, ic:ic + 1]),
              reads=["rc1", "g1t"], writes=["g1vt"])


    es6, sb6 = mk_scope()
    NHC = HMOE // 128
    BLKS = []
    _r = 0
    while _r < CAP:
        BLKS.append((len(BLKS), min(NBLK, CAP - _r)))
        _r += NBLK
    NB6 = len(BLKS)
    assert CAP % 128 == 0 and NBLK % 128 == 0
    w1_sb = sb6("w1_sb", (128, 8 * HMOE), BF16)
    w2_sb = sb6("w2_sb", (128, NHC * D), BF16)
    b1_sb = sb6("b1_sb", (128, NHC), F32)
    b2r = sb6("b2r", (1, D), BF16)
    ones1b = sb6("ones1b", (1, 128), BF16)
    kb.op("dve", lambda e: e.memset(ones1b[:, :], 1.0), writes=["c:ones1b"])
    XeFb = [sb6(f"XeF{i}", (128, 8 * NBLK), BF16) for i in range(2)]
    h1_sb = sb6("h1_sb", (128, NHC * NBLK), BF16)
    ye_sb = [sb6("ye_sb0", (128, D), BF16)]
    s_ldw1 = kb.stream("qldw1", "pool", 2)
    s_ldw2 = kb.stream("qldw2", "pool", 2)
    s_ldxe = kb.stream("qldxe", "sp", 2)
    s_stye = kb.stream("qstye", "act", 2)

    def ld_w1(e_):
        kb.dma(s_ldw1(e_), w1_sb[:, :],
               bass.AP(w1_h, e_ * D * HMOE,
                       [[HMOE, 128], [128 * HMOE, 8], [1, HMOE]]),
               writes=["w:w1"])
        kb.dma(s_ldw1(e_), b1_sb[:, :], b1_h[e_, :, :], writes=["w:b1"])

    def ld_w2(e_):
        kb.dma(s_ldw2(e_), w2_sb[:, :],
               bass.AP(w2_h, e_ * HMOE * D,
                       [[D, 128], [128 * D, NHC], [1, D]]),
               writes=["w:w2"])
        kb.dma(s_ldw2(e_), b2r[:, :], b2_h[e_:e_ + 1, :], writes=["w:b2"])

    def ld_xe(e_, blk, bs):
        r0 = e_ * CAP + blk * NBLK
        gb = e_ * NB6 + blk
        xe_r = [f"h:xe:{bb}" for bb in range(r0 // 128, (r0 + bs) // 128)]
        Xe = XeFb[gb % 2]
        for ci in range(8):
            kb.dma(s_ldxe(gb), Xe[:, ci * NBLK: ci * NBLK + bs],
                   hbm_xe[r0:r0 + bs, ci * 128:(ci + 1) * 128],
                   reads=xe_r, writes=[f"XeF{gb % 2}"], transpose=True)

    ld_w1(0)
    ld_w2(0)
    ld_xe(0, 0, BLKS[0][1])
    for e_ in range(E):
        for blk, bs in BLKS:
            gb = e_ * NB6 + blk
            if blk + 1 < NB6:
                ld_xe(e_, blk + 1, BLKS[blk + 1][1])
            elif e_ + 1 < E:
                ld_xe(e_ + 1, 0, BLKS[0][1])
            Xe = XeFb[gb % 2]
            xrg = f"XeF{gb % 2}"
            for hc in range(NHC):
                ps, pk = (pA, "pA") if hc % 2 == 0 else (pB, "pB")
                kb.mmg([(ps[:, 0:bs],
                         w1_sb[:, ci * HMOE + hc * 128: ci * HMOE + (hc + 1) * 128],
                         Xe[:, ci * NBLK: ci * NBLK + bs]) for ci in range(8)],
                       reads=[xrg, "w:w1"], writes=[pk])
                kb.op("act", lambda e, ps=ps, hc=hc, bs=bs: e.activation(
                    h1_sb[:, hc * NBLK: hc * NBLK + bs], ps[:, 0:bs],
                    AF.Relu, bias=b1_sb[:, hc:hc + 1]),
                    reads=[pk, "w:b1"], writes=[f"h1:{hc}"])
            if blk == NB6 - 1 and e_ + 1 < E:
                ld_w1(e_ + 1)
            h1r = [f"h1:{hc}" for hc in range(NHC)]
            r0 = e_ * CAP + blk * NBLK
            for g in range(bs // 128):
                pcs = (pC, pD, "pC", "pD") if g % 2 == 0 else (pA, pB, "pA", "pB")
                psl, psh, pkl, pkh = pcs
                mml = [(psl[:, 0:512], ones1b[:, :], b2r[:, 0:512]),
                       (psh[:, 0:512], ones1b[:, :], b2r[:, 512:1024])]
                for hc in range(NHC):
                    lh = h1_sb[:, hc * NBLK + g * 128: hc * NBLK + (g + 1) * 128]
                    mml.append((psl[:, 0:512], lh, w2_sb[:, hc * D: hc * D + 512]))
                    mml.append((psh[:, 0:512], lh, w2_sb[:, hc * D + 512: hc * D + 1024]))
                kb.mmg(mml, reads=h1r + ["w:w2", "w:b2", "c:ones1b"],
                       writes=[pkl, pkh])
                yb = ye_sb[0]
                for half, ps, pk in ((0, psl, pkl), (1, psh, pkh)):
                    kb.op("dve", lambda e, ps=ps, half=half, yb=yb: e.tensor_copy(
                        yb[:, half * 512:(half + 1) * 512], ps[:, 0:512]),
                        reads=[pk], writes=["ye_sb0"])
                rr = r0 + g * 128
                kb.dma(s_stye(g), hbm_ye[rr:rr + 128, :], yb[:, :],
                       reads=["ye_sb0"], writes=[f"h:ye:{rr}"])
        if e_ + 1 < E:
            ld_w2(e_ + 1)

    kb.barrier()
    es6.close()

    NP7 = 4
    r0b = [sb(f"r0b{i}", (128, D), BF16) for i in range(NP7)]
    r1b = [sb(f"r1b{i}", (128, D), BF16) for i in range(NP7)]
    yv = [sb(f"yv{i}", (128, D), F32) for i in range(NP7)]
    gv = [sb(f"gv{i}", (128, D), F32) for i in range(NP7)]
    s_tm = [sb(f"s_tm{i}", (128, D), BF16) for i in range(NP7)]
    x_tm2 = [sb(f"x_tm2{i}", (128, D), F32) for i in range(NP7)]
    outb = [sb(f"outb{i}", (128, D), F32) for i in range(NP7)]
    s_gy0 = kb.stream("qgy0", "pool", NP7)
    s_lds = kb.stream("qlds", "sp", NP7)
    s_ldxf = kb.stream("qldxf", "act", NP7)
    s_out = kb.stream("qout", "act", 2)
    for ic in range(NC):
        cp = ic % NP7
        R0, R1, YV, GV, SS, XX, OB = (r0b[cp], r1b[cp], yv[cp], gv[cp],
                                      s_tm[cp], x_tm2[cp], outb[cp])
        kR0, kR1, kYV, kGV, kSS, kXX, kOB = (f"r0b{cp}", f"r1b{cp}", f"yv{cp}",
                                             f"gv{cp}", f"s_tm{cp}", f"x_tm2{cp}",
                                             f"outb{cp}")
        tok = slice(ic * 128, (ic + 1) * 128)
        kb.idma(s_gy0(ic), R0[:, :], None, hbm_ye[:, :],
                IndirectOffsetOnAxis(slotg0i[:, ic:ic + 1], 0),
                bounds_check=bc_slots, oob_is_err=False,
                reads=["slotg0i"], writes=[kR0])
        kb.idma(s_gy0(ic), R1[:, :], None, hbm_ye[:, :],
                IndirectOffsetOnAxis(slotg1i[:, ic:ic + 1], 0),
                bounds_check=bc_slots, oob_is_err=False,
                reads=["slotg1i"], writes=[kR1])
        kb.dma(s_lds(ic), SS[:, :], hbm_s[tok, :], reads=[f"h:s:{ic}"], writes=[kSS])
        kb.dma(s_ldxf(ic), XX[:, :], x_h[tok, :], writes=[kXX])
        kb.op("dve", lambda e, ic=ic, R0=R0, YV=YV: e.tensor_scalar_mul(
            YV[:, :], R0[:, :], g0vt[:, ic:ic + 1]),
            reads=[kR0, "g0vt"], writes=[kYV])
        kb.op("dve", lambda e, ic=ic, R1=R1, YV=YV: e.scalar_tensor_tensor(
            YV[:, :], R1[:, :], g1vt[:, ic:ic + 1], YV[:, :], ALU.mult, ALU.add),
            reads=[kR1, "g1vt", kYV], writes=[kYV])
        kb.op("act", lambda e, YV=YV, GV=GV: e.activation(GV[:, :], YV[:, :], AF.Silu),
              reads=[kYV], writes=[kGV])
        kb.op("dve", lambda e, GV=GV, SS=SS: e.tensor_mul(GV[:, :], GV[:, :], SS[:, :]),
              reads=[kGV, kSS], writes=[kGV])
        kb.op("dve", lambda e, GV=GV, XX=XX, OB=OB: e.tensor_add(OB[:, :], GV[:, :], XX[:, :]),
              reads=[kGV, kXX], writes=[kOB])
        kb.dma(s_out(ic), out_h[tok, :], OB[:, :], reads=[kOB], writes=["h:out"])

    kb.finish()
    es.close()
    return nc



def _consts(T, ST):
    bf = ml_dtypes.bfloat16
    c = {}
    c["ident"] = np.eye(128, dtype=bf)
    s = np.arange(128)[:, None]; t = np.arange(128)[None, :]
    c["maskA"] = (s <= t).astype(bf)
    c["tril128s"] = (s < t).astype(np.float32)
    c["ones1x128"] = np.ones((1, 128), np.float32)
    c["onescol"] = np.ones((128, 1), np.float32)
    c["iotaE"] = np.tile(np.arange(8, dtype=np.float32)[None, :], (128, 1))
    c["iota_col"] = np.arange(128, dtype=np.float32)[:, None]
    c["ones8"] = np.ones((128, 8), np.float32)
    sm = np.ones((128, 4 * ST), ml_dtypes.bfloat16)
    sm[:, ::C] = 0.0
    c["scanmask"] = sm
    c["ones8col"] = np.ones((8, 1), np.float32)
    NCc = T // C
    rm = np.ones((1, 8 * NCc), ml_dtypes.bfloat16)
    rm[0, ::NCc] = 0.0
    c["rowmask"] = rm
    c["identF"] = np.eye(8, dtype=np.float32)
    fc = np.zeros((128, 8), np.float32)
    fc[:, 0] = 1e-5; fc[:, 1] = 1e-6; fc[:, 2] = 1.0
    c["fconst"] = fc
    return c


def _core_inputs(inputs, core, T, ST):
    i = inputs
    g = lambda k: np.ascontiguousarray(np.asarray(i[k]))
    d = dict(
        x=g("L0_attn_hidden_states")[core],
        wq=g("L0_attn_Wq"), wk=g("L0_attn_Wk"), wv=g("L0_attn_Wv"),
        wgo=g("L0_attn_Wg"), wgk1=g("L0_attn_Wgk1"), wgk2=g("L0_attn_Wgk2"),
        wo=g("L0_attn_Wo"), wgate=g("L0_attn_W_gate"), wup=g("L0_attn_W_up"),
        wdown=g("L0_attn_W_down"), mwg=g("L0_ffn_wg"),
        w1=g("L0_ffn_w1"), w2=g("L0_ffn_w2"), b2=g("L0_ffn_b2"),
        anw_l=np.ascontiguousarray(g("L0_attn_attn_norm_w").reshape(8, 128).T),
        mnw_l=np.ascontiguousarray(g("L0_attn_mlp_norm_w").reshape(8, 128).T),
        gnw_l=np.ascontiguousarray(np.tile(g("L0_attn_gnorm_w"), NH).reshape(8, 128).T),
        bgk2_l=np.ascontiguousarray(g("L0_attn_bgk2").reshape(4, 128).T),
        b1_l=np.ascontiguousarray(g("L0_ffn_b1").reshape(E, HMOE // 128, 128).transpose(0, 2, 1)),
        core_oh=np.eye(8, dtype=np.float32)[core][None, :],
        core_pm=(np.arange(8) < core).astype(np.float32)[:, None],
    )
    d.update(_consts(T, ST))
    return d


def kernel(**inputs):
    T = 8192
    nc = build_nc(T=T)
    in_maps = [_core_inputs(inputs, b, T, 512) for b in range(NCORE)]
    res = run_bass_kernel_spmd(nc, in_maps, core_ids=list(range(NCORE)))
    out = np.stack([r["out"] for r in res.results], axis=0)
    return out


# version 39
# speedup vs baseline: 1.0124x; 1.0124x over previous
import math
import contextlib
import numpy as np
import ml_dtypes

import concourse.bass as bass
import concourse.mybir as mybir
from concourse.bass_utils import run_bass_kernel_spmd
from concourse.bass_core_rust import IndirectOffsetOnAxis

F32 = mybir.dt.float32
BF16 = mybir.dt.bfloat16
I32 = mybir.dt.int32
AF = mybir.ActivationFunctionType
ALU = mybir.AluOpType
AXX = mybir.AxisListType.X

D = 1024
NH = 4
DK = 128
DV = 256
DKQ = 512
R = 16
IMLP = 2816
E = 8
HMOE = 4096
C = 128
S16 = 1.0 / 16.0
SCL = DK ** -0.5
EPS_LN = 1e-5
EPS_RMS = 1e-6
NCORE = 8

ENG = ("pe", "dve", "act", "pool", "sp")
QUEUE_ENG = {"spq": "sp", "actq": "act", "poolq": "pool"}
ENG_ATTR = {"pe": "tensor", "dve": "vector", "act": "scalar", "pool": "gpsimd", "sp": "sync"}


class KB:

    def __init__(self, nc, es):
        self.nc = nc
        self.es = es
        self.e = {k: getattr(nc, v) for k, v in ENG_ATTR.items()}
        self.sem = {}
        for name in ENG + ("cc",):
            self.sem[name] = es.enter_context(nc.semaphore(f"s_{name}"))
        self.cnt = {name: 0 for name in self.sem}
        self.w = {}
        self.r = {}
        self.waited = {en: {s: 0 for s in self.sem} for en in ENG}
        self.dma_sems = set()
        self.stream_eng = {}

    def stream(self, name, eng, npar=2):
        for i in range(npar):
            sn = f"{name}{i}"
            if sn in self.sem:
                continue
            self.sem[sn] = self.es.enter_context(self.nc.semaphore(f"s_{sn}"))
            self.cnt[sn] = 0
            self.dma_sems.add(sn)
            self.stream_eng[sn] = eng
            for en in ENG:
                self.waited[en][sn] = 0
        return lambda idx: f"{name}{idx % npar}"

    def _wait(self, eng, sem, val):
        if self.waited[eng][sem] >= val:
            return
        self.e[eng].wait_ge(self.sem[sem], val)
        self.waited[eng][sem] = val

    def _edge(self, eng, sem, val):
        if sem in self.dma_sems:
            val = self.cnt[sem]
        self._wait(eng, sem, val)

    def _pre(self, eng, reads, writes):
        for rg in reads:
            lw = self.w.get(rg)
            if lw is not None:
                self._edge(eng, *lw)
        for rg in writes:
            lw = self.w.get(rg)
            if lw is not None:
                self._edge(eng, *lw)
            for rr in self.r.get(rg, ()):
                self._edge(eng, *rr)

    def _post(self, sem, reads, writes):
        cn = self.cnt[sem]
        for rg in reads:
            self.r.setdefault(rg, []).append((sem, cn))
        for rg in writes:
            self.w[rg] = (sem, cn)
            self.r[rg] = []

    def op(self, eng, fn, reads=(), writes=()):
        self._pre(eng, reads, writes)
        inst = fn(self.e[eng])
        self.cnt[eng] += 1
        inst.then_inc(self.sem[eng], 1)
        self._post(eng, reads, writes)
        return inst

    def mmg(self, fn_list, reads=(), writes=(), first_start=True, last_stop=True):
        self._pre("pe", reads, writes)
        keys = [(ps.tensor.name, ps.offset) for ps, _, _ in fn_list]
        last = {k: i for i, k in enumerate(keys)}
        seen = set()
        for i, (ps, lh, rh) in enumerate(fn_list):
            k = keys[i]
            st = first_start and (k not in seen)
            seen.add(k)
            inst = self.e["pe"].matmul(
                ps, lh, rh, start=st, stop=(last_stop and last[k] == i))
        self.cnt["pe"] += 1
        inst.then_inc(self.sem["pe"], 1)
        self._post("pe", reads, writes)

    def transpose(self, out_ps, in_ap, ident_ap, reads=(), writes=()):
        self._pre("pe", reads, writes)
        inst = self.e["pe"].transpose(out_ps, in_ap, ident_ap)
        self.cnt["pe"] += 1
        inst.then_inc(self.sem["pe"], 1)
        self._post("pe", reads, writes)

    def dma(self, sem, dst, src, reads=(), writes=(), **kw):
        eng = self.stream_eng[sem]
        self._pre(eng, reads, writes)
        inst = self.e[eng].dma_start(dst, src, **kw)
        self.cnt[sem] += 16
        inst.then_inc(self.sem[sem], 16)
        self._post(sem, reads, writes)

    def idma(self, sem, dst, dst_off, src, src_off, reads=(), writes=(), **kw):
        assert self.stream_eng[sem] == "pool"
        self._pre("pool", reads, writes)
        inst = self.e["pool"].indirect_dma_start(dst, dst_off, src, src_off, **kw)
        self.cnt[sem] += 16
        inst.then_inc(self.sem[sem], 16)
        self._post(sem, reads, writes)

    def collective(self, kind, op, groups, ins, outs, reads=(), writes=()):
        self._pre("pool", reads, writes)
        inst = self.e["pool"].collective_compute(
            kind, op, replica_groups=groups, ins=ins, outs=outs)
        self.cnt["cc"] += 1
        inst.then_inc(self.sem["cc"])
        self._post("cc", reads, writes)

    def barrier(self):
        for sn in self.dma_sems:
            self._wait(self.stream_eng[sn], sn, self.cnt[sn])
        self.nc.all_engine_barrier()
        for en in ENG:
            for s in self.sem:
                if s != "cc":
                    self.waited[en][s] = self.cnt[s]

    def finish(self):
        self._wait("pool", "cc", self.cnt["cc"])
        self.barrier()


def _ap_bcast_free(t, part, grp, inner):
    base = t[:].ap
    pitch = base[0][0]
    return bass.AP(t, 0, [[pitch, part], [0, grp], [1, inner]])


def build_nc(T=8192, ST=512, CAP=2432, NBLK=512, GCAP=16384, n_cores=8, sim=False):
    assert T % C == 0 and T % ST == 0 and ST % C == 0
    NC = T // C
    NST = T // ST
    CST = ST // C
    NCHIP = n_cores
    SLOTS = E * CAP

    nc = bass.Bass("TRN2", target_bir_lowering=False)
    es = contextlib.ExitStack()

    din = {}
    def inp(name, shape, dt=F32):
        din[name] = nc.dram_tensor(name, list(shape), dt, kind="ExternalInput")
        return din[name]

    x_h = inp("x", (T, D))
    wq_h = inp("wq", (D, DKQ)); wk_h = inp("wk", (D, DKQ))
    wv_h = inp("wv", (D, D)); wgo_h = inp("wgo", (D, D))
    wgk1_h = inp("wgk1", (D, R)); wgk2_h = inp("wgk2", (R, DKQ))
    wo_h = inp("wo", (D, D))
    wgate_h = inp("wgate", (D, IMLP)); wup_h = inp("wup", (D, IMLP))
    wdown_h = inp("wdown", (IMLP, D))
    mwg_h = inp("mwg", (D, E))
    w1_h = inp("w1", (E, D, HMOE)); w2_h = inp("w2", (E, HMOE, D))
    anw_h = inp("anw_l", (128, 8)); mnw_h = inp("mnw_l", (128, 8))
    gnw_h = inp("gnw_l", (128, 8)); bgk2_h = inp("bgk2_l", (128, 4))
    b1_h = inp("b1_l", (E, 128, HMOE // 128)); b2_h = inp("b2", (E, D))
    ident_h = inp("ident", (128, 128), BF16)
    maskA_h = inp("maskA", (128, 128), BF16)
    tril128s_h = inp("tril128s", (128, 128), F32)
    ones1x128_h = inp("ones1x128", (1, 128), F32)
    onescol_h = inp("onescol", (128, 1), F32)
    iotaE_h = inp("iotaE", (128, 8))
    iota_col_h = inp("iota_col", (128, 1))
    ones8_h = inp("ones8", (128, 8))
    scanmask_h = inp("scanmask", (128, NH * ST), BF16)
    rowmask_h = inp("rowmask", (1, 8 * (T // C)), BF16)
    identF_h = inp("identF", (8, 8))
    core_oh_h = inp("core_oh", (1, 8))
    fconst_h = inp("fconst", (128, 8))
    core_pm_h = inp("core_pm", (8, 1))
    ones8col_h = inp("ones8col", (8, 1))

    out_h = nc.dram_tensor("out", [T, D], F32, kind="ExternalOutput")

    hbm_xn = nc.dram_tensor("hbm_xn", [T, D], BF16)
    hbm_hpre = nc.dram_tensor("hbm_hpre", [T, D], BF16)
    hbm_x2 = nc.dram_tensor("hbm_x2", [T, D], BF16)
    hbm_npre = nc.dram_tensor("hbm_npre", [T, D], BF16)
    hbm_s = nc.dram_tensor("hbm_s", [T, D], BF16)
    hbm_xe = nc.dram_tensor("hbm_xe", [SLOTS, D], BF16)
    hbm_ye = nc.dram_tensor("hbm_ye", [SLOTS, D], BF16)
    hbm_tok = nc.dram_tensor("hbm_tok", [SLOTS, 1], I32)
    hbm_ccin = nc.dram_tensor("hbm_ccin", [8, 16], F32)
    hbm_ccout = nc.dram_tensor("hbm_ccout", [8, 16], F32)

    kb = KB(nc, es)
    sb = lambda name, shape, dt: es.enter_context(nc.sbuf_tensor(name, list(shape), dt))
    psum = lambda name, shape, dt=F32: es.enter_context(nc.psum_tensor(name, list(shape), dt))
    def mk_scope():
        st_ = contextlib.ExitStack()
        return st_, (lambda name, shape, dt: st_.enter_context(
            nc.sbuf_tensor(name, list(shape), dt)))

    s_const = kb.stream("qconst", "sp", 1)
    s_ldx = kb.stream("qldx", "pool", 2)
    s_stxn = kb.stream("qstxn", "act", 2)
    s_sthp = kb.stream("qsthp", "act", 2)

    ident = sb("k_ident", (128, 128), BF16)
    maskA = sb("k_maskA", (128, 128), BF16)
    tril128s = sb("k_tril128s", (128, 128), F32)
    ones1x128 = sb("k_ones1x128", (1, 128), F32)
    onescol = sb("k_onescol", (128, 1), F32)
    iotaE = sb("k_iotaE", (128, 8), F32)
    iota_col = sb("k_iota_col", (128, 1), F32)
    ones8 = sb("k_ones8", (128, 8), F32)
    rowmask = sb("k_rowmask", (1, 8 * NC), BF16)
    identF = sb("k_identF", (8, 8), F32)
    core_oh = sb("k_core_oh", (1, 8), F32)
    core_pm = sb("k_core_pm", (8, 1), F32)
    ones8col = sb("k_ones8col", (8, 1), F32)
    anw = sb("k_anw", (128, 8), F32)
    mnw = sb("k_mnw", (128, 8), F32)
    gnw = sb("k_gnw", (128, 8), F32)
    bgk2 = sb("k_bgk2", (128, 4), F32)
    fconst = sb("k_fconst", (128, 8), F32)
    B_EPSLN = fconst[:, 0:1]; B_EPSRMS = fconst[:, 1:2]; B_ONE = fconst[:, 2:3]

    NCT = NC
    sig_h = sb("sig_h", (128, NCT), F32)
    g0t = sb("g0t", (128, NCT), F32)
    g1t = sb("g1t", (128, NCT), F32)
    e0t = sb("e0t", (128, NCT), F32)
    e1t = sb("e1t", (128, NCT), F32)
    oh0s = sb("oh0s", (128, NCT * 8), F32)
    oh1s = sb("oh1s", (128, NCT * 8), F32)
    tabC0 = sb("tabC0", (1, 8 * NC), F32)
    tabC1 = sb("tabC1", (1, 8 * NC), F32)
    tabC01 = sb("tabC01", (1, 8 * NC), F32)
    slot0t = sb("slot0t", (128, NCT), F32)
    slot1t = sb("slot1t", (128, NCT), F32)
    pre0t = sb("pre0t", (128, NCT), F32)
    pre1t = sb("pre1t", (128, NCT), F32)
    slotg0i = sb("slotg0i", (128, NCT), I32)
    slotg1i = sb("slotg1i", (128, NCT), I32)
    g0vt = sb("g0vt", (128, NCT), F32)
    g1vt = sb("g1vt", (128, NCT), F32)
    cnt_loc = sb("cnt_loc", (1, 16), F32)
    gb0 = sb("gb0", (128, 8), F32)
    gb1 = sb("gb1", (128, 8), F32)
    jt8 = sb("jt8", (128, 8), F32)
    lgtall = sb("lgtall", (128, NCT * 8), F32)
    rc0 = sb("rc0", (128, 1), F32)
    rc1 = sb("rc1", (128, 1), F32)
    rc2 = sb("rc2", (128, 1), F32)
    ccin_sb = sb("ccin_sb", (8, 16), F32)
    ccout_sb = sb("ccout_sb", (8, 16), F32)
    csc = sb("csc", (1, 16), F32)

    for t, h in ((ident, ident_h), (maskA, maskA_h), (tril128s, tril128s_h),
                 (ones1x128, ones1x128_h), (onescol, onescol_h),
                 (iotaE, iotaE_h), (iota_col, iota_col_h), (ones8, ones8_h),
                 (rowmask, rowmask_h), (identF, identF_h),
                 (core_oh, core_oh_h), (core_pm, core_pm_h),
                 (ones8col, ones8col_h), (fconst, fconst_h),
                 (anw, anw_h), (mnw, mnw_h), (gnw, gnw_h), (bgk2, bgk2_h)):
        kb.dma(s_const(0), t[:], h[:], writes=[f"c:{t.name}"])


    es2, sb2 = mk_scope()
    s_ldw = kb.stream("qldw", "pool", 2)
    wq_sb = sb2("wq_sb", (128, 8 * DKQ), BF16)
    wk_sb = sb2("wk_sb", (128, 8 * DKQ), BF16)
    wv_sb = sb2("wv_sb", (128, 8 * D), BF16)
    wgo_sb = sb2("wgo_sb", (128, 8 * D), BF16)
    wgk1_sb = sb2("wgk1_sb", (128, 8 * R), BF16)
    wgk2_sb = sb2("wgk2_sb", (16, DKQ), BF16)
    mwg_sb = sb2("mwg_sb", (128, 8 * E), BF16)
    wspec = ((wq_sb, wq_h, "wq", DKQ, True), (wk_sb, wk_h, "wk", DKQ, True),
             (wv_sb, wv_h, "wv", D, True), (wgo_sb, wgo_h, "wgo", D, True),
             (wgk1_sb, wgk1_h, "wgk1", R, True), (mwg_sb, mwg_h, "mwg", E, False))
    for wi, (wsb, wh, wid, cols, scaled) in enumerate(wspec):
        kb.dma(s_ldw(wi), wsb[:, :],
               bass.AP(wh, 0, [[cols, 128], [128 * cols, 8], [1, cols]]),
               writes=[f"w:{wid}:{ci}" for ci in range(8)])
    kb.dma(s_ldw(0), wgk2_sb[:, :], wgk2_h[:, :], writes=["w:wgk2"])
    wo_sb = sb2("wo_sb", (128, 8 * D), BF16)
    kb.dma(s_ldw(6), wo_sb[:, :],
           bass.AP(wo_h, 0, [[D, 128], [128 * D, 8], [1, D]]),
           writes=[f"w:wo:{ci}" for ci in range(8)])
    for ci in range(8):
        for (wsb, wh, wid, cols, scaled) in wspec:
            if scaled:
                sl = wsb[:, ci * cols:(ci + 1) * cols]
                kb.op("dve", lambda e, sl=sl, ci=ci: e.tensor_scalar_mul(
                    sl, sl, anw[:, ci:ci + 1]),
                    reads=[f"w:{wid}:{ci}", "c:k_anw"], writes=[f"w:{wid}:{ci}"])
        kb.op("dve", lambda e, ci=ci: e.tensor_scalar_mul(
            wo_sb[:, ci * D:(ci + 1) * D], wo_sb[:, ci * D:(ci + 1) * D],
            gnw[:, ci:ci + 1]),
            reads=[f"w:wo:{ci}", "c:k_gnw"], writes=[f"w:wo:{ci}"])

    NB = 2
    xin = [sb2(f"p1x{i}", (128, D), BF16) for i in range(NB)]
    xnout = [sb2(f"p1xn{i}", (128, D), BF16) for i in range(NB)]
    hpout = [sb2(f"p1hp{i}", (128, D), BF16) for i in range(NB)]
    sqscr_one = sb2("p1sq", (128, D), BF16)
    st1_l = [sb2(f"p1st{i}", (128, 8), F32) for i in range(NB)]

    def emit_p1_tile(i):
        b = i % NB
        xt, xnt, hpt = xin[b], xnout[b], hpout[b]
        st1, sqscr = st1_l[b], sqscr_one
        rx, rxn, rhp = f"p1x{b}", f"p1xn{b}", f"p1hp{b}"
        rst, rsq = f"p1st{b}", "p1sq"
        tok = slice(i * 128, (i + 1) * 128)
        kb.dma(s_ldx(i), xt[:], x_h[tok, :], writes=[rx])
        kb.op("act", lambda e: e.activation(sqscr[:], xt[:], AF.Identity,
                                            accum_out=st1[:, 0:1]),
              reads=[rx], writes=[rsq, rst])
        kb.op("act", lambda e: e.activation(sqscr[:], xt[:], AF.Square,
                                            accum_out=st1[:, 1:2]),
              reads=[rx], writes=[rsq, rst])
        kb.op("dve", lambda e: e.tensor_scalar(st1[:, 2:3], st1[:, 0:1], 1.0 / D, None, ALU.mult),
              reads=[rst], writes=[rst])
        kb.op("dve", lambda e: e.tensor_tensor(st1[:, 3:4], st1[:, 2:3], st1[:, 2:3], ALU.mult),
              reads=[rst], writes=[rst])
        kb.op("dve", lambda e: e.scalar_tensor_tensor(st1[:, 3:4], st1[:, 1:2], 1.0 / D,
                                                      st1[:, 3:4], ALU.mult, ALU.subtract),
              reads=[rst], writes=[rst])
        kb.op("act", lambda e: e.activation(st1[:, 4:5], st1[:, 3:4], AF.Ln, bias=B_EPSLN),
              reads=[rst, "c:k_fconst"], writes=[rst])
        kb.op("act", lambda e: e.activation(st1[:, 4:5], st1[:, 4:5], AF.Exp, scale=-0.5),
              reads=[rst], writes=[rst])
        kb.op("dve", lambda e: e.tensor_tensor(st1[:, 5:6], st1[:, 4:5], st1[:, 4:5], ALU.mult),
              reads=[rst], writes=[rst])
        kb.op("dve", lambda e: e.tensor_tensor(st1[:, 5:6], st1[:, 5:6], st1[:, 3:4], ALU.mult),
              reads=[rst], writes=[rst])
        kb.op("act", lambda e: e.activation(st1[:, 5:6], st1[:, 5:6], AF.Ln, bias=B_EPSRMS),
              reads=[rst, "c:k_fconst"], writes=[rst])
        kb.op("act", lambda e: e.activation(st1[:, 6:7], st1[:, 5:6], AF.Exp, scale=0.5),
              reads=[rst], writes=[rst])
        kb.op("act", lambda e: e.activation(st1[:, 5:6], st1[:, 5:6], AF.Exp, scale=-0.5),
              reads=[rst], writes=[rst])
        kb.op("dve", lambda e: e.tensor_copy(sig_h[:, i:i + 1], st1[:, 6:7]),
              reads=[rst], writes=[f"m:sig_h:{i}"])
        kb.op("dve", lambda e: e.tensor_tensor(st1[:, 7:8], st1[:, 4:5], st1[:, 5:6], ALU.mult),
              reads=[rst], writes=[rst])
        kb.op("dve", lambda e: e.tensor_scalar(xnt[:], xt[:], st1[:, 2:3], st1[:, 4:5],
                                               ALU.subtract, ALU.mult),
              reads=[rx, rst], writes=[rxn])
        kb.op("dve", lambda e: e.tensor_scalar(hpt[:], xt[:], st1[:, 2:3], st1[:, 7:8],
                                               ALU.subtract, ALU.mult),
              reads=[rx, rst], writes=[rhp])
        kb.dma(s_stxn(i), hbm_xn[tok, :], xnt[:], reads=[rxn], writes=[f"h:xn:{i}"])
        kb.dma(s_sthp(i), hbm_hpre[tok, :], hpt[:], reads=[rhp], writes=[f"h:hp:{i}"])


    s_ldh = kb.stream("qldh", "sp", 2)
    s_const2 = kb.stream("qconst2", "sp", 1)
    scanmask = sb2("k_scanmask", (128, NH * ST), BF16)
    kb.dma(s_const2(0), scanmask[:], scanmask_h[:], writes=["c:k_scanmask"])
    hpreF_l = [sb2(f"hpreF{i}", (128, 8 * ST), BF16) for i in range(2)]
    Gt = sb2("Gt", (128, NH * ST), F32)
    bt = Gt
    vt = sb2("vt", (128, CST * D), BF16)
    gout = sb2("gout", (128, CST * D), BF16)
    og = gout
    qe_t = [sb2(f"qe_t{i}", (128, ST), BF16) for i in range(2)]
    kt_t = [sb2(f"kt_t{i}", (128, ST), BF16) for i in range(2)]
    kd_t = [sb2(f"kd_t{i}", (128, ST), BF16) for i in range(2)]
    kdT_t = [sb2(f"kdT_t{i}", (128, ST), BF16) for i in range(2)]
    sA = sb2("sA", (128, NH * ST), F32)
    sB = sb2("sB", (128, NH * ST), F32)
    gk1b = sb2("gk1b", (16, ST), BF16)
    Ut_l = [sb2(f"Ut{i}", (128, DV * CST), F32) for i in range(2)]
    Ssc_l = Ut_l
    d0m_one = sb2("d0m", (128, DV * CST), F32)
    d0m_l = [d0m_one, d0m_one]
    lamALL = sb2("lamALL", (128, NH * CST), F32)
    lamm_one = sb2("lamm", (128, CST), F32)
    lamm = [lamm_one, lamm_one]
    Sprev = sb2("Sprev", (128, NH * DV), F32)
    Sbc_one = sb2("Sbcp", (128, DV), BF16)
    Sbc = [Sbc_one, Sbc_one]
    Sbc0_l = [sb2(f"Sbc0_{i}", (128, DV), BF16) for i in range(2)]
    asb = [sb2(f"asb{i}", (128, 128), BF16) for i in range(2)]
    jt256 = sqscr_one
    sst = sb2("sst", (128, CST * NH), F32)
    rstd4 = [sb2(f"rstd4_{i}", (128, NH), F32) for i in range(2)]
    lg = sb2("lg", (8, ST), F32)

    pA = psum("pA", (128, 512))
    pB = psum("pB", (128, 512))
    pC = psum("pC", (128, 512))
    pD = psum("pD", (128, 512))
    pO0 = psum("pO0", (128, DV))
    pO1 = psum("pO1", (128, DV))
    pAA = psum("pAA", (128, 128))
    pAA2 = pAA
    pTR = psum("pTR", (128, 128), BF16)
    pTR2 = pTR

    U_pitch = DV * CST

    for _h in range(NH):
        kb.op("dve", lambda e, _h=_h: e.memset(Sprev[:, _h * DV:(_h + 1) * DV], 0.0),
              writes=[f"Sprev:{_h}"])
    assert ST <= 512

    def emit_p2_st(st):
        t0 = st * ST
        hpreF = hpreF_l[st % 2]
        hpk = f"hpreF{st % 2}"
        hp_r = [f"h:hp:{j}" for j in range(st * CST, (st + 1) * CST)]
        for ci in range(8):
            kb.dma(s_ldh(st), hpreF[:, ci * ST:(ci + 1) * ST],
                   hbm_hpre[t0:t0 + ST, ci * 128:(ci + 1) * 128],
                   reads=hp_r, writes=[hpk], transpose=True)

        for c in range(CST):
            lhs = lambda ci, c=c: hpreF[:, ci * ST + c * C: ci * ST + (c + 1) * C]
            pL_, pH_, kL, kH = (pC, pD, "pC", "pD") if c % 2 == 0 else (pA, pB, "pA", "pB")
            for (wsb, wid, dst, dk_) in ((wv_sb, "wv", vt, "vt"), (wgo_sb, "wgo", gout, "gout")):
                mml = []
                for half in range(2):
                    for ci in range(8):
                        mml.append((
                            (pL_ if half == 0 else pH_)[:, 0:512],
                            lhs(ci),
                            wsb[:, ci * D + half * 512: ci * D + half * 512 + 512]))
                kb.mmg(mml, reads=[hpk] + [f"w:{wid}:{ci}" for ci in range(8)],
                       writes=[kL, kH])
                for half, ps, pk in ((0, pL_, kL), (1, pH_, kH)):
                    if half == 0:
                        kb.op("dve", lambda e, ps=ps, dst=dst, c=c, half=half: e.tensor_copy(
                            dst[:, c * D + half * 512: c * D + half * 512 + 512], ps[:, 0:512]),
                            reads=[pk], writes=[f"{dk_}:{c}"])
                    else:
                        kb.op("act", lambda e, ps=ps, dst=dst, c=c, half=half: e.activation(
                            dst[:, c * D + half * 512: c * D + half * 512 + 512], ps[:, 0:512],
                            AF.Identity),
                            reads=[pk], writes=[f"{dk_}:{c}"])
            kb.op("act", lambda e, c=c: e.activation(
                gout[:, c * D:(c + 1) * D], gout[:, c * D:(c + 1) * D], AF.Silu),
                reads=[f"gout:{c}"], writes=[f"gout:{c}"])
            yield

        kb.mmg([(pA[0:16, 0:ST], wgk1_sb[:, ci * R:(ci + 1) * R],
                 hpreF[:, ci * ST:(ci + 1) * ST]) for ci in range(8)],
               reads=[hpk] + [f"w:wgk1:{ci}" for ci in range(8)], writes=["pA"])
        kb.op("dve", lambda e: e.tensor_copy(gk1b[:, :], pA[0:16, 0:ST]),
              reads=["pA"], writes=["gk1b"])
        yield
        GK = NH * ST
        GTW = [f"Gt:{h}" for h in range(NH)]
        for h in range(NH):
            hS = h * ST
            pgk, kgk = (pB, "pB") if h % 2 == 0 else (pD, "pD")
            kb.mmg([(pgk[:, 0:ST], wgk2_sb[:, h * 128:(h + 1) * 128], gk1b[:, :])],
                   reads=["gk1b", "w:wgk2"], writes=[kgk])
            kb.op("act", lambda e, h=h, hS=hS, pgk=pgk: e.activation(
                sA[:, hS:hS + ST], pgk[:, 0:ST], AF.Identity, bias=bgk2[:, h:h + 1]),
                reads=[kgk, "c:k_bgk2"], writes=["sA"])
            yield
        kb.op("act", lambda e: e.activation(sB[:, 0:GK], sA[:, 0:GK], AF.Abs),
              reads=["sA"], writes=["sB"])
        kb.op("act", lambda e: e.activation(sB[:, 0:GK], sB[:, 0:GK], AF.Exp, scale=-1.0),
              reads=["sB"], writes=["sB"])
        kb.op("act", lambda e: e.activation(sB[:, 0:GK], sB[:, 0:GK], AF.Ln, bias=B_ONE),
              reads=["sB", "c:k_fconst"], writes=["sB"])
        kb.op("dve", lambda e: e.scalar_tensor_tensor(
            Gt[:, 0:GK], sA[:, 0:GK], 0.0, sB[:, 0:GK], ALU.min, ALU.subtract),
            reads=["sA", "sB"], writes=GTW)
        yield
        kb.op("dve", lambda e: e.tensor_tensor_scan(
            bt[:, 0:GK], scanmask[:, 0:GK], Gt[:, 0:GK], 0.0, ALU.mult, ALU.add),
            reads=GTW + ["c:k_scanmask"], writes=GTW)
        bl_view = bass.AP(bt, C - 1, [[NH * ST, 128], [ST, NH], [C, CST]])
        kb.op("dve", lambda e, bl_view=bl_view: e.tensor_scalar_mul(
            lamALL[:, :], bl_view, S16), reads=GTW, writes=["lamALL"])
        kb.op("act", lambda e: e.activation(lamALL[:, :], lamALL[:, :], AF.Exp),
              reads=["lamALL"], writes=["lamALL"])
        yield
        kb.op("act", lambda e: e.activation(sA[:, 0:GK], bt[:, 0:GK], AF.Exp, scale=S16),
              reads=GTW, writes=["sA"])
        kb.op("act", lambda e: e.activation(sB[:, 0:GK], bt[:, 0:GK], AF.Exp, scale=-S16),
              reads=GTW, writes=["sB"])
        yield

        def gen_gla(h):
            hS = h * ST
            hp_ = h % 2
            qe_, kt_, kd_, kdT_ = qe_t[hp_], kt_t[hp_], kd_t[hp_], kdT_t[hp_]
            rqe, rkt, rkd, rkdT = (f"qe_t{hp_}", f"kt_t{hp_}", f"kd_t{hp_}", f"kdT_t{hp_}")
            lamm_ = lamm_one
            rlam, rlamm = "lamALL", "lamm"
            pq, kq = (pA, "pA") if hp_ == 0 else (pC, "pC")
            pk_, kk_ = (pB, "pB") if hp_ == 0 else (pD, "pD")
            Ut, Ssc, d0m = Ut_l[hp_], Ssc_l[hp_], d0m_l[hp_]
            Sbc0 = Sbc0_l[hp_]
            kSb0 = f"Sbc0_{hp_}"
            kUt = kSsc = f"Ut{hp_}"
            kd0 = "d0m"
            kb.mmg([(pq[:, 0:ST], wq_sb[:, ci * DKQ + h * 128: ci * DKQ + (h + 1) * 128],
                     hpreF[:, ci * ST:(ci + 1) * ST]) for ci in range(8)],
                   reads=[hpk] + [f"w:wq:{ci}" for ci in range(8)], writes=[kq])
            kb.op("dve", lambda e, hS=hS, qe_=qe_, pq=pq: e.scalar_tensor_tensor(
                qe_[:, 0:ST], pq[:, 0:ST], SCL, sA[:, hS:hS + ST], ALU.mult, ALU.mult),
                reads=[kq, "sA"], writes=[rqe])
            kb.mmg([(pk_[:, 0:ST], wk_sb[:, ci * DKQ + h * 128: ci * DKQ + (h + 1) * 128],
                     hpreF[:, ci * ST:(ci + 1) * ST]) for ci in range(8)],
                   reads=[hpk] + [f"w:wk:{ci}" for ci in range(8)], writes=[kk_])
            kb.op("dve", lambda e, hS=hS, kt_=kt_, pk_=pk_: e.tensor_mul(
                kt_[:, 0:ST], pk_[:, 0:ST], sB[:, hS:hS + ST]),
                reads=[kk_, "sB"], writes=[rkt])
            yield
            for c in range(CST):
                kb.op("dve", lambda e, c=c, h=h, kd_=kd_, kt_=kt_: e.tensor_scalar_mul(
                    kd_[:, c * C:(c + 1) * C], kt_[:, c * C:(c + 1) * C],
                    lamALL[:, h * CST + c: h * CST + c + 1]),
                    reads=[rkt, rlam], writes=[rkd])
                yield
            for c in range(CST):
                ptr, ktr = (pTR, "pTR")
                kb.transpose(ptr[:, :], kd_[:, c * C:(c + 1) * C], ident[:, :],
                             reads=[rkd, "c:k_ident"], writes=[ktr])
                kb.op("dve", lambda e, c=c, ptr=ptr, kdT_=kdT_: e.tensor_copy(
                    kdT_[:, c * C:(c + 1) * C], ptr[:, :]),
                    reads=[ktr], writes=[rkdT])
                yield
            kb.op("act", lambda e, h=h: e.activation(
                Sbc0[:, :], Sprev[:, h * DV:(h + 1) * DV], AF.Identity),
                reads=[f"Sprev:{h}"], writes=[kSb0])
            pu, ku = (pO0, "pO0") if hp_ == 0 else (pO1, "pO1")
            for c in range(CST):
                kb.mmg([(pu[:, 0:DV], kdT_[:, c * C:(c + 1) * C],
                         vt[:, c * D + h * DV: c * D + (h + 1) * DV])],
                       reads=[rkdT, f"vt:{c}"], writes=[ku])
                udst = bass.AP(Ut, c, [[U_pitch, 128], [CST, DV]])
                if c == 0:
                    kb.op("dve", lambda e, udst=udst, h=h, pu=pu: e.scalar_tensor_tensor(
                        udst, Sprev[:, h * DV:(h + 1) * DV],
                        lamALL[:, h * CST: h * CST + 1],
                        pu[:, 0:DV], ALU.mult, ALU.add),
                        reads=[ku, f"Sprev:{h}", rlam], writes=[kUt])
                else:
                    kb.op("act", lambda e, udst=udst, pu=pu: e.activation(
                        udst, pu[:, 0:DV], AF.Identity),
                        reads=[ku], writes=[kUt])
                yield
            kb.op("dve", lambda e, h=h, lamm_=lamm_: e.tensor_copy(
                lamm_[:, 0:CST], lamALL[:, h * CST:(h + 1) * CST]),
                reads=[rlam], writes=[rlamm])
            kb.op("dve", lambda e, lamm_=lamm_: e.memset(lamm_[:, 0:1], 0.0), writes=[rlamm])
            d0v = bass.AP(d0m, 0, [[U_pitch, 128], [CST, DV], [1, CST]])
            lmv = bass.AP(lamm_, 0, [[CST, 128], [0, DV], [1, CST]])
            kb.op("dve", lambda e, d0v=d0v, lmv=lmv: e.tensor_copy(d0v, lmv),
                  reads=[rlamm], writes=[kd0])
            kb.op("dve", lambda e, d0m=d0m, Ssc=Ssc, Ut=Ut: e.tensor_tensor_scan(
                Ssc[:, 0:U_pitch], d0m[:, 0:U_pitch], Ut[:, 0:U_pitch], 0.0,
                ALU.mult, ALU.add),
                reads=[kd0, kUt], writes=[kSsc])
            yield
            kb.op("act", lambda e, h=h: e.activation(
                Sprev[:, h * DV:(h + 1) * DV],
                bass.AP(Ssc, CST - 1, [[U_pitch, 128], [CST, DV]]), AF.Identity),
                reads=[kSsc], writes=[f"Sprev:{h}"])
            yield
            for c in range(CST):
                pa_, ka_ = (pAA, "pAA")
                ai = c % 2
                as_, kas = asb[ai], f"asb{ai}"
                kb.mmg([(pa_[:, :], kt_[:, c * C:(c + 1) * C],
                         qe_[:, c * C:(c + 1) * C])],
                       reads=[rkt, rqe], writes=[ka_])
                kb.op("dve", lambda e, pa_=pa_, as_=as_: e.tensor_mul(
                    as_[:, :], pa_[:, :], maskA[:, :]),
                    reads=[ka_, "c:k_maskA"], writes=[kas])
                if c == 0:
                    sb_reads = [kSb0]
                    sbv = Sbc0[:, :]
                else:
                    sbci = Sbc_one
                    kb.op("act", lambda e, c=c, sbci=sbci: e.activation(
                        sbci[:, :], bass.AP(Ssc, c - 1, [[U_pitch, 128], [CST, DV]]),
                        AF.Identity),
                        reads=[kSsc], writes=["Sbc"])
                    sb_reads = ["Sbc"]
                    sbv = sbci[:, :]
                po, ko = pu, ku
                kb.mmg([(po[:, 0:DV], as_[:, :],
                         vt[:, c * D + h * DV: c * D + (h + 1) * DV]),
                        (po[:, 0:DV], qe_[:, c * C:(c + 1) * C], sbv)],
                       reads=[kas, f"vt:{c}", rqe] + sb_reads, writes=[ko])
                kb.op("dve", lambda e, c=c, h=h, po=po: e.tensor_mul(
                    og[:, c * D + h * DV: c * D + (h + 1) * DV], po[:, 0:DV],
                    gout[:, c * D + h * DV: c * D + (h + 1) * DV]),
                    reads=[ko], writes=[f"gout:{c}", f"poR{hp_}"])
                yield
                kb.op("act", lambda e, c=c, h=h, po=po: e.activation(
                    jt256[:, 0:DV], po[:, 0:DV], AF.Square,
                    accum_out=sst[:, c * NH + h: c * NH + h + 1]),
                    reads=[ko, f"poR{hp_}"], writes=["jt256", f"sst:{c}:{h}"])


        for hp in range(NH // 2):
            gA, gB = gen_gla(2 * hp), gen_gla(2 * hp + 1)
            live = [gA, gB]
            while live:
                for g_ in list(live):
                    try:
                        next(g_)
                    except StopIteration:
                        live.remove(g_)
                yield

        for c in range(CST):
            rs_, krs = rstd4[c % 2], f"rstd4_{c % 2}"
            sst_r = [f"sst:{c}:{h}" for h in range(NH)]
            kb.op("act", lambda e, c=c, rs_=rs_: e.activation(
                rs_[:, 0:NH], sst[:, c * NH:(c + 1) * NH], AF.Ln,
                scale=1.0 / DV, bias=B_EPSRMS),
                reads=sst_r + ["c:k_fconst"], writes=[krs])
            kb.op("act", lambda e, rs_=rs_: e.activation(rs_[:, 0:NH], rs_[:, 0:NH],
                                                AF.Exp, scale=-0.5),
                reads=[krs], writes=[krs])
            for h in range(NH):
                kb.op("dve", lambda e, c=c, h=h, rs_=rs_: e.tensor_scalar_mul(
                    og[:, c * D + h * DV: c * D + (h + 1) * DV],
                    og[:, c * D + h * DV: c * D + (h + 1) * DV],
                    rs_[:, h:h + 1]),
                    reads=[krs], writes=[f"gout:{c}"])
            yield

        kb.mmg([(pA[0:8, 0:ST], mwg_sb[:, ci * E:(ci + 1) * E],
                 hpreF[:, ci * ST:(ci + 1) * ST]) for ci in range(8)],
               reads=[hpk] + [f"w:mwg:{ci}" for ci in range(8)], writes=["pA"])
        kb.op("dve", lambda e: e.tensor_copy(lg[:, 0:ST], pA[0:8, 0:ST]),
              reads=["pA"], writes=["lg"])
        for c in range(CST):
            ic = st * CST + c
            pa_, ka_ = (pAA, "pAA")
            kb.transpose(pa_[:, 0:8], lg[:, c * C:(c + 1) * C], identF[0:8, 0:8],
                         reads=["lg", "c:k_identF"], writes=[ka_])
            kb.op("dve", lambda e, ic=ic, pa_=pa_: e.tensor_scalar_mul(
                lgtall[:, ic * 8:(ic + 1) * 8], pa_[:, 0:8], sig_h[:, ic:ic + 1]),
                reads=[ka_, f"m:sig_h:{ic}"], writes=[f"lgtall:{ic}"])
        yield

    ogF = sb2("ogF", (128, 8 * ST), BF16)
    xn_tm = [sb2(f"xn_tm{i}", (128, D), BF16) for i in range(2)]
    x2b = [sb2(f"x2b{i}", (128, D), BF16) for i in range(2)]
    npb = [sb2(f"npb{i}", (128, D), BF16) for i in range(2)]
    st3 = [sb2(f"st3_{i}", (128, 4), F32) for i in range(2)]
    sq3a = sqscr_one
    r8c = [sb2(f"r8c{i}", (128, 8), F32) for i in range(2)]
    r8d = [sb2(f"r8d{i}", (128, 8), F32) for i in range(2)]
    rcc = [sb2(f"rcc{i}", (128, 4), F32) for i in range(2)]
    jtc = [sb2(f"jtc{i}", (128, 8), F32) for i in range(2)]
    s_ldxn = kb.stream("qldxn", "sp", 2)
    s_stx2 = kb.stream("qstx2", "act", 2)
    s_stnp = kb.stream("qstnp", "act", 2)
    s_cc = kb.stream("qcc", "pool", 1)

    def emit_top2_chunk(ic):
        if True:
            cp = ic % 2
            lga = lgtall[:, ic * 8:(ic + 1) * 8]
            ra_, rb_ = r8c[cp], r8d[cp]
            kra, krb = f"r8c{cp}", f"r8d{cp}"
            rc_, jt_ = rcc[cp], jtc[cp]
            krc, kjt = f"rcc{cp}", f"jtc{cp}"
            kb.op("dve", lambda e, lga=lga, rc_=rc_: e.reduce_max(
                rc_[:, 0:1], lga, axis=AXX),
                reads=[f"lgtall:{ic}"], writes=[krc])
            kb.op("dve", lambda e, lga=lga, ra_=ra_, rc_=rc_: e.tensor_scalar(
                ra_[:, :], lga, rc_[:, 0:1], None, ALU.is_ge),
                reads=[f"lgtall:{ic}", krc], writes=[kra])
            kb.op("dve", lambda e, ra_=ra_, rb_=rb_: e.tensor_tensor_scan(
                rb_[:, :], ones8[:, :], ra_[:, :], 0.0, ALU.mult, ALU.add),
                reads=[kra, "c:k_ones8"], writes=[krb])
            kb.op("dve", lambda e, rb_=rb_: e.tensor_single_scalar(
                rb_[:, :], rb_[:, :], 1.0, ALU.is_le),
                reads=[krb], writes=[krb])
            kb.op("dve", lambda e, ic=ic, ra_=ra_, rb_=rb_: e.tensor_mul(
                oh0s[:, ic * 8:(ic + 1) * 8], ra_[:, :], rb_[:, :]),
                reads=[kra, krb], writes=["oh0s"])
            kb.op("dve", lambda e, ic=ic, ra_=ra_, lga=lga: e.scalar_tensor_tensor(
                ra_[:, :], oh0s[:, ic * 8:(ic + 1) * 8], -1e30, lga,
                ALU.mult, ALU.add),
                reads=["oh0s", f"lgtall:{ic}"], writes=[kra])
            kb.op("dve", lambda e, ra_=ra_, rc_=rc_: e.reduce_max(
                rc_[:, 1:2], ra_[:, :], axis=AXX),
                reads=[kra], writes=[krc])
            kb.op("dve", lambda e, ra_=ra_, rb_=rb_, rc_=rc_: e.tensor_scalar(
                rb_[:, :], ra_[:, :], rc_[:, 1:2], None, ALU.is_ge),
                reads=[kra, krc], writes=[krb])
            kb.op("dve", lambda e, ra_=ra_, rb_=rb_: e.tensor_tensor_scan(
                ra_[:, :], ones8[:, :], rb_[:, :], 0.0, ALU.mult, ALU.add),
                reads=[krb, "c:k_ones8"], writes=[kra])
            kb.op("dve", lambda e, ra_=ra_: e.tensor_single_scalar(
                ra_[:, :], ra_[:, :], 1.0, ALU.is_le),
                reads=[kra], writes=[kra])
            kb.op("dve", lambda e, ic=ic, ra_=ra_, rb_=rb_: e.tensor_mul(
                oh1s[:, ic * 8:(ic + 1) * 8], rb_[:, :], ra_[:, :]),
                reads=[kra, krb], writes=["oh1s"])
            kb.op("dve", lambda e, ic=ic, jt_=jt_: e.scalar_tensor_tensor(
                jt_[:, :], oh0s[:, ic * 8:(ic + 1) * 8], 1.0, iotaE[:, :],
                ALU.mult, ALU.mult, accum_out=e0t[:, ic:ic + 1]),
                reads=["oh0s", "c:k_iotaE"], writes=[kjt, "e0t"])
            kb.op("dve", lambda e, ic=ic, jt_=jt_: e.scalar_tensor_tensor(
                jt_[:, :], oh1s[:, ic * 8:(ic + 1) * 8], 1.0, iotaE[:, :],
                ALU.mult, ALU.mult, accum_out=e1t[:, ic:ic + 1]),
                reads=["oh1s", "c:k_iotaE"], writes=[kjt, "e1t"])
            kb.op("dve", lambda e, ic=ic, jt_=jt_, lga=lga, rc_=rc_: e.scalar_tensor_tensor(
                jt_[:, :], oh0s[:, ic * 8:(ic + 1) * 8], 1.0, lga,
                ALU.mult, ALU.mult, accum_out=rc_[:, 0:1]),
                reads=["oh0s", f"lgtall:{ic}"], writes=[kjt, krc])
            kb.op("dve", lambda e, ic=ic, jt_=jt_, lga=lga, rc_=rc_: e.scalar_tensor_tensor(
                jt_[:, :], oh1s[:, ic * 8:(ic + 1) * 8], 1.0, lga,
                ALU.mult, ALU.mult, accum_out=rc_[:, 1:2]),
                reads=["oh1s", f"lgtall:{ic}"], writes=[kjt, krc])
            kb.op("dve", lambda e, rc_=rc_: e.tensor_tensor(
                rc_[:, 2:3], rc_[:, 1:2], rc_[:, 0:1], ALU.subtract),
                reads=[krc], writes=[krc])
            kb.op("act", lambda e, rc_=rc_: e.activation(rc_[:, 2:3], rc_[:, 2:3], AF.Exp),
                  reads=[krc], writes=[krc])
            kb.op("dve", lambda e, rc_=rc_: e.tensor_scalar_add(rc_[:, 2:3], rc_[:, 2:3], 1.0),
                  reads=[krc], writes=[krc])
            kb.op("dve", lambda e, rc_=rc_: e.reciprocal(rc_[:, 2:3], rc_[:, 2:3]),
                  reads=[krc], writes=[krc])
            kb.op("dve", lambda e, ic=ic, rc_=rc_: e.tensor_copy(
                g0t[:, ic:ic + 1], rc_[:, 2:3]),
                reads=[krc], writes=["g0t"])
            kb.op("dve", lambda e, ic=ic, rc_=rc_: e.tensor_scalar(
                g1t[:, ic:ic + 1], rc_[:, 2:3], -1.0, 1.0, ALU.mult, ALU.add),
                reads=[krc], writes=["g1t"])
            
    def emit_p3a_st(st):
        t0 = st * ST
        for c in range(CST):
            for fb in range(8):
                kb.transpose(pTR[:, :], og[:, c * D + fb * 128: c * D + (fb + 1) * 128],
                             ident[:, :],
                             reads=[f"gout:{c}", "c:k_ident"], writes=["pTR"])
                kb.op("dve", lambda e, c=c, fb=fb: e.tensor_copy(
                    ogF[:, fb * ST + c * C: fb * ST + (c + 1) * C], pTR[:, :]),
                    reads=["pTR"], writes=["ogF"])
        for c in range(CST):
            cp = (st * CST + c) % 2
            xnt_, x2_, np_, s3 = xn_tm[cp], x2b[cp], npb[cp], st3[cp]
            kxn, kx2, knp, ks3 = (f"xn_tm{cp}", f"x2b{cp}", f"npb{cp}", f"st3_{cp}")
            tok = slice(t0 + c * C, t0 + (c + 1) * C)
            kb.dma(s_ldxn(st * CST + c), xnt_[:, 0:D], hbm_xn[tok, :],
                   reads=[f"h:xn:{st * CST + c}"], writes=[kxn])
            pL_, pH_, kL, kH = (pC, pD, "pC", "pD") if cp == 0 else (pA, pB, "pA", "pB")
            mml = []
            for half, ps in ((0, pL_), (1, pH_)):
                for ci in range(8):
                    mml.append((ps[:, 0:512],
                                ogF[:, ci * ST + c * C: ci * ST + (c + 1) * C],
                                wo_sb[:, ci * D + half * 512: ci * D + half * 512 + 512]))
            kb.mmg(mml, reads=["ogF"] + [f"w:wo:{ci}" for ci in range(8)],
                   writes=[kL, kH])
            for half, ps, pk in ((0, pL_, kL), (1, pH_, kH)):
                kb.op("dve", lambda e, ps=ps, half=half, x2_=x2_, xnt_=xnt_: e.scalar_tensor_tensor(
                    x2_[:, half * 512:(half + 1) * 512], ps[:, 0:512], 1.0,
                    xnt_[:, half * 512:(half + 1) * 512], ALU.mult, ALU.add),
                    reads=[pk, kxn], writes=[kx2])
            kb.dma(s_stx2(st * CST + c), hbm_x2[tok, :], x2_[:, :],
                   reads=[kx2], writes=[f"h:x2:{st * CST + c}"])
            kb.op("act", lambda e, x2_=x2_, s3=s3: e.activation(
                sq3a[:, :], x2_[:, :], AF.Square, accum_out=s3[:, 0:1]),
                reads=[kx2], writes=["sq3a", ks3])
            kb.op("act", lambda e, s3=s3: e.activation(s3[:, 1:2], s3[:, 0:1], AF.Ln,
                                                scale=1.0 / D, bias=B_EPSRMS),
                  reads=[ks3, "c:k_fconst"], writes=[ks3])
            kb.op("act", lambda e, s3=s3: e.activation(s3[:, 1:2], s3[:, 1:2], AF.Exp,
                                                scale=-0.5),
                  reads=[ks3], writes=[ks3])
            kb.op("dve", lambda e, x2_=x2_, np_=np_, s3=s3: e.tensor_scalar_mul(
                np_[:, :], x2_[:, :], s3[:, 1:2]),
                reads=[kx2, ks3], writes=[knp])
            kb.dma(s_stnp(st * CST + c), hbm_npre[tok, :], np_[:, :],
                   reads=[knp], writes=[f"h:np:{st * CST + c}"])
            yield

    def gen_aux(stp):
        gp = emit_p3a_st(stp)
        for c in range(CST):
            next(gp)
            yield
            emit_top2_chunk(stp * CST + c)
            yield

    for i in range(min(2 * CST, NC)):
        emit_p1_tile(i)
    p1_next = min(2 * CST, NC)
    for st in range(NST + 1):
        gens = []
        if st >= 1:
            gens.append(gen_aux(st - 1))
        if st < NST:
            gens.append(emit_p2_st(st))
        p1_hi = min((st + 3) * CST, NC)
        live = list(gens)
        rnd = 0
        aux_started = False
        while live:
            rnd += 1
            for g_ in list(live):
                if (len(gens) > 1 and g_ is gens[0] and aux_started
                        and rnd % 3 != 0 and len(live) > 1):
                    continue
                try:
                    next(g_)
                    if len(gens) > 1 and g_ is gens[0]:
                        aux_started = True
                except StopIteration:
                    live.remove(g_)
            if p1_next < p1_hi:
                emit_p1_tile(p1_next)
                p1_next += 1

    def emit_colsums_cc():
     for ic in range(NC):
         for ti, (ohl, ohk, tabC, tk) in enumerate((
                 ([oh0s], ["oh0s"], tabC0, "tabC0"),
                 ([oh1s], ["oh1s"], tabC1, "tabC1"),
                 ([oh0s, oh1s], ["oh0s", "oh1s"], tabC01, "tabC01"))):
             pcs, kcs = (pO0, "pO0") if ti % 2 == 0 else (pO1, "pO1")
             kb.mmg([(pcs[0:1, 0:8], onescol[:, :], ohs[:, ic * 8:(ic + 1) * 8])
                     for ohs in ohl],
                    reads=ohk + ["c:k_onescol"], writes=[kcs])
             kb.op("dve", lambda e, tabC=tabC, ic=ic, pcs=pcs: e.tensor_copy(
                 bass.AP(tabC, ic, [[8 * NC, 1], [NC, 8]]), pcs[0:1, 0:8]),
                 reads=[kcs], writes=[tk])
     for tabC, tk in ((tabC0, "tabC0"), (tabC1, "tabC1"), (tabC01, "tabC01")):
         kb.op("dve", lambda e, tabC=tabC: e.tensor_tensor_scan(
             tabC[:, :], rowmask[:, :], tabC[:, :], 0.0, ALU.mult, ALU.add),
             reads=[tk, "c:k_rowmask"], writes=[tk])
     kb.op("dve", lambda e: e.tensor_copy(
         cnt_loc[:, 0:8], bass.AP(tabC0, NC - 1, [[8 * NC, 1], [NC, 8]])),
         reads=["tabC0"], writes=["cnt_loc"])
     kb.op("dve", lambda e: e.tensor_copy(
         cnt_loc[:, 8:16], bass.AP(tabC1, NC - 1, [[8 * NC, 1], [NC, 8]])),
         reads=["tabC1"], writes=["cnt_loc"])
     kb.mmg([(pA[0:8, 0:16], core_oh[:, :], cnt_loc[:, :])],
            reads=["cnt_loc", "c:k_core_oh"], writes=["pA"])
     kb.op("dve", lambda e: e.tensor_copy(ccin_sb[:, :], pA[0:8, 0:16]),
           reads=["pA"], writes=["ccin_sb"])
     kb.dma(s_cc(0), hbm_ccin[:, :], ccin_sb[:, :], reads=["ccin_sb"], writes=["h:ccin"])
     kb.collective("AllReduce", ALU.add, [list(range(NCHIP))],
                   [hbm_ccin.ap().opt()], [hbm_ccout.ap().opt()],
                   reads=["h:ccin"], writes=["h:ccout"])

    emit_colsums_cc()

    kb.barrier()
    es2.close()

    capok0 = sb("capok0", (128, NC), F32)
    capok1 = sb("capok1", (128, NC), F32)
    tokf = sb("tokf", (128, 1), F32)
    toki = sb("toki", (128, 1), I32)
    sgf = sb("sgf", (128, 2), F32)
    sgi = sb("sgi", (128, 2), I32)
    isc1 = sb("isc1", (128, 2), I32)
    isc2 = sb("isc2", (128, 2), I32)
    s_sc = kb.stream("qsc", "pool", 2)
    bc_slots = kb.e["pool"].to_reg(SLOTS - 1)
    bc_tok = kb.e["pool"].to_reg(T - 1)
    es3b, sb3b = mk_scope()
    NI = IMLP // 128
    wga_sb = sb3b("wga_sb", (128, 8 * IMLP), BF16)
    wup_sb = sb3b("wup_sb", (128, 8 * IMLP), BF16)
    wdn_sb = sb3b("wdn_sb", (128, NI * D), BF16)
    kb.dma(s_ldw(0), wga_sb[:, :],
           bass.AP(wgate_h, 0, [[IMLP, 128], [128 * IMLP, 8], [1, IMLP]]),
           writes=[f"w:wga:{ci}" for ci in range(8)])
    kb.dma(s_ldw(1), wup_sb[:, :],
           bass.AP(wup_h, 0, [[IMLP, 128], [128 * IMLP, 8], [1, IMLP]]),
           writes=[f"w:wup:{ci}" for ci in range(8)])
    kb.dma(s_ldw(0), wdn_sb[:, :],
           bass.AP(wdown_h, 0, [[D, 128], [128 * D, NI], [1, D]]),
           writes=[f"w:wdn:{ci}" for ci in range(NI)])
    for ci in range(8):
        kb.op("dve", lambda e, ci=ci: e.tensor_scalar_mul(
            wga_sb[:, ci * IMLP:(ci + 1) * IMLP], wga_sb[:, ci * IMLP:(ci + 1) * IMLP],
            mnw[:, ci:ci + 1]),
            reads=[f"w:wga:{ci}", "c:k_mnw"], writes=[f"w:wga:{ci}"])
        kb.op("dve", lambda e, ci=ci: e.tensor_scalar_mul(
            wup_sb[:, ci * IMLP:(ci + 1) * IMLP], wup_sb[:, ci * IMLP:(ci + 1) * IMLP],
            mnw[:, ci:ci + 1]),
            reads=[f"w:wup:{ci}", "c:k_mnw"], writes=[f"w:wup:{ci}"])
    tokt = sb3b("tokt", (128, SLOTS // 128), I32)
    kb.op("dve", lambda e: e.memset(tokt[:, :], 0), writes=["tokt"])
    kb.dma(s_sc(0), bass.AP(hbm_tok, 0, [[SLOTS // 128, 128], [1, SLOTS // 128]]),
           tokt[:, :], reads=["tokt"], writes=["h:tok:z"])

    def emit_p4b_chunk(ic):
        o8 = slice(ic * 8, (ic + 1) * 8)
        mml = [(pO0[:, 0:8], tril128s[:, :], oh0s[:, o8]),
               (pO0[:, 0:8], tril128s[:, :], oh1s[:, o8])]
        if ic > 0:
            mml.append((pO0[:, 0:8], ones1x128[:, :],
                        bass.AP(tabC01, ic - 1, [[8 * NC, 1], [NC, 8]])))
        kb.mmg(mml, reads=["oh0s", "oh1s", "tabC01", "c:k_tril128s", "c:k_ones1x128"],
               writes=["pO0"])
        kb.op("dve", lambda e, ic=ic, o8=o8: e.scalar_tensor_tensor(
            jt8[:, :], oh0s[:, o8], 1.0, pO0[:, 0:8], ALU.mult, ALU.mult,
            accum_out=slot0t[:, ic:ic + 1]),
            reads=["oh0s", "pO0"], writes=["jt8", "slot0t"])
        kb.op("dve", lambda e, ic=ic, o8=o8: e.scalar_tensor_tensor(
            jt8[:, :], oh1s[:, o8], 1.0, pO0[:, 0:8], ALU.mult, ALU.mult,
            accum_out=slot1t[:, ic:ic + 1]),
            reads=["oh1s", "pO0"], writes=["jt8", "slot1t"])
        mml = [(pO1[:, 0:8], tril128s[:, :], oh0s[:, o8])]
        if ic > 0:
            mml.append((pO1[:, 0:8], ones1x128[:, :],
                        bass.AP(tabC0, ic - 1, [[8 * NC, 1], [NC, 8]])))
        kb.mmg(mml, reads=["oh0s", "tabC0", "c:k_tril128s", "c:k_ones1x128"],
               writes=["pO1"])
        kb.op("dve", lambda e, ic=ic, o8=o8: e.scalar_tensor_tensor(
            jt8[:, :], oh0s[:, o8], 1.0, pO1[:, 0:8], ALU.mult, ALU.mult,
            accum_out=pre0t[:, ic:ic + 1]),
            reads=["oh0s", "pO1"], writes=["jt8", "pre0t"])
        mml = [(pO1[:, 0:8], tril128s[:, :], oh1s[:, o8])]
        if ic > 0:
            mml.append((pO1[:, 0:8], ones1x128[:, :],
                        bass.AP(tabC1, ic - 1, [[8 * NC, 1], [NC, 8]])))
        kb.mmg(mml, reads=["oh1s", "tabC1", "c:k_tril128s", "c:k_ones1x128"],
               writes=["pO1"])
        kb.op("dve", lambda e, ic=ic, o8=o8: e.scalar_tensor_tensor(
            jt8[:, :], oh1s[:, o8], 1.0, pO1[:, 0:8], ALU.mult, ALU.mult,
            accum_out=pre1t[:, ic:ic + 1]),
            reads=["oh1s", "pO1"], writes=["jt8", "pre1t"])
        kb.op("dve", lambda e, ic=ic: e.tensor_single_scalar(
            capok0[:, ic:ic + 1], slot0t[:, ic:ic + 1], float(CAP), ALU.is_lt),
            reads=["slot0t"], writes=["capok0"])
        kb.op("dve", lambda e, ic=ic: e.tensor_single_scalar(
            capok1[:, ic:ic + 1], slot1t[:, ic:ic + 1], float(CAP), ALU.is_lt),
            reads=["slot1t"], writes=["capok1"])
        kb.op("dve", lambda e, ic=ic: e.scalar_tensor_tensor(
            sgf[:, 0:1], e0t[:, ic:ic + 1], float(CAP), slot0t[:, ic:ic + 1],
            ALU.mult, ALU.add),
            reads=["e0t", "slot0t"], writes=["sgf"])
        kb.op("dve", lambda e, ic=ic: e.scalar_tensor_tensor(
            sgf[:, 1:2], e1t[:, ic:ic + 1], float(CAP), slot1t[:, ic:ic + 1],
            ALU.mult, ALU.add),
            reads=["e1t", "slot1t"], writes=["sgf"])
        for col, cap_t, cap_r in ((0, capok0, "capok0"), (1, capok1, "capok1")):
            kb.op("dve", lambda e, ic=ic, col=col, cap_t=cap_t: e.scalar_tensor_tensor(
                sgf[:, col:col + 1], cap_t[:, ic:ic + 1], 1e7, sgf[:, col:col + 1],
                ALU.mult, ALU.subtract),
                reads=[cap_r, "sgf"], writes=["sgf"])
            kb.op("dve", lambda e, col=col: e.tensor_scalar(
                sgf[:, col:col + 1], sgf[:, col:col + 1], -1.0, 1e7, ALU.mult, ALU.add),
                reads=["sgf"], writes=["sgf"])
        kb.op("dve", lambda e, ic=ic: e.tensor_copy(sgi[:, :], sgf[:, :]),
              reads=["sgf"], writes=["sgi"])
        kb.op("dve", lambda e, ic=ic: e.tensor_copy(slotg0i[:, ic:ic + 1], sgi[:, 0:1]),
              reads=["sgi"], writes=["slotg0i"])
        kb.op("dve", lambda e, ic=ic: e.tensor_copy(slotg1i[:, ic:ic + 1], sgi[:, 1:2]),
              reads=["sgi"], writes=["slotg1i"])
        kb.op("dve", lambda e, ic=ic: e.tensor_scalar_add(
            tokf[:, :], iota_col[:, :], float(ic * 128)),
            reads=["c:k_iota_col"], writes=["tokf"])
        kb.op("dve", lambda e: e.tensor_copy(toki[:, :], tokf[:, :]),
              reads=["tokf"], writes=["toki"])
        for kk, (sgcol, sgrg) in enumerate(((slotg0i, "slotg0i"), (slotg1i, "slotg1i"))):
            kb.op("dve", lambda e, sgcol=sgcol, ic=ic: e.tensor_single_scalar(
                isc1[:, 0:1], sgcol[:, ic:ic + 1], 127, ALU.bitwise_and),
                reads=[sgrg], writes=["isc1"])
            kb.op("dve", lambda e, sgcol=sgcol, ic=ic: e.tensor_single_scalar(
                isc2[:, 0:1], sgcol[:, ic:ic + 1], 7, ALU.arith_shift_right),
                reads=[sgrg], writes=["isc2"])
            kb.op("dve", lambda e: e.tensor_scalar(
                isc1[:, 0:1], isc1[:, 0:1], SLOTS // 128, None, ALU.mult),
                reads=["isc1"], writes=["isc1"])
            kb.op("dve", lambda e: e.tensor_add(isc1[:, 0:1], isc1[:, 0:1], isc2[:, 0:1]),
                  reads=["isc1", "isc2"], writes=["isc1"])
            kb.idma(s_sc(ic + kk), bass.AP(hbm_tok, 0, [[1, SLOTS], [1, 1]]),
                    IndirectOffsetOnAxis(isc1[:, 0:1], 0), toki[:, :], None,
                    bounds_check=bc_slots, oob_is_err=False,
                    reads=["isc1", "toki", "h:tok:z"], writes=[f"h:tok:{ic}:{kk}"])

    xg = [sb3b(f"xg{i}", (128, D), BF16) for i in range(2)]
    s_gx = kb.stream("qgx", "pool", 2)
    s_sx = kb.stream("qsx", "sp", 2)
    def emit_p5_head():
        kb.dma(s_sc(0), tokt[:, :],
               bass.AP(hbm_tok, 0, [[SLOTS // 128, 128], [1, SLOTS // 128]]),
               reads=["h:tok:z"] + [f"h:tok:{ic}:{kk}" for ic in range(NC) for kk in range(2)],
               writes=["tokt"])

    def emit_p5_blk(blk):
        bb = blk % 2
        kb.idma(s_gx(blk), xg[bb][:, :], None, hbm_xn[:, :],
                IndirectOffsetOnAxis(tokt[:, blk:blk + 1], 0),
                bounds_check=bc_tok, oob_is_err=False,
                reads=["tokt", "h:xn"], writes=[f"xg{bb}"])
        kb.dma(s_sx(blk), hbm_xe[blk * 128:(blk + 1) * 128, :], xg[bb][:, :],
               reads=[f"xg{bb}"], writes=[f"h:xe:{blk}"])

    npF_l = [sb3b(f"npF{i}", (128, 8 * ST), BF16) for i in range(2)]
    mt = sb3b("mt", (128, NI * ST), BF16)
    sgl_l = [sb3b(f"sgl{i}", (128, 512), F32) for i in range(2)]
    sbuf_s_l = [sb3b(f"sbuf_s{i}", (128, D), BF16) for i in range(2)]
    x2_tm_one = sb3b("x2_tm", (128, D), BF16)
    x2_tm = [x2_tm_one, x2_tm_one]
    s_ldnp = kb.stream("qldnp", "sp", 2)
    s_ldx2 = kb.stream("qldx2", "sp", 2)
    s_sts = kb.stream("qsts", "act", 2)
    def emit_p3b_ld(st):
        t0 = st * ST
        npF = npF_l[st % 2]
        npk = f"npF{st % 2}"
        np_r = [f"h:np:{j}" for j in range(st * CST, (st + 1) * CST)]
        for ci in range(8):
            kb.dma(s_ldnp(st), npF[:, ci * ST:(ci + 1) * ST],
                   hbm_npre[t0:t0 + ST, ci * 128:(ci + 1) * 128],
                   reads=np_r, writes=[npk], transpose=True)

    def emit_p3b_st(st):
        t0 = st * ST
        npF = npF_l[st % 2]
        npk = f"npF{st % 2}"
        for ii in range(NI):
            if ii == 2 and st + 1 < NST:
                emit_p3b_ld(st + 1)
            pg, kg, pu_, ku_ = (pA, "pA", pB, "pB") if ii % 2 == 0 else (pC, "pC", pD, "pD")
            sg_, ksg = sgl_l[ii % 2], f"sgl{ii % 2}"
            kb.mmg([(pg[:, 0:ST], wga_sb[:, ci * IMLP + ii * 128: ci * IMLP + (ii + 1) * 128],
                     npF[:, ci * ST:(ci + 1) * ST]) for ci in range(8)],
                   reads=[npk] + [f"w:wga:{ci}" for ci in range(8)], writes=[kg])
            kb.mmg([(pu_[:, 0:ST], wup_sb[:, ci * IMLP + ii * 128: ci * IMLP + (ii + 1) * 128],
                     npF[:, ci * ST:(ci + 1) * ST]) for ci in range(8)],
                   reads=[npk] + [f"w:wup:{ci}" for ci in range(8)], writes=[ku_])
            kb.op("act", lambda e, pg=pg, sg_=sg_: e.activation(
                sg_[:, 0:ST], pg[:, 0:ST], AF.Silu),
                reads=[kg], writes=[ksg])
            kb.op("dve", lambda e, ii=ii, sg_=sg_, pu_=pu_: e.tensor_mul(
                mt[:, ii * ST:(ii + 1) * ST], sg_[:, 0:ST], pu_[:, 0:ST]),
                reads=[ksg, ku_], writes=["mt"])
            yield
        for c in range(CST):
            x2t, kx2t = x2_tm_one, "x2_tm"
            sbuf_s, ksbs = sbuf_s_l[c % 2], f"sbuf_s{c % 2}"
            tok = slice(t0 + c * C, t0 + (c + 1) * C)
            kb.dma(s_ldx2(st * CST + c), x2t[:, :], hbm_x2[tok, :],
                   reads=[f"h:x2:{st * CST + c}"], writes=[kx2t])
            pd0, pd1 = (pA, pB) if c % 2 == 0 else (pC, pD)
            kpd0, kpd1 = ("pA", "pB") if c % 2 == 0 else ("pC", "pD")
            mml = []
            for half, ps in ((0, pd0), (1, pd1)):
                for ii in range(NI):
                    mml.append((ps[:, 0:512],
                                mt[:, ii * ST + c * C: ii * ST + (c + 1) * C],
                                wdn_sb[:, ii * D + half * 512: ii * D + half * 512 + 512]))
            kb.mmg(mml, reads=["mt"] + [f"w:wdn:{ci}" for ci in range(NI)],
                   writes=[kpd0, kpd1])
            for half, ps, pk in ((0, pd0, kpd0), (1, pd1, kpd1)):
                kb.op("dve", lambda e, ps=ps, half=half, x2t=x2t, sbuf_s=sbuf_s: e.scalar_tensor_tensor(
                    sbuf_s[:, half * 512:(half + 1) * 512], ps[:, 0:512], 1.0,
                    x2t[:, half * 512:(half + 1) * 512], ALU.mult, ALU.add),
                    reads=[pk, kx2t], writes=[ksbs])
            kb.dma(s_sts(st * CST + c), hbm_s[tok, :], sbuf_s[:, :],
                   reads=[ksbs], writes=[f"h:s:{st * CST + c}"])
            yield

    def gen_p4b5():
        for ic in range(NC):
            emit_p4b_chunk(ic)
            yield
        emit_p5_head()
        for blk in range(SLOTS // 128):
            emit_p5_blk(blk)
            if blk % 2:
                yield

    def gen_p3b_all():
        emit_p3b_ld(0)
        for st in range(NST):
            yield from emit_p3b_st(st)

    gA_, gB_ = gen_p4b5(), gen_p3b_all()
    live = [gA_, gB_]
    turn = 0
    while live:
        turn += 1
        for g in list(live):
            if g is gA_ and turn % 2 != 0:
                continue
            try:
                next(g)
            except StopIteration:
                live.remove(g)

    kb.barrier()
    es3b.close()

    kb.dma(s_cc(0), ccout_sb[:, :], hbm_ccout[:, :], reads=["h:ccout"], writes=["ccout_sb"])
    kb.mmg([(pA[0:1, 0:8], core_pm[:, :], ccout_sb[:, 0:8])],
           reads=["ccout_sb", "c:k_core_pm"], writes=["pA"])
    kb.op("dve", lambda e: e.tensor_copy(csc[:, 0:8], pA[0:1, 0:8]),
          reads=["pA"], writes=["csc"])
    kb.mmg([(pA[0:1, 0:8], ones8col[:, :], ccout_sb[:, 0:8]),
            (pA[0:1, 0:8], core_pm[:, :], ccout_sb[:, 8:16])],
           reads=["ccout_sb", "c:k_ones8col", "c:k_core_pm"], writes=["pA"])
    kb.op("dve", lambda e: e.tensor_copy(csc[:, 8:16], pA[0:1, 0:8]),
          reads=["pA"], writes=["csc"])
    kb.mmg([(pA[:, 0:8], ones1x128[:, :], csc[:, 0:8])],
           reads=["csc", "c:k_ones1x128"], writes=["pA"])
    kb.op("dve", lambda e: e.tensor_copy(gb0[:, :], pA[:, 0:8]),
          reads=["pA"], writes=["gb0"])
    kb.mmg([(pA[:, 0:8], ones1x128[:, :], csc[:, 8:16])],
           reads=["csc", "c:k_ones1x128"], writes=["pA"])
    kb.op("dve", lambda e: e.tensor_copy(gb1[:, :], pA[:, 0:8]),
          reads=["pA"], writes=["gb1"])
    for ic in range(NC):
        o8 = slice(ic * 8, (ic + 1) * 8)
        kb.op("dve", lambda e, o8=o8: e.scalar_tensor_tensor(
            jt8[:, :], oh0s[:, o8], 1.0, gb0[:, :], ALU.mult, ALU.mult,
            accum_out=rc0[:, :]),
            reads=["oh0s", "gb0"], writes=["jt8", "rc0"])
        kb.op("dve", lambda e, ic=ic: e.tensor_add(rc0[:, :], rc0[:, :],
                                                   pre0t[:, ic:ic + 1]),
              reads=["rc0", "pre0t"], writes=["rc0"])
        kb.op("dve", lambda e: e.tensor_single_scalar(rc0[:, :], rc0[:, :],
                                                      float(GCAP), ALU.is_lt),
              reads=["rc0"], writes=["rc0"])
        kb.op("dve", lambda e, ic=ic: e.tensor_mul(rc0[:, :], rc0[:, :],
                                                   capok0[:, ic:ic + 1]),
              reads=["rc0", "capok0"], writes=["rc0"])
        kb.op("dve", lambda e, ic=ic: e.tensor_mul(g0vt[:, ic:ic + 1],
                                                   rc0[:, :], g0t[:, ic:ic + 1]),
              reads=["rc0", "g0t"], writes=["g0vt"])
        kb.op("dve", lambda e, o8=o8: e.scalar_tensor_tensor(
            jt8[:, :], oh1s[:, o8], 1.0, gb1[:, :], ALU.mult, ALU.mult,
            accum_out=rc1[:, :]),
            reads=["oh1s", "gb1"], writes=["jt8", "rc1"])
        kb.op("dve", lambda e, ic=ic: e.tensor_add(rc1[:, :], rc1[:, :],
                                                   pre1t[:, ic:ic + 1]),
              reads=["rc1", "pre1t"], writes=["rc1"])
        kb.op("dve", lambda e: e.tensor_single_scalar(rc1[:, :], rc1[:, :],
                                                      float(GCAP), ALU.is_lt),
              reads=["rc1"], writes=["rc1"])
        kb.op("dve", lambda e, ic=ic: e.tensor_mul(rc1[:, :], rc1[:, :],
                                                   capok1[:, ic:ic + 1]),
              reads=["rc1", "capok1"], writes=["rc1"])
        kb.op("dve", lambda e, ic=ic: e.tensor_mul(g1vt[:, ic:ic + 1],
                                                   rc1[:, :], g1t[:, ic:ic + 1]),
              reads=["rc1", "g1t"], writes=["g1vt"])


    es6, sb6 = mk_scope()
    NHC = HMOE // 128
    BLKS = []
    _r = 0
    while _r < CAP:
        BLKS.append((len(BLKS), min(NBLK, CAP - _r)))
        _r += NBLK
    NB6 = len(BLKS)
    assert CAP % 128 == 0 and NBLK % 128 == 0
    w1_sb = sb6("w1_sb", (128, 8 * HMOE), BF16)
    w2_sb = sb6("w2_sb", (128, NHC * D), BF16)
    b1_sb = sb6("b1_sb", (128, NHC), F32)
    b2r = sb6("b2r", (1, D), BF16)
    ones1b = sb6("ones1b", (1, 128), BF16)
    kb.op("dve", lambda e: e.memset(ones1b[:, :], 1.0), writes=["c:ones1b"])
    XeFb = [sb6(f"XeF{i}", (128, 8 * NBLK), BF16) for i in range(2)]
    h1_sb = sb6("h1_sb", (128, NHC * NBLK), BF16)
    ye_sb = [sb6("ye_sb0", (128, D), BF16)]
    s_ldw1 = kb.stream("qldw1", "pool", 2)
    s_ldw2 = kb.stream("qldw2", "pool", 2)
    s_ldxe = kb.stream("qldxe", "sp", 2)
    s_stye = kb.stream("qstye", "act", 2)

    def ld_w1(e_):
        kb.dma(s_ldw1(e_), w1_sb[:, :],
               bass.AP(w1_h, e_ * D * HMOE,
                       [[HMOE, 128], [128 * HMOE, 8], [1, HMOE]]),
               writes=["w:w1"])
        kb.dma(s_ldw1(e_), b1_sb[:, :], b1_h[e_, :, :], writes=["w:b1"])

    def ld_w2(e_):
        kb.dma(s_ldw2(e_), w2_sb[:, :],
               bass.AP(w2_h, e_ * HMOE * D,
                       [[D, 128], [128 * D, NHC], [1, D]]),
               writes=["w:w2"])
        kb.dma(s_ldw2(e_), b2r[:, :], b2_h[e_:e_ + 1, :], writes=["w:b2"])

    def ld_xe(e_, blk, bs):
        r0 = e_ * CAP + blk * NBLK
        gb = e_ * NB6 + blk
        xe_r = [f"h:xe:{bb}" for bb in range(r0 // 128, (r0 + bs) // 128)]
        Xe = XeFb[gb % 2]
        for ci in range(8):
            kb.dma(s_ldxe(gb), Xe[:, ci * NBLK: ci * NBLK + bs],
                   hbm_xe[r0:r0 + bs, ci * 128:(ci + 1) * 128],
                   reads=xe_r, writes=[f"XeF{gb % 2}"], transpose=True)

    ld_w1(0)
    ld_w2(0)
    ld_xe(0, 0, BLKS[0][1])
    for e_ in range(E):
        for blk, bs in BLKS:
            gb = e_ * NB6 + blk
            if blk + 1 < NB6:
                ld_xe(e_, blk + 1, BLKS[blk + 1][1])
            elif e_ + 1 < E:
                ld_xe(e_ + 1, 0, BLKS[0][1])
            Xe = XeFb[gb % 2]
            xrg = f"XeF{gb % 2}"
            for hc in range(NHC):
                ps, pk = (pA, "pA") if hc % 2 == 0 else (pB, "pB")
                kb.mmg([(ps[:, 0:bs],
                         w1_sb[:, ci * HMOE + hc * 128: ci * HMOE + (hc + 1) * 128],
                         Xe[:, ci * NBLK: ci * NBLK + bs]) for ci in range(8)],
                       reads=[xrg, "w:w1"], writes=[pk])
                kb.op("act", lambda e, ps=ps, hc=hc, bs=bs: e.activation(
                    h1_sb[:, hc * NBLK: hc * NBLK + bs], ps[:, 0:bs],
                    AF.Relu, bias=b1_sb[:, hc:hc + 1]),
                    reads=[pk, "w:b1"], writes=[f"h1:{hc}"])
            if blk == NB6 - 1 and e_ + 1 < E:
                ld_w1(e_ + 1)
            h1r = [f"h1:{hc}" for hc in range(NHC)]
            r0 = e_ * CAP + blk * NBLK
            for g in range(bs // 128):
                pcs = (pC, pD, "pC", "pD") if g % 2 == 0 else (pA, pB, "pA", "pB")
                psl, psh, pkl, pkh = pcs
                mml = [(psl[:, 0:512], ones1b[:, :], b2r[:, 0:512]),
                       (psh[:, 0:512], ones1b[:, :], b2r[:, 512:1024])]
                for hc in range(NHC):
                    lh = h1_sb[:, hc * NBLK + g * 128: hc * NBLK + (g + 1) * 128]
                    mml.append((psl[:, 0:512], lh, w2_sb[:, hc * D: hc * D + 512]))
                    mml.append((psh[:, 0:512], lh, w2_sb[:, hc * D + 512: hc * D + 1024]))
                kb.mmg(mml, reads=h1r + ["w:w2", "w:b2", "c:ones1b"],
                       writes=[pkl, pkh])
                yb = ye_sb[0]
                for half, ps, pk in ((0, psl, pkl), (1, psh, pkh)):
                    kb.op("dve", lambda e, ps=ps, half=half, yb=yb: e.tensor_copy(
                        yb[:, half * 512:(half + 1) * 512], ps[:, 0:512]),
                        reads=[pk], writes=["ye_sb0"])
                rr = r0 + g * 128
                kb.dma(s_stye(g), hbm_ye[rr:rr + 128, :], yb[:, :],
                       reads=["ye_sb0"], writes=[f"h:ye:{rr}"])
        if e_ + 1 < E:
            ld_w2(e_ + 1)

    kb.barrier()
    es6.close()

    NP7 = 4
    r0b = [sb(f"r0b{i}", (128, D), BF16) for i in range(NP7)]
    r1b = [sb(f"r1b{i}", (128, D), BF16) for i in range(NP7)]
    yv = [sb(f"yv{i}", (128, D), F32) for i in range(NP7)]
    gv = [sb(f"gv{i}", (128, D), F32) for i in range(NP7)]
    s_tm = [sb(f"s_tm{i}", (128, D), BF16) for i in range(NP7)]
    x_tm2 = [sb(f"x_tm2{i}", (128, D), F32) for i in range(NP7)]
    outb = [sb(f"outb{i}", (128, D), F32) for i in range(NP7)]
    s_gy0 = kb.stream("qgy0", "pool", NP7)
    s_lds = kb.stream("qlds", "sp", NP7)
    s_ldxf = kb.stream("qldxf", "act", NP7)
    s_out = kb.stream("qout", "act", 2)
    for ic in range(NC):
        cp = ic % NP7
        R0, R1, YV, GV, SS, XX, OB = (r0b[cp], r1b[cp], yv[cp], gv[cp],
                                      s_tm[cp], x_tm2[cp], outb[cp])
        kR0, kR1, kYV, kGV, kSS, kXX, kOB = (f"r0b{cp}", f"r1b{cp}", f"yv{cp}",
                                             f"gv{cp}", f"s_tm{cp}", f"x_tm2{cp}",
                                             f"outb{cp}")
        tok = slice(ic * 128, (ic + 1) * 128)
        kb.idma(s_gy0(ic), R0[:, :], None, hbm_ye[:, :],
                IndirectOffsetOnAxis(slotg0i[:, ic:ic + 1], 0),
                bounds_check=bc_slots, oob_is_err=False,
                reads=["slotg0i"], writes=[kR0])
        kb.idma(s_gy0(ic), R1[:, :], None, hbm_ye[:, :],
                IndirectOffsetOnAxis(slotg1i[:, ic:ic + 1], 0),
                bounds_check=bc_slots, oob_is_err=False,
                reads=["slotg1i"], writes=[kR1])
        kb.dma(s_lds(ic), SS[:, :], hbm_s[tok, :], reads=[f"h:s:{ic}"], writes=[kSS])
        kb.dma(s_ldxf(ic), XX[:, :], x_h[tok, :], writes=[kXX])
        kb.op("dve", lambda e, ic=ic, R0=R0, YV=YV: e.tensor_scalar_mul(
            YV[:, :], R0[:, :], g0vt[:, ic:ic + 1]),
            reads=[kR0, "g0vt"], writes=[kYV])
        kb.op("dve", lambda e, ic=ic, R1=R1, YV=YV: e.scalar_tensor_tensor(
            YV[:, :], R1[:, :], g1vt[:, ic:ic + 1], YV[:, :], ALU.mult, ALU.add),
            reads=[kR1, "g1vt", kYV], writes=[kYV])
        kb.op("act", lambda e, YV=YV, GV=GV: e.activation(GV[:, :], YV[:, :], AF.Silu),
              reads=[kYV], writes=[kGV])
        kb.op("dve", lambda e, GV=GV, SS=SS: e.tensor_mul(GV[:, :], GV[:, :], SS[:, :]),
              reads=[kGV, kSS], writes=[kGV])
        kb.op("dve", lambda e, GV=GV, XX=XX, OB=OB: e.tensor_add(OB[:, :], GV[:, :], XX[:, :]),
              reads=[kGV, kXX], writes=[kOB])
        kb.dma(s_out(ic), out_h[tok, :], OB[:, :], reads=[kOB], writes=["h:out"])

    kb.finish()
    es.close()
    return nc



def _consts(T, ST):
    bf = ml_dtypes.bfloat16
    c = {}
    c["ident"] = np.eye(128, dtype=bf)
    s = np.arange(128)[:, None]; t = np.arange(128)[None, :]
    c["maskA"] = (s <= t).astype(bf)
    c["tril128s"] = (s < t).astype(np.float32)
    c["ones1x128"] = np.ones((1, 128), np.float32)
    c["onescol"] = np.ones((128, 1), np.float32)
    c["iotaE"] = np.tile(np.arange(8, dtype=np.float32)[None, :], (128, 1))
    c["iota_col"] = np.arange(128, dtype=np.float32)[:, None]
    c["ones8"] = np.ones((128, 8), np.float32)
    sm = np.ones((128, 4 * ST), ml_dtypes.bfloat16)
    sm[:, ::C] = 0.0
    c["scanmask"] = sm
    c["ones8col"] = np.ones((8, 1), np.float32)
    NCc = T // C
    rm = np.ones((1, 8 * NCc), ml_dtypes.bfloat16)
    rm[0, ::NCc] = 0.0
    c["rowmask"] = rm
    c["identF"] = np.eye(8, dtype=np.float32)
    fc = np.zeros((128, 8), np.float32)
    fc[:, 0] = 1e-5; fc[:, 1] = 1e-6; fc[:, 2] = 1.0
    c["fconst"] = fc
    return c


def _core_inputs(inputs, core, T, ST):
    i = inputs
    g = lambda k: np.ascontiguousarray(np.asarray(i[k]))
    d = dict(
        x=g("L0_attn_hidden_states")[core],
        wq=g("L0_attn_Wq"), wk=g("L0_attn_Wk"), wv=g("L0_attn_Wv"),
        wgo=g("L0_attn_Wg"), wgk1=g("L0_attn_Wgk1"), wgk2=g("L0_attn_Wgk2"),
        wo=g("L0_attn_Wo"), wgate=g("L0_attn_W_gate"), wup=g("L0_attn_W_up"),
        wdown=g("L0_attn_W_down"), mwg=g("L0_ffn_wg"),
        w1=g("L0_ffn_w1"), w2=g("L0_ffn_w2"), b2=g("L0_ffn_b2"),
        anw_l=np.ascontiguousarray(g("L0_attn_attn_norm_w").reshape(8, 128).T),
        mnw_l=np.ascontiguousarray(g("L0_attn_mlp_norm_w").reshape(8, 128).T),
        gnw_l=np.ascontiguousarray(np.tile(g("L0_attn_gnorm_w"), NH).reshape(8, 128).T),
        bgk2_l=np.ascontiguousarray(g("L0_attn_bgk2").reshape(4, 128).T),
        b1_l=np.ascontiguousarray(g("L0_ffn_b1").reshape(E, HMOE // 128, 128).transpose(0, 2, 1)),
        core_oh=np.eye(8, dtype=np.float32)[core][None, :],
        core_pm=(np.arange(8) < core).astype(np.float32)[:, None],
    )
    d.update(_consts(T, ST))
    return d


def kernel(**inputs):
    T = 8192
    nc = build_nc(T=T)
    in_maps = [_core_inputs(inputs, b, T, 512) for b in range(NCORE)]
    res = run_bass_kernel_spmd(nc, in_maps, core_ids=list(range(NCORE)))
    out = np.stack([r["out"] for r in res.results], axis=0)
    return out


# version 40 (final)
# speedup vs baseline: 1.0139x; 1.0015x over previous
import math
import contextlib
import numpy as np
import ml_dtypes

import concourse.bass as bass
import concourse.mybir as mybir
from concourse.bass_utils import run_bass_kernel_spmd
from concourse.bass_core_rust import IndirectOffsetOnAxis

F32 = mybir.dt.float32
BF16 = mybir.dt.bfloat16
I32 = mybir.dt.int32
AF = mybir.ActivationFunctionType
ALU = mybir.AluOpType
AXX = mybir.AxisListType.X

D = 1024
NH = 4
DK = 128
DV = 256
DKQ = 512
R = 16
IMLP = 2816
E = 8
HMOE = 4096
C = 128
S16 = 1.0 / 16.0
SCL = DK ** -0.5
EPS_LN = 1e-5
EPS_RMS = 1e-6
NCORE = 8

ENG = ("pe", "dve", "act", "pool", "sp")
QUEUE_ENG = {"spq": "sp", "actq": "act", "poolq": "pool"}
ENG_ATTR = {"pe": "tensor", "dve": "vector", "act": "scalar", "pool": "gpsimd", "sp": "sync"}


class KB:

    def __init__(self, nc, es):
        self.nc = nc
        self.es = es
        self.e = {k: getattr(nc, v) for k, v in ENG_ATTR.items()}
        self.sem = {}
        for name in ENG + ("cc",):
            self.sem[name] = es.enter_context(nc.semaphore(f"s_{name}"))
        self.cnt = {name: 0 for name in self.sem}
        self.w = {}
        self.r = {}
        self.waited = {en: {s: 0 for s in self.sem} for en in ENG}
        self.dma_sems = set()
        self.stream_eng = {}

    def stream(self, name, eng, npar=2):
        for i in range(npar):
            sn = f"{name}{i}"
            if sn in self.sem:
                continue
            self.sem[sn] = self.es.enter_context(self.nc.semaphore(f"s_{sn}"))
            self.cnt[sn] = 0
            self.dma_sems.add(sn)
            self.stream_eng[sn] = eng
            for en in ENG:
                self.waited[en][sn] = 0
        return lambda idx: f"{name}{idx % npar}"

    def _wait(self, eng, sem, val):
        if self.waited[eng][sem] >= val:
            return
        self.e[eng].wait_ge(self.sem[sem], val)
        self.waited[eng][sem] = val

    def _edge(self, eng, sem, val):
        if sem in self.dma_sems:
            val = self.cnt[sem]
        self._wait(eng, sem, val)

    def _pre(self, eng, reads, writes):
        for rg in reads:
            lw = self.w.get(rg)
            if lw is not None:
                self._edge(eng, *lw)
        for rg in writes:
            lw = self.w.get(rg)
            if lw is not None:
                self._edge(eng, *lw)
            for rr in self.r.get(rg, ()):
                self._edge(eng, *rr)

    def _post(self, sem, reads, writes):
        cn = self.cnt[sem]
        for rg in reads:
            self.r.setdefault(rg, []).append((sem, cn))
        for rg in writes:
            self.w[rg] = (sem, cn)
            self.r[rg] = []

    def op(self, eng, fn, reads=(), writes=()):
        self._pre(eng, reads, writes)
        inst = fn(self.e[eng])
        self.cnt[eng] += 1
        inst.then_inc(self.sem[eng], 1)
        self._post(eng, reads, writes)
        return inst

    def mmg(self, fn_list, reads=(), writes=(), first_start=True, last_stop=True):
        self._pre("pe", reads, writes)
        keys = [(ps.tensor.name, ps.offset) for ps, _, _ in fn_list]
        last = {k: i for i, k in enumerate(keys)}
        seen = set()
        for i, (ps, lh, rh) in enumerate(fn_list):
            k = keys[i]
            st = first_start and (k not in seen)
            seen.add(k)
            inst = self.e["pe"].matmul(
                ps, lh, rh, start=st, stop=(last_stop and last[k] == i))
        self.cnt["pe"] += 1
        inst.then_inc(self.sem["pe"], 1)
        self._post("pe", reads, writes)

    def transpose(self, out_ps, in_ap, ident_ap, reads=(), writes=()):
        self._pre("pe", reads, writes)
        inst = self.e["pe"].transpose(out_ps, in_ap, ident_ap)
        self.cnt["pe"] += 1
        inst.then_inc(self.sem["pe"], 1)
        self._post("pe", reads, writes)

    def dma(self, sem, dst, src, reads=(), writes=(), **kw):
        eng = self.stream_eng[sem]
        self._pre(eng, reads, writes)
        inst = self.e[eng].dma_start(dst, src, **kw)
        self.cnt[sem] += 16
        inst.then_inc(self.sem[sem], 16)
        self._post(sem, reads, writes)

    def idma(self, sem, dst, dst_off, src, src_off, reads=(), writes=(), **kw):
        assert self.stream_eng[sem] == "pool"
        self._pre("pool", reads, writes)
        inst = self.e["pool"].indirect_dma_start(dst, dst_off, src, src_off, **kw)
        self.cnt[sem] += 16
        inst.then_inc(self.sem[sem], 16)
        self._post(sem, reads, writes)

    def collective(self, kind, op, groups, ins, outs, reads=(), writes=()):
        self._pre("pool", reads, writes)
        inst = self.e["pool"].collective_compute(
            kind, op, replica_groups=groups, ins=ins, outs=outs)
        self.cnt["cc"] += 1
        inst.then_inc(self.sem["cc"])
        self._post("cc", reads, writes)

    def barrier(self):
        for sn in self.dma_sems:
            self._wait(self.stream_eng[sn], sn, self.cnt[sn])
        self.nc.all_engine_barrier()
        for en in ENG:
            for s in self.sem:
                if s != "cc":
                    self.waited[en][s] = self.cnt[s]

    def finish(self):
        self._wait("pool", "cc", self.cnt["cc"])
        self.barrier()


def _ap_bcast_free(t, part, grp, inner):
    base = t[:].ap
    pitch = base[0][0]
    return bass.AP(t, 0, [[pitch, part], [0, grp], [1, inner]])


def build_nc(T=8192, ST=512, CAP=2432, NBLK=512, GCAP=16384, n_cores=8, sim=False):
    assert T % C == 0 and T % ST == 0 and ST % C == 0
    NC = T // C
    NST = T // ST
    CST = ST // C
    NCHIP = n_cores
    SLOTS = E * CAP

    nc = bass.Bass("TRN2", target_bir_lowering=False)
    es = contextlib.ExitStack()

    din = {}
    def inp(name, shape, dt=F32):
        din[name] = nc.dram_tensor(name, list(shape), dt, kind="ExternalInput")
        return din[name]

    x_h = inp("x", (T, D))
    wq_h = inp("wq", (D, DKQ)); wk_h = inp("wk", (D, DKQ))
    wv_h = inp("wv", (D, D)); wgo_h = inp("wgo", (D, D))
    wgk1_h = inp("wgk1", (D, R)); wgk2_h = inp("wgk2", (R, DKQ))
    wo_h = inp("wo", (D, D))
    wgate_h = inp("wgate", (D, IMLP)); wup_h = inp("wup", (D, IMLP))
    wdown_h = inp("wdown", (IMLP, D))
    mwg_h = inp("mwg", (D, E))
    w1_h = inp("w1", (E, D, HMOE)); w2_h = inp("w2", (E, HMOE, D))
    anw_h = inp("anw_l", (128, 8)); mnw_h = inp("mnw_l", (128, 8))
    gnw_h = inp("gnw_l", (128, 8)); bgk2_h = inp("bgk2_l", (128, 4))
    b1_h = inp("b1_l", (E, 128, HMOE // 128)); b2_h = inp("b2", (E, D))
    ident_h = inp("ident", (128, 128), BF16)
    maskA_h = inp("maskA", (128, 128), BF16)
    tril128s_h = inp("tril128s", (128, 128), F32)
    ones1x128_h = inp("ones1x128", (1, 128), F32)
    onescol_h = inp("onescol", (128, 1), F32)
    iotaE_h = inp("iotaE", (128, 8))
    iota_col_h = inp("iota_col", (128, 1))
    ones8_h = inp("ones8", (128, 8))
    scanmask_h = inp("scanmask", (128, NH * ST), BF16)
    rowmask_h = inp("rowmask", (1, 8 * (T // C)), BF16)
    identF_h = inp("identF", (8, 8))
    core_oh_h = inp("core_oh", (1, 8))
    fconst_h = inp("fconst", (128, 8))
    core_pm_h = inp("core_pm", (8, 1))
    ones8col_h = inp("ones8col", (8, 1))

    out_h = nc.dram_tensor("out", [T, D], F32, kind="ExternalOutput")

    hbm_xn = nc.dram_tensor("hbm_xn", [T, D], BF16)
    hbm_hpre = nc.dram_tensor("hbm_hpre", [T, D], BF16)
    hbm_x2 = nc.dram_tensor("hbm_x2", [T, D], BF16)
    hbm_npre = nc.dram_tensor("hbm_npre", [T, D], BF16)
    hbm_s = nc.dram_tensor("hbm_s", [T, D], BF16)
    hbm_xe = nc.dram_tensor("hbm_xe", [SLOTS, D], BF16)
    hbm_ye = nc.dram_tensor("hbm_ye", [SLOTS, D], BF16)
    hbm_tok = nc.dram_tensor("hbm_tok", [SLOTS, 1], I32)
    hbm_ccin = nc.dram_tensor("hbm_ccin", [8, 16], F32)
    hbm_ccout = nc.dram_tensor("hbm_ccout", [8, 16], F32)

    kb = KB(nc, es)
    sb = lambda name, shape, dt: es.enter_context(nc.sbuf_tensor(name, list(shape), dt))
    psum = lambda name, shape, dt=F32: es.enter_context(nc.psum_tensor(name, list(shape), dt))
    def mk_scope():
        st_ = contextlib.ExitStack()
        return st_, (lambda name, shape, dt: st_.enter_context(
            nc.sbuf_tensor(name, list(shape), dt)))

    s_const = kb.stream("qconst", "sp", 1)
    s_ldx = kb.stream("qldx", "pool", 2)
    s_stxn = kb.stream("qstxn", "act", 2)
    s_sthp = kb.stream("qsthp", "act", 2)

    ident = sb("k_ident", (128, 128), BF16)
    maskA = sb("k_maskA", (128, 128), BF16)
    tril128s = sb("k_tril128s", (128, 128), F32)
    ones1x128 = sb("k_ones1x128", (1, 128), F32)
    onescol = sb("k_onescol", (128, 1), F32)
    iotaE = sb("k_iotaE", (128, 8), F32)
    iota_col = sb("k_iota_col", (128, 1), F32)
    ones8 = sb("k_ones8", (128, 8), F32)
    rowmask = sb("k_rowmask", (1, 8 * NC), BF16)
    identF = sb("k_identF", (8, 8), F32)
    core_oh = sb("k_core_oh", (1, 8), F32)
    core_pm = sb("k_core_pm", (8, 1), F32)
    ones8col = sb("k_ones8col", (8, 1), F32)
    anw = sb("k_anw", (128, 8), F32)
    mnw = sb("k_mnw", (128, 8), F32)
    gnw = sb("k_gnw", (128, 8), F32)
    bgk2 = sb("k_bgk2", (128, 4), F32)
    fconst = sb("k_fconst", (128, 8), F32)
    B_EPSLN = fconst[:, 0:1]; B_EPSRMS = fconst[:, 1:2]; B_ONE = fconst[:, 2:3]

    NCT = NC
    sig_h = sb("sig_h", (128, NCT), F32)
    g0t = sb("g0t", (128, NCT), F32)
    g1t = sb("g1t", (128, NCT), F32)
    e0t = sb("e0t", (128, NCT), F32)
    e1t = sb("e1t", (128, NCT), F32)
    oh0s = sb("oh0s", (128, NCT * 8), F32)
    oh1s = sb("oh1s", (128, NCT * 8), F32)
    tabC0 = sb("tabC0", (1, 8 * NC), F32)
    tabC1 = sb("tabC1", (1, 8 * NC), F32)
    tabC01 = sb("tabC01", (1, 8 * NC), F32)
    slot0t = sb("slot0t", (128, NCT), F32)
    slot1t = sb("slot1t", (128, NCT), F32)
    pre0t = sb("pre0t", (128, NCT), F32)
    pre1t = sb("pre1t", (128, NCT), F32)
    slotg0i = sb("slotg0i", (128, NCT), I32)
    slotg1i = sb("slotg1i", (128, NCT), I32)
    g0vt = sb("g0vt", (128, NCT), F32)
    g1vt = sb("g1vt", (128, NCT), F32)
    cnt_loc = sb("cnt_loc", (1, 16), F32)
    gb0 = sb("gb0", (128, 8), F32)
    gb1 = sb("gb1", (128, 8), F32)
    jt8 = sb("jt8", (128, 8), F32)
    lgtall = sb("lgtall", (128, NCT * 8), F32)
    rc0 = sb("rc0", (128, 1), F32)
    rc1 = sb("rc1", (128, 1), F32)
    rc2 = sb("rc2", (128, 1), F32)
    ccin_sb = sb("ccin_sb", (8, 16), F32)
    ccout_sb = sb("ccout_sb", (8, 16), F32)
    csc = sb("csc", (1, 16), F32)

    for t, h in ((ident, ident_h), (maskA, maskA_h), (tril128s, tril128s_h),
                 (ones1x128, ones1x128_h), (onescol, onescol_h),
                 (iotaE, iotaE_h), (iota_col, iota_col_h), (ones8, ones8_h),
                 (rowmask, rowmask_h), (identF, identF_h),
                 (core_oh, core_oh_h), (core_pm, core_pm_h),
                 (ones8col, ones8col_h), (fconst, fconst_h),
                 (anw, anw_h), (mnw, mnw_h), (gnw, gnw_h), (bgk2, bgk2_h)):
        kb.dma(s_const(0), t[:], h[:], writes=[f"c:{t.name}"])


    es2, sb2 = mk_scope()
    s_ldw = kb.stream("qldw", "pool", 2)
    wq_sb = sb2("wq_sb", (128, 8 * DKQ), BF16)
    wk_sb = sb2("wk_sb", (128, 8 * DKQ), BF16)
    wv_sb = sb2("wv_sb", (128, 8 * D), BF16)
    wgo_sb = sb2("wgo_sb", (128, 8 * D), BF16)
    wgk1_sb = sb2("wgk1_sb", (128, 8 * R), BF16)
    wgk2_sb = sb2("wgk2_sb", (16, DKQ), BF16)
    mwg_sb = sb2("mwg_sb", (128, 8 * E), BF16)
    wspec = ((wq_sb, wq_h, "wq", DKQ, True), (wk_sb, wk_h, "wk", DKQ, True),
             (wv_sb, wv_h, "wv", D, True), (wgo_sb, wgo_h, "wgo", D, True),
             (wgk1_sb, wgk1_h, "wgk1", R, True), (mwg_sb, mwg_h, "mwg", E, False))
    for wi, (wsb, wh, wid, cols, scaled) in enumerate(wspec):
        kb.dma(s_ldw(wi), wsb[:, :],
               bass.AP(wh, 0, [[cols, 128], [128 * cols, 8], [1, cols]]),
               writes=[f"w:{wid}:{ci}" for ci in range(8)])
    kb.dma(s_ldw(0), wgk2_sb[:, :], wgk2_h[:, :], writes=["w:wgk2"])
    wo_sb = sb2("wo_sb", (128, 8 * D), BF16)
    kb.dma(s_ldw(6), wo_sb[:, :],
           bass.AP(wo_h, 0, [[D, 128], [128 * D, 8], [1, D]]),
           writes=[f"w:wo:{ci}" for ci in range(8)])
    for ci in range(8):
        for (wsb, wh, wid, cols, scaled) in wspec:
            if scaled:
                sl = wsb[:, ci * cols:(ci + 1) * cols]
                kb.op("dve", lambda e, sl=sl, ci=ci: e.tensor_scalar_mul(
                    sl, sl, anw[:, ci:ci + 1]),
                    reads=[f"w:{wid}:{ci}", "c:k_anw"], writes=[f"w:{wid}:{ci}"])
        kb.op("dve", lambda e, ci=ci: e.tensor_scalar_mul(
            wo_sb[:, ci * D:(ci + 1) * D], wo_sb[:, ci * D:(ci + 1) * D],
            gnw[:, ci:ci + 1]),
            reads=[f"w:wo:{ci}", "c:k_gnw"], writes=[f"w:wo:{ci}"])

    NB = 2
    xin = [sb2(f"p1x{i}", (128, D), BF16) for i in range(NB)]
    xnout = [sb2(f"p1xn{i}", (128, D), BF16) for i in range(NB)]
    hpout = [sb2(f"p1hp{i}", (128, D), BF16) for i in range(NB)]
    sqscr_one = sb2("p1sq", (128, D), BF16)
    st1_l = [sb2(f"p1st{i}", (128, 8), F32) for i in range(NB)]

    def emit_p1_tile(i):
        b = i % NB
        xt, xnt, hpt = xin[b], xnout[b], hpout[b]
        st1, sqscr = st1_l[b], sqscr_one
        rx, rxn, rhp = f"p1x{b}", f"p1xn{b}", f"p1hp{b}"
        rst, rsq = f"p1st{b}", "p1sq"
        tok = slice(i * 128, (i + 1) * 128)
        kb.dma(s_ldx(i), xt[:], x_h[tok, :], writes=[rx])
        kb.op("act", lambda e: e.activation(sqscr[:], xt[:], AF.Identity,
                                            accum_out=st1[:, 0:1]),
              reads=[rx], writes=[rsq, rst])
        kb.op("act", lambda e: e.activation(sqscr[:], xt[:], AF.Square,
                                            accum_out=st1[:, 1:2]),
              reads=[rx], writes=[rsq, rst])
        kb.op("dve", lambda e: e.tensor_scalar(st1[:, 2:3], st1[:, 0:1], 1.0 / D, None, ALU.mult),
              reads=[rst], writes=[rst])
        kb.op("dve", lambda e: e.tensor_tensor(st1[:, 3:4], st1[:, 2:3], st1[:, 2:3], ALU.mult),
              reads=[rst], writes=[rst])
        kb.op("dve", lambda e: e.scalar_tensor_tensor(st1[:, 3:4], st1[:, 1:2], 1.0 / D,
                                                      st1[:, 3:4], ALU.mult, ALU.subtract),
              reads=[rst], writes=[rst])
        kb.op("act", lambda e: e.activation(st1[:, 4:5], st1[:, 3:4], AF.Ln, bias=B_EPSLN),
              reads=[rst, "c:k_fconst"], writes=[rst])
        kb.op("act", lambda e: e.activation(st1[:, 4:5], st1[:, 4:5], AF.Exp, scale=-0.5),
              reads=[rst], writes=[rst])
        kb.op("dve", lambda e: e.tensor_tensor(st1[:, 5:6], st1[:, 4:5], st1[:, 4:5], ALU.mult),
              reads=[rst], writes=[rst])
        kb.op("dve", lambda e: e.tensor_tensor(st1[:, 5:6], st1[:, 5:6], st1[:, 3:4], ALU.mult),
              reads=[rst], writes=[rst])
        kb.op("act", lambda e: e.activation(st1[:, 5:6], st1[:, 5:6], AF.Ln, bias=B_EPSRMS),
              reads=[rst, "c:k_fconst"], writes=[rst])
        kb.op("act", lambda e: e.activation(st1[:, 6:7], st1[:, 5:6], AF.Exp, scale=0.5),
              reads=[rst], writes=[rst])
        kb.op("act", lambda e: e.activation(st1[:, 5:6], st1[:, 5:6], AF.Exp, scale=-0.5),
              reads=[rst], writes=[rst])
        kb.op("dve", lambda e: e.tensor_copy(sig_h[:, i:i + 1], st1[:, 6:7]),
              reads=[rst], writes=[f"m:sig_h:{i}"])
        kb.op("dve", lambda e: e.tensor_tensor(st1[:, 7:8], st1[:, 4:5], st1[:, 5:6], ALU.mult),
              reads=[rst], writes=[rst])
        kb.op("dve", lambda e: e.tensor_scalar(xnt[:], xt[:], st1[:, 2:3], st1[:, 4:5],
                                               ALU.subtract, ALU.mult),
              reads=[rx, rst], writes=[rxn])
        kb.op("dve", lambda e: e.tensor_scalar(hpt[:], xt[:], st1[:, 2:3], st1[:, 7:8],
                                               ALU.subtract, ALU.mult),
              reads=[rx, rst], writes=[rhp])
        kb.dma(s_stxn(i), hbm_xn[tok, :], xnt[:], reads=[rxn], writes=[f"h:xn:{i}"])
        kb.dma(s_sthp(i), hbm_hpre[tok, :], hpt[:], reads=[rhp], writes=[f"h:hp:{i}"])


    s_ldh = kb.stream("qldh", "sp", 2)
    s_const2 = kb.stream("qconst2", "sp", 1)
    scanmask = sb2("k_scanmask", (128, NH * ST), BF16)
    kb.dma(s_const2(0), scanmask[:], scanmask_h[:], writes=["c:k_scanmask"])
    hpreF_l = [sb2(f"hpreF{i}", (128, 8 * ST), BF16) for i in range(2)]
    Gt = sb2("Gt", (128, NH * ST), F32)
    bt = Gt
    vt = sb2("vt", (128, CST * D), BF16)
    gout = sb2("gout", (128, CST * D), BF16)
    og = gout
    qe_t = [sb2(f"qe_t{i}", (128, ST), BF16) for i in range(2)]
    kt_t = [sb2(f"kt_t{i}", (128, ST), BF16) for i in range(2)]
    kd_t = [sb2(f"kd_t{i}", (128, ST), BF16) for i in range(2)]
    kdT_t = [sb2(f"kdT_t{i}", (128, ST), BF16) for i in range(2)]
    sA = sb2("sA", (128, NH * ST), F32)
    sB = sb2("sB", (128, NH * ST), F32)
    gk1b = sb2("gk1b", (16, ST), BF16)
    Ut_l = [sb2(f"Ut{i}", (128, DV * CST), F32) for i in range(2)]
    Ssc_l = Ut_l
    d0m_one = sb2("d0m", (128, DV * CST), F32)
    d0m_l = [d0m_one, d0m_one]
    lamALL = sb2("lamALL", (128, NH * CST), F32)
    lamm_one = sb2("lamm", (128, CST), F32)
    lamm = [lamm_one, lamm_one]
    Sprev = sb2("Sprev", (128, NH * DV), F32)
    Sbc_one = sb2("Sbcp", (128, DV), BF16)
    Sbc = [Sbc_one, Sbc_one]
    Sbc0_l = [sb2(f"Sbc0_{i}", (128, DV), BF16) for i in range(2)]
    asb = [sb2(f"asb{i}", (128, 128), BF16) for i in range(2)]
    jt256 = sqscr_one
    sst = sb2("sst", (128, CST * NH), F32)
    rstd4 = [sb2(f"rstd4_{i}", (128, NH), F32) for i in range(2)]
    lg = sb2("lg", (8, ST), F32)

    pA = psum("pA", (128, 512))
    pB = psum("pB", (128, 512))
    pC = psum("pC", (128, 512))
    pD = psum("pD", (128, 512))
    pO0 = psum("pO0", (128, DV))
    pO1 = psum("pO1", (128, DV))
    pAA = psum("pAA", (128, 128))
    pAA2 = pAA
    pTR = psum("pTR", (128, 128), BF16)
    pTR2 = pTR

    U_pitch = DV * CST

    for _h in range(NH):
        kb.op("dve", lambda e, _h=_h: e.memset(Sprev[:, _h * DV:(_h + 1) * DV], 0.0),
              writes=[f"Sprev:{_h}"])
    assert ST <= 512

    def emit_p2_st(st):
        t0 = st * ST
        hpreF = hpreF_l[st % 2]
        hpk = f"hpreF{st % 2}"
        hp_r = [f"h:hp:{j}" for j in range(st * CST, (st + 1) * CST)]
        for ci in range(8):
            kb.dma(s_ldh(st), hpreF[:, ci * ST:(ci + 1) * ST],
                   hbm_hpre[t0:t0 + ST, ci * 128:(ci + 1) * 128],
                   reads=hp_r, writes=[hpk], transpose=True)

        for c in range(CST):
            lhs = lambda ci, c=c: hpreF[:, ci * ST + c * C: ci * ST + (c + 1) * C]
            pL_, pH_, kL, kH = (pC, pD, "pC", "pD") if c % 2 == 0 else (pA, pB, "pA", "pB")
            for (wsb, wid, dst, dk_) in ((wv_sb, "wv", vt, "vt"), (wgo_sb, "wgo", gout, "gout")):
                mml = []
                for half in range(2):
                    for ci in range(8):
                        mml.append((
                            (pL_ if half == 0 else pH_)[:, 0:512],
                            lhs(ci),
                            wsb[:, ci * D + half * 512: ci * D + half * 512 + 512]))
                kb.mmg(mml, reads=[hpk] + [f"w:{wid}:{ci}" for ci in range(8)],
                       writes=[kL, kH])
                for half, ps, pk in ((0, pL_, kL), (1, pH_, kH)):
                    if half == 0:
                        kb.op("dve", lambda e, ps=ps, dst=dst, c=c, half=half: e.tensor_copy(
                            dst[:, c * D + half * 512: c * D + half * 512 + 512], ps[:, 0:512]),
                            reads=[pk], writes=[f"{dk_}:{c}"])
                    else:
                        kb.op("act", lambda e, ps=ps, dst=dst, c=c, half=half: e.activation(
                            dst[:, c * D + half * 512: c * D + half * 512 + 512], ps[:, 0:512],
                            AF.Identity),
                            reads=[pk], writes=[f"{dk_}:{c}"])
            kb.op("act", lambda e, c=c: e.activation(
                gout[:, c * D:(c + 1) * D], gout[:, c * D:(c + 1) * D], AF.Silu),
                reads=[f"gout:{c}"], writes=[f"gout:{c}"])
            yield

        kb.mmg([(pA[0:16, 0:ST], wgk1_sb[:, ci * R:(ci + 1) * R],
                 hpreF[:, ci * ST:(ci + 1) * ST]) for ci in range(8)],
               reads=[hpk] + [f"w:wgk1:{ci}" for ci in range(8)], writes=["pA"])
        kb.op("dve", lambda e: e.tensor_copy(gk1b[:, :], pA[0:16, 0:ST]),
              reads=["pA"], writes=["gk1b"])
        yield
        GK = NH * ST
        GTW = [f"Gt:{h}" for h in range(NH)]
        for h in range(NH):
            hS = h * ST
            pgk, kgk = (pB, "pB") if h % 2 == 0 else (pD, "pD")
            kb.mmg([(pgk[:, 0:ST], wgk2_sb[:, h * 128:(h + 1) * 128], gk1b[:, :])],
                   reads=["gk1b", "w:wgk2"], writes=[kgk])
            kb.op("act", lambda e, h=h, hS=hS, pgk=pgk: e.activation(
                sA[:, hS:hS + ST], pgk[:, 0:ST], AF.Identity, bias=bgk2[:, h:h + 1]),
                reads=[kgk, "c:k_bgk2"], writes=["sA"])
            yield
        kb.op("act", lambda e: e.activation(sB[:, 0:GK], sA[:, 0:GK], AF.Abs),
              reads=["sA"], writes=["sB"])
        kb.op("act", lambda e: e.activation(sB[:, 0:GK], sB[:, 0:GK], AF.Exp, scale=-1.0),
              reads=["sB"], writes=["sB"])
        kb.op("act", lambda e: e.activation(sB[:, 0:GK], sB[:, 0:GK], AF.Ln, bias=B_ONE),
              reads=["sB", "c:k_fconst"], writes=["sB"])
        kb.op("dve", lambda e: e.scalar_tensor_tensor(
            Gt[:, 0:GK], sA[:, 0:GK], 0.0, sB[:, 0:GK], ALU.min, ALU.subtract),
            reads=["sA", "sB"], writes=GTW)
        yield
        kb.op("dve", lambda e: e.tensor_tensor_scan(
            bt[:, 0:GK], scanmask[:, 0:GK], Gt[:, 0:GK], 0.0, ALU.mult, ALU.add),
            reads=GTW + ["c:k_scanmask"], writes=GTW)
        bl_view = bass.AP(bt, C - 1, [[NH * ST, 128], [ST, NH], [C, CST]])
        kb.op("dve", lambda e, bl_view=bl_view: e.tensor_scalar_mul(
            lamALL[:, :], bl_view, S16), reads=GTW, writes=["lamALL"])
        kb.op("act", lambda e: e.activation(lamALL[:, :], lamALL[:, :], AF.Exp),
              reads=["lamALL"], writes=["lamALL"])
        yield
        kb.op("act", lambda e: e.activation(sA[:, 0:GK], bt[:, 0:GK], AF.Exp, scale=S16),
              reads=GTW, writes=["sA"])
        kb.op("act", lambda e: e.activation(sB[:, 0:GK], bt[:, 0:GK], AF.Exp, scale=-S16),
              reads=GTW, writes=["sB"])
        yield

        def gen_gla(h):
            hS = h * ST
            hp_ = h % 2
            qe_, kt_, kd_, kdT_ = qe_t[hp_], kt_t[hp_], kd_t[hp_], kdT_t[hp_]
            rqe, rkt, rkd, rkdT = (f"qe_t{hp_}", f"kt_t{hp_}", f"kd_t{hp_}", f"kdT_t{hp_}")
            lamm_ = lamm_one
            rlam, rlamm = "lamALL", "lamm"
            pq, kq = (pA, "pA") if hp_ == 0 else (pC, "pC")
            pk_, kk_ = (pB, "pB") if hp_ == 0 else (pD, "pD")
            Ut, Ssc, d0m = Ut_l[hp_], Ssc_l[hp_], d0m_l[hp_]
            Sbc0 = Sbc0_l[hp_]
            kSb0 = f"Sbc0_{hp_}"
            kUt = kSsc = f"Ut{hp_}"
            kd0 = "d0m"
            kb.mmg([(pq[:, 0:ST], wq_sb[:, ci * DKQ + h * 128: ci * DKQ + (h + 1) * 128],
                     hpreF[:, ci * ST:(ci + 1) * ST]) for ci in range(8)],
                   reads=[hpk] + [f"w:wq:{ci}" for ci in range(8)], writes=[kq])
            kb.op("dve", lambda e, hS=hS, qe_=qe_, pq=pq: e.scalar_tensor_tensor(
                qe_[:, 0:ST], pq[:, 0:ST], SCL, sA[:, hS:hS + ST], ALU.mult, ALU.mult),
                reads=[kq, "sA"], writes=[rqe])
            kb.mmg([(pk_[:, 0:ST], wk_sb[:, ci * DKQ + h * 128: ci * DKQ + (h + 1) * 128],
                     hpreF[:, ci * ST:(ci + 1) * ST]) for ci in range(8)],
                   reads=[hpk] + [f"w:wk:{ci}" for ci in range(8)], writes=[kk_])
            kb.op("dve", lambda e, hS=hS, kt_=kt_, pk_=pk_: e.tensor_mul(
                kt_[:, 0:ST], pk_[:, 0:ST], sB[:, hS:hS + ST]),
                reads=[kk_, "sB"], writes=[rkt])
            yield
            for c in range(CST):
                kb.op("dve", lambda e, c=c, h=h, kd_=kd_, kt_=kt_: e.tensor_scalar_mul(
                    kd_[:, c * C:(c + 1) * C], kt_[:, c * C:(c + 1) * C],
                    lamALL[:, h * CST + c: h * CST + c + 1]),
                    reads=[rkt, rlam], writes=[rkd])
                yield
            for c in range(CST):
                ptr, ktr = (pTR, "pTR")
                kb.transpose(ptr[:, :], kd_[:, c * C:(c + 1) * C], ident[:, :],
                             reads=[rkd, "c:k_ident"], writes=[ktr])
                kb.op("dve", lambda e, c=c, ptr=ptr, kdT_=kdT_: e.tensor_copy(
                    kdT_[:, c * C:(c + 1) * C], ptr[:, :]),
                    reads=[ktr], writes=[rkdT])
                yield
            kb.op("act", lambda e, h=h: e.activation(
                Sbc0[:, :], Sprev[:, h * DV:(h + 1) * DV], AF.Identity),
                reads=[f"Sprev:{h}"], writes=[kSb0])
            pu, ku = (pO0, "pO0") if hp_ == 0 else (pO1, "pO1")
            for c in range(CST):
                kb.mmg([(pu[:, 0:DV], kdT_[:, c * C:(c + 1) * C],
                         vt[:, c * D + h * DV: c * D + (h + 1) * DV])],
                       reads=[rkdT, f"vt:{c}"], writes=[ku])
                udst = bass.AP(Ut, c, [[U_pitch, 128], [CST, DV]])
                if c == 0:
                    kb.op("dve", lambda e, udst=udst, h=h, pu=pu: e.scalar_tensor_tensor(
                        udst, Sprev[:, h * DV:(h + 1) * DV],
                        lamALL[:, h * CST: h * CST + 1],
                        pu[:, 0:DV], ALU.mult, ALU.add),
                        reads=[ku, f"Sprev:{h}", rlam], writes=[kUt])
                else:
                    kb.op("act", lambda e, udst=udst, pu=pu: e.activation(
                        udst, pu[:, 0:DV], AF.Identity),
                        reads=[ku], writes=[kUt])
                yield
            kb.op("dve", lambda e, h=h, lamm_=lamm_: e.tensor_copy(
                lamm_[:, 0:CST], lamALL[:, h * CST:(h + 1) * CST]),
                reads=[rlam], writes=[rlamm])
            kb.op("dve", lambda e, lamm_=lamm_: e.memset(lamm_[:, 0:1], 0.0), writes=[rlamm])
            d0v = bass.AP(d0m, 0, [[U_pitch, 128], [CST, DV], [1, CST]])
            lmv = bass.AP(lamm_, 0, [[CST, 128], [0, DV], [1, CST]])
            kb.op("dve", lambda e, d0v=d0v, lmv=lmv: e.tensor_copy(d0v, lmv),
                  reads=[rlamm], writes=[kd0])
            kb.op("dve", lambda e, d0m=d0m, Ssc=Ssc, Ut=Ut: e.tensor_tensor_scan(
                Ssc[:, 0:U_pitch], d0m[:, 0:U_pitch], Ut[:, 0:U_pitch], 0.0,
                ALU.mult, ALU.add),
                reads=[kd0, kUt], writes=[kSsc])
            yield
            kb.op("act", lambda e, h=h: e.activation(
                Sprev[:, h * DV:(h + 1) * DV],
                bass.AP(Ssc, CST - 1, [[U_pitch, 128], [CST, DV]]), AF.Identity),
                reads=[kSsc], writes=[f"Sprev:{h}"])
            yield
            for c in range(CST):
                pa_, ka_ = (pAA, "pAA")
                ai = c % 2
                as_, kas = asb[ai], f"asb{ai}"
                kb.mmg([(pa_[:, :], kt_[:, c * C:(c + 1) * C],
                         qe_[:, c * C:(c + 1) * C])],
                       reads=[rkt, rqe], writes=[ka_])
                kb.op("dve", lambda e, pa_=pa_, as_=as_: e.tensor_mul(
                    as_[:, :], pa_[:, :], maskA[:, :]),
                    reads=[ka_, "c:k_maskA"], writes=[kas])
                if c == 0:
                    sb_reads = [kSb0]
                    sbv = Sbc0[:, :]
                else:
                    sbci = Sbc_one
                    kb.op("act", lambda e, c=c, sbci=sbci: e.activation(
                        sbci[:, :], bass.AP(Ssc, c - 1, [[U_pitch, 128], [CST, DV]]),
                        AF.Identity),
                        reads=[kSsc], writes=["Sbc"])
                    sb_reads = ["Sbc"]
                    sbv = sbci[:, :]
                po, ko = pu, ku
                kb.mmg([(po[:, 0:DV], as_[:, :],
                         vt[:, c * D + h * DV: c * D + (h + 1) * DV]),
                        (po[:, 0:DV], qe_[:, c * C:(c + 1) * C], sbv)],
                       reads=[kas, f"vt:{c}", rqe] + sb_reads, writes=[ko])
                kb.op("dve", lambda e, c=c, h=h, po=po: e.tensor_mul(
                    og[:, c * D + h * DV: c * D + (h + 1) * DV], po[:, 0:DV],
                    gout[:, c * D + h * DV: c * D + (h + 1) * DV]),
                    reads=[ko], writes=[f"gout:{c}", f"poR{hp_}"])
                yield
                kb.op("act", lambda e, c=c, h=h, po=po: e.activation(
                    jt256[:, 0:DV], po[:, 0:DV], AF.Square,
                    accum_out=sst[:, c * NH + h: c * NH + h + 1]),
                    reads=[ko, f"poR{hp_}"], writes=["jt256", f"sst:{c}:{h}"])


        for hp in range(NH // 2):
            gA, gB = gen_gla(2 * hp), gen_gla(2 * hp + 1)
            live = [gA, gB]
            while live:
                for g_ in list(live):
                    try:
                        next(g_)
                    except StopIteration:
                        live.remove(g_)
                yield

        for c in range(CST):
            rs_, krs = rstd4[c % 2], f"rstd4_{c % 2}"
            sst_r = [f"sst:{c}:{h}" for h in range(NH)]
            kb.op("act", lambda e, c=c, rs_=rs_: e.activation(
                rs_[:, 0:NH], sst[:, c * NH:(c + 1) * NH], AF.Ln,
                scale=1.0 / DV, bias=B_EPSRMS),
                reads=sst_r + ["c:k_fconst"], writes=[krs])
            kb.op("act", lambda e, rs_=rs_: e.activation(rs_[:, 0:NH], rs_[:, 0:NH],
                                                AF.Exp, scale=-0.5),
                reads=[krs], writes=[krs])
            for h in range(NH):
                kb.op("dve", lambda e, c=c, h=h, rs_=rs_: e.tensor_scalar_mul(
                    og[:, c * D + h * DV: c * D + (h + 1) * DV],
                    og[:, c * D + h * DV: c * D + (h + 1) * DV],
                    rs_[:, h:h + 1]),
                    reads=[krs], writes=[f"gout:{c}"])
            yield

        kb.mmg([(pA[0:8, 0:ST], mwg_sb[:, ci * E:(ci + 1) * E],
                 hpreF[:, ci * ST:(ci + 1) * ST]) for ci in range(8)],
               reads=[hpk] + [f"w:mwg:{ci}" for ci in range(8)], writes=["pA"])
        kb.op("dve", lambda e: e.tensor_copy(lg[:, 0:ST], pA[0:8, 0:ST]),
              reads=["pA"], writes=["lg"])
        for c in range(CST):
            ic = st * CST + c
            pa_, ka_ = (pAA, "pAA")
            kb.transpose(pa_[:, 0:8], lg[:, c * C:(c + 1) * C], identF[0:8, 0:8],
                         reads=["lg", "c:k_identF"], writes=[ka_])
            kb.op("dve", lambda e, ic=ic, pa_=pa_: e.tensor_scalar_mul(
                lgtall[:, ic * 8:(ic + 1) * 8], pa_[:, 0:8], sig_h[:, ic:ic + 1]),
                reads=[ka_, f"m:sig_h:{ic}"], writes=[f"lgtall:{ic}"])
        yield

    ogF = sb2("ogF", (128, 8 * ST), BF16)
    xn_tm = [sb2(f"xn_tm{i}", (128, D), BF16) for i in range(2)]
    x2b = [sb2(f"x2b{i}", (128, D), BF16) for i in range(2)]
    npb = [sb2(f"npb{i}", (128, D), BF16) for i in range(2)]
    st3 = [sb2(f"st3_{i}", (128, 4), F32) for i in range(2)]
    sq3a = sqscr_one
    r8c = [sb2(f"r8c{i}", (128, 8), F32) for i in range(2)]
    r8d = [sb2(f"r8d{i}", (128, 8), F32) for i in range(2)]
    rcc = [sb2(f"rcc{i}", (128, 4), F32) for i in range(2)]
    jtc = [sb2(f"jtc{i}", (128, 8), F32) for i in range(2)]
    s_ldxn = kb.stream("qldxn", "sp", 2)
    s_stx2 = kb.stream("qstx2", "act", 2)
    s_stnp = kb.stream("qstnp", "act", 2)
    s_cc = kb.stream("qcc", "pool", 1)

    def emit_top2_chunk(ic):
        if True:
            cp = ic % 2
            lga = lgtall[:, ic * 8:(ic + 1) * 8]
            ra_, rb_ = r8c[cp], r8d[cp]
            kra, krb = f"r8c{cp}", f"r8d{cp}"
            rc_, jt_ = rcc[cp], jtc[cp]
            krc, kjt = f"rcc{cp}", f"jtc{cp}"
            kb.op("dve", lambda e, lga=lga, rc_=rc_: e.reduce_max(
                rc_[:, 0:1], lga, axis=AXX),
                reads=[f"lgtall:{ic}"], writes=[krc])
            kb.op("dve", lambda e, lga=lga, ra_=ra_, rc_=rc_: e.tensor_scalar(
                ra_[:, :], lga, rc_[:, 0:1], None, ALU.is_ge),
                reads=[f"lgtall:{ic}", krc], writes=[kra])
            kb.op("dve", lambda e, ra_=ra_, rb_=rb_: e.tensor_tensor_scan(
                rb_[:, :], ones8[:, :], ra_[:, :], 0.0, ALU.mult, ALU.add),
                reads=[kra, "c:k_ones8"], writes=[krb])
            kb.op("dve", lambda e, rb_=rb_: e.tensor_single_scalar(
                rb_[:, :], rb_[:, :], 1.0, ALU.is_le),
                reads=[krb], writes=[krb])
            kb.op("dve", lambda e, ic=ic, ra_=ra_, rb_=rb_: e.tensor_mul(
                oh0s[:, ic * 8:(ic + 1) * 8], ra_[:, :], rb_[:, :]),
                reads=[kra, krb], writes=["oh0s"])
            kb.op("dve", lambda e, ic=ic, ra_=ra_, lga=lga: e.scalar_tensor_tensor(
                ra_[:, :], oh0s[:, ic * 8:(ic + 1) * 8], -1e30, lga,
                ALU.mult, ALU.add),
                reads=["oh0s", f"lgtall:{ic}"], writes=[kra])
            kb.op("dve", lambda e, ra_=ra_, rc_=rc_: e.reduce_max(
                rc_[:, 1:2], ra_[:, :], axis=AXX),
                reads=[kra], writes=[krc])
            kb.op("dve", lambda e, ra_=ra_, rb_=rb_, rc_=rc_: e.tensor_scalar(
                rb_[:, :], ra_[:, :], rc_[:, 1:2], None, ALU.is_ge),
                reads=[kra, krc], writes=[krb])
            kb.op("dve", lambda e, ra_=ra_, rb_=rb_: e.tensor_tensor_scan(
                ra_[:, :], ones8[:, :], rb_[:, :], 0.0, ALU.mult, ALU.add),
                reads=[krb, "c:k_ones8"], writes=[kra])
            kb.op("dve", lambda e, ra_=ra_: e.tensor_single_scalar(
                ra_[:, :], ra_[:, :], 1.0, ALU.is_le),
                reads=[kra], writes=[kra])
            kb.op("dve", lambda e, ic=ic, ra_=ra_, rb_=rb_: e.tensor_mul(
                oh1s[:, ic * 8:(ic + 1) * 8], rb_[:, :], ra_[:, :]),
                reads=[kra, krb], writes=["oh1s"])
            kb.op("dve", lambda e, ic=ic, jt_=jt_: e.scalar_tensor_tensor(
                jt_[:, :], oh0s[:, ic * 8:(ic + 1) * 8], 1.0, iotaE[:, :],
                ALU.mult, ALU.mult, accum_out=e0t[:, ic:ic + 1]),
                reads=["oh0s", "c:k_iotaE"], writes=[kjt, "e0t"])
            kb.op("dve", lambda e, ic=ic, jt_=jt_: e.scalar_tensor_tensor(
                jt_[:, :], oh1s[:, ic * 8:(ic + 1) * 8], 1.0, iotaE[:, :],
                ALU.mult, ALU.mult, accum_out=e1t[:, ic:ic + 1]),
                reads=["oh1s", "c:k_iotaE"], writes=[kjt, "e1t"])
            kb.op("dve", lambda e, ic=ic, jt_=jt_, lga=lga, rc_=rc_: e.scalar_tensor_tensor(
                jt_[:, :], oh0s[:, ic * 8:(ic + 1) * 8], 1.0, lga,
                ALU.mult, ALU.mult, accum_out=rc_[:, 0:1]),
                reads=["oh0s", f"lgtall:{ic}"], writes=[kjt, krc])
            kb.op("dve", lambda e, ic=ic, jt_=jt_, lga=lga, rc_=rc_: e.scalar_tensor_tensor(
                jt_[:, :], oh1s[:, ic * 8:(ic + 1) * 8], 1.0, lga,
                ALU.mult, ALU.mult, accum_out=rc_[:, 1:2]),
                reads=["oh1s", f"lgtall:{ic}"], writes=[kjt, krc])
            kb.op("dve", lambda e, rc_=rc_: e.tensor_tensor(
                rc_[:, 2:3], rc_[:, 1:2], rc_[:, 0:1], ALU.subtract),
                reads=[krc], writes=[krc])
            kb.op("act", lambda e, rc_=rc_: e.activation(rc_[:, 2:3], rc_[:, 2:3], AF.Exp),
                  reads=[krc], writes=[krc])
            kb.op("dve", lambda e, rc_=rc_: e.tensor_scalar_add(rc_[:, 2:3], rc_[:, 2:3], 1.0),
                  reads=[krc], writes=[krc])
            kb.op("dve", lambda e, rc_=rc_: e.reciprocal(rc_[:, 2:3], rc_[:, 2:3]),
                  reads=[krc], writes=[krc])
            kb.op("dve", lambda e, ic=ic, rc_=rc_: e.tensor_copy(
                g0t[:, ic:ic + 1], rc_[:, 2:3]),
                reads=[krc], writes=["g0t"])
            kb.op("dve", lambda e, ic=ic, rc_=rc_: e.tensor_scalar(
                g1t[:, ic:ic + 1], rc_[:, 2:3], -1.0, 1.0, ALU.mult, ALU.add),
                reads=[krc], writes=["g1t"])
            
    def emit_p3a_st(st):
        t0 = st * ST
        for c in range(CST):
            for fb in range(8):
                kb.transpose(pTR[:, :], og[:, c * D + fb * 128: c * D + (fb + 1) * 128],
                             ident[:, :],
                             reads=[f"gout:{c}", "c:k_ident"], writes=["pTR"])
                kb.op("dve", lambda e, c=c, fb=fb: e.tensor_copy(
                    ogF[:, fb * ST + c * C: fb * ST + (c + 1) * C], pTR[:, :]),
                    reads=["pTR"], writes=["ogF"])
        for c in range(CST):
            cp = (st * CST + c) % 2
            xnt_, x2_, np_, s3 = xn_tm[cp], x2b[cp], npb[cp], st3[cp]
            kxn, kx2, knp, ks3 = (f"xn_tm{cp}", f"x2b{cp}", f"npb{cp}", f"st3_{cp}")
            tok = slice(t0 + c * C, t0 + (c + 1) * C)
            kb.dma(s_ldxn(st * CST + c), xnt_[:, 0:D], hbm_xn[tok, :],
                   reads=[f"h:xn:{st * CST + c}"], writes=[kxn])
            pL_, pH_, kL, kH = (pC, pD, "pC", "pD") if cp == 0 else (pA, pB, "pA", "pB")
            mml = []
            for half, ps in ((0, pL_), (1, pH_)):
                for ci in range(8):
                    mml.append((ps[:, 0:512],
                                ogF[:, ci * ST + c * C: ci * ST + (c + 1) * C],
                                wo_sb[:, ci * D + half * 512: ci * D + half * 512 + 512]))
            kb.mmg(mml, reads=["ogF"] + [f"w:wo:{ci}" for ci in range(8)],
                   writes=[kL, kH])
            for half, ps, pk in ((0, pL_, kL), (1, pH_, kH)):
                kb.op("dve", lambda e, ps=ps, half=half, x2_=x2_, xnt_=xnt_: e.scalar_tensor_tensor(
                    x2_[:, half * 512:(half + 1) * 512], ps[:, 0:512], 1.0,
                    xnt_[:, half * 512:(half + 1) * 512], ALU.mult, ALU.add),
                    reads=[pk, kxn], writes=[kx2])
            kb.dma(s_stx2(st * CST + c), hbm_x2[tok, :], x2_[:, :],
                   reads=[kx2], writes=[f"h:x2:{st * CST + c}"])
            kb.op("act", lambda e, x2_=x2_, s3=s3: e.activation(
                sq3a[:, :], x2_[:, :], AF.Square, accum_out=s3[:, 0:1]),
                reads=[kx2], writes=["sq3a", ks3])
            kb.op("act", lambda e, s3=s3: e.activation(s3[:, 1:2], s3[:, 0:1], AF.Ln,
                                                scale=1.0 / D, bias=B_EPSRMS),
                  reads=[ks3, "c:k_fconst"], writes=[ks3])
            kb.op("act", lambda e, s3=s3: e.activation(s3[:, 1:2], s3[:, 1:2], AF.Exp,
                                                scale=-0.5),
                  reads=[ks3], writes=[ks3])
            kb.op("dve", lambda e, x2_=x2_, np_=np_, s3=s3: e.tensor_scalar_mul(
                np_[:, :], x2_[:, :], s3[:, 1:2]),
                reads=[kx2, ks3], writes=[knp])
            kb.dma(s_stnp(st * CST + c), hbm_npre[tok, :], np_[:, :],
                   reads=[knp], writes=[f"h:np:{st * CST + c}"])
            yield

    def gen_aux(stp):
        gp = emit_p3a_st(stp)
        for c in range(CST):
            next(gp)
            yield
            emit_top2_chunk(stp * CST + c)
            yield

    for i in range(min(2 * CST, NC)):
        emit_p1_tile(i)
    p1_next = min(2 * CST, NC)
    for st in range(NST + 1):
        gens = []
        if st >= 1:
            gens.append(gen_aux(st - 1))
        if st < NST:
            gens.append(emit_p2_st(st))
        p1_hi = min((st + 3) * CST, NC)
        live = list(gens)
        rnd = 0
        aux_started = False
        while live:
            rnd += 1
            for g_ in list(live):
                if (len(gens) > 1 and g_ is gens[0] and aux_started
                        and rnd % 3 != 0 and len(live) > 1):
                    continue
                try:
                    next(g_)
                    if len(gens) > 1 and g_ is gens[0]:
                        aux_started = True
                except StopIteration:
                    live.remove(g_)
            if p1_next < p1_hi:
                emit_p1_tile(p1_next)
                p1_next += 1

    def emit_colsums_cc():
     for ic in range(NC):
         for ti, (ohl, ohk, tabC, tk) in enumerate((
                 ([oh0s], ["oh0s"], tabC0, "tabC0"),
                 ([oh1s], ["oh1s"], tabC1, "tabC1"),
                 ([oh0s, oh1s], ["oh0s", "oh1s"], tabC01, "tabC01"))):
             pcs, kcs = (pO0, "pO0") if ti % 2 == 0 else (pO1, "pO1")
             kb.mmg([(pcs[0:1, 0:8], onescol[:, :], ohs[:, ic * 8:(ic + 1) * 8])
                     for ohs in ohl],
                    reads=ohk + ["c:k_onescol"], writes=[kcs])
             kb.op("dve", lambda e, tabC=tabC, ic=ic, pcs=pcs: e.tensor_copy(
                 bass.AP(tabC, ic, [[8 * NC, 1], [NC, 8]]), pcs[0:1, 0:8]),
                 reads=[kcs], writes=[tk])
     for tabC, tk in ((tabC0, "tabC0"), (tabC1, "tabC1"), (tabC01, "tabC01")):
         kb.op("dve", lambda e, tabC=tabC: e.tensor_tensor_scan(
             tabC[:, :], rowmask[:, :], tabC[:, :], 0.0, ALU.mult, ALU.add),
             reads=[tk, "c:k_rowmask"], writes=[tk])
     kb.op("dve", lambda e: e.tensor_copy(
         cnt_loc[:, 0:8], bass.AP(tabC0, NC - 1, [[8 * NC, 1], [NC, 8]])),
         reads=["tabC0"], writes=["cnt_loc"])
     kb.op("dve", lambda e: e.tensor_copy(
         cnt_loc[:, 8:16], bass.AP(tabC1, NC - 1, [[8 * NC, 1], [NC, 8]])),
         reads=["tabC1"], writes=["cnt_loc"])
     kb.mmg([(pA[0:8, 0:16], core_oh[:, :], cnt_loc[:, :])],
            reads=["cnt_loc", "c:k_core_oh"], writes=["pA"])
     kb.op("dve", lambda e: e.tensor_copy(ccin_sb[:, :], pA[0:8, 0:16]),
           reads=["pA"], writes=["ccin_sb"])
     kb.dma(s_cc(0), hbm_ccin[:, :], ccin_sb[:, :], reads=["ccin_sb"], writes=["h:ccin"])
     kb.collective("AllReduce", ALU.add, [list(range(NCHIP))],
                   [hbm_ccin.ap().opt()], [hbm_ccout.ap().opt()],
                   reads=["h:ccin"], writes=["h:ccout"])

    emit_colsums_cc()

    kb.barrier()
    es2.close()

    capok0 = sb("capok0", (128, NC), F32)
    capok1 = sb("capok1", (128, NC), F32)
    tokf = sb("tokf", (128, 1), F32)
    toki = sb("toki", (128, 1), I32)
    sgf = sb("sgf", (128, 2), F32)
    sgi = sb("sgi", (128, 2), I32)
    isc1 = sb("isc1", (128, 2), I32)
    isc2 = sb("isc2", (128, 2), I32)
    s_sc = kb.stream("qsc", "pool", 2)
    bc_slots = kb.e["pool"].to_reg(SLOTS - 1)
    bc_tok = kb.e["pool"].to_reg(T - 1)
    es3b, sb3b = mk_scope()
    NI = IMLP // 128
    wga_sb = sb3b("wga_sb", (128, 8 * IMLP), BF16)
    wup_sb = sb3b("wup_sb", (128, 8 * IMLP), BF16)
    wdn_sb = sb3b("wdn_sb", (128, NI * D), BF16)
    kb.dma(s_ldw(0), wga_sb[:, :],
           bass.AP(wgate_h, 0, [[IMLP, 128], [128 * IMLP, 8], [1, IMLP]]),
           writes=[f"w:wga:{ci}" for ci in range(8)])
    kb.dma(s_ldw(1), wup_sb[:, :],
           bass.AP(wup_h, 0, [[IMLP, 128], [128 * IMLP, 8], [1, IMLP]]),
           writes=[f"w:wup:{ci}" for ci in range(8)])
    kb.dma(s_ldw(0), wdn_sb[:, :],
           bass.AP(wdown_h, 0, [[D, 128], [128 * D, NI], [1, D]]),
           writes=[f"w:wdn:{ci}" for ci in range(NI)])
    for ci in range(8):
        kb.op("dve", lambda e, ci=ci: e.tensor_scalar_mul(
            wga_sb[:, ci * IMLP:(ci + 1) * IMLP], wga_sb[:, ci * IMLP:(ci + 1) * IMLP],
            mnw[:, ci:ci + 1]),
            reads=[f"w:wga:{ci}", "c:k_mnw"], writes=[f"w:wga:{ci}"])
        kb.op("dve", lambda e, ci=ci: e.tensor_scalar_mul(
            wup_sb[:, ci * IMLP:(ci + 1) * IMLP], wup_sb[:, ci * IMLP:(ci + 1) * IMLP],
            mnw[:, ci:ci + 1]),
            reads=[f"w:wup:{ci}", "c:k_mnw"], writes=[f"w:wup:{ci}"])
    tokt = sb3b("tokt", (128, SLOTS // 128), I32)
    kb.op("dve", lambda e: e.memset(tokt[:, :], 0), writes=["tokt"])
    kb.dma(s_sc(0), bass.AP(hbm_tok, 0, [[SLOTS // 128, 128], [1, SLOTS // 128]]),
           tokt[:, :], reads=["tokt"], writes=["h:tok:z"])

    def emit_p4b_chunk(ic):
        o8 = slice(ic * 8, (ic + 1) * 8)
        mml = [(pO0[:, 0:8], tril128s[:, :], oh0s[:, o8]),
               (pO0[:, 0:8], tril128s[:, :], oh1s[:, o8])]
        if ic > 0:
            mml.append((pO0[:, 0:8], ones1x128[:, :],
                        bass.AP(tabC01, ic - 1, [[8 * NC, 1], [NC, 8]])))
        kb.mmg(mml, reads=["oh0s", "oh1s", "tabC01", "c:k_tril128s", "c:k_ones1x128"],
               writes=["pO0"])
        kb.op("dve", lambda e, ic=ic, o8=o8: e.scalar_tensor_tensor(
            jt8[:, :], oh0s[:, o8], 1.0, pO0[:, 0:8], ALU.mult, ALU.mult,
            accum_out=slot0t[:, ic:ic + 1]),
            reads=["oh0s", "pO0"], writes=["jt8", "slot0t"])
        kb.op("dve", lambda e, ic=ic, o8=o8: e.scalar_tensor_tensor(
            jt8[:, :], oh1s[:, o8], 1.0, pO0[:, 0:8], ALU.mult, ALU.mult,
            accum_out=slot1t[:, ic:ic + 1]),
            reads=["oh1s", "pO0"], writes=["jt8", "slot1t"])
        mml = [(pO1[:, 0:8], tril128s[:, :], oh0s[:, o8])]
        if ic > 0:
            mml.append((pO1[:, 0:8], ones1x128[:, :],
                        bass.AP(tabC0, ic - 1, [[8 * NC, 1], [NC, 8]])))
        kb.mmg(mml, reads=["oh0s", "tabC0", "c:k_tril128s", "c:k_ones1x128"],
               writes=["pO1"])
        kb.op("dve", lambda e, ic=ic, o8=o8: e.scalar_tensor_tensor(
            jt8[:, :], oh0s[:, o8], 1.0, pO1[:, 0:8], ALU.mult, ALU.mult,
            accum_out=pre0t[:, ic:ic + 1]),
            reads=["oh0s", "pO1"], writes=["jt8", "pre0t"])
        mml = [(pO1[:, 0:8], tril128s[:, :], oh1s[:, o8])]
        if ic > 0:
            mml.append((pO1[:, 0:8], ones1x128[:, :],
                        bass.AP(tabC1, ic - 1, [[8 * NC, 1], [NC, 8]])))
        kb.mmg(mml, reads=["oh1s", "tabC1", "c:k_tril128s", "c:k_ones1x128"],
               writes=["pO1"])
        kb.op("dve", lambda e, ic=ic, o8=o8: e.scalar_tensor_tensor(
            jt8[:, :], oh1s[:, o8], 1.0, pO1[:, 0:8], ALU.mult, ALU.mult,
            accum_out=pre1t[:, ic:ic + 1]),
            reads=["oh1s", "pO1"], writes=["jt8", "pre1t"])
        kb.op("dve", lambda e, ic=ic: e.tensor_single_scalar(
            capok0[:, ic:ic + 1], slot0t[:, ic:ic + 1], float(CAP), ALU.is_lt),
            reads=["slot0t"], writes=["capok0"])
        kb.op("dve", lambda e, ic=ic: e.tensor_single_scalar(
            capok1[:, ic:ic + 1], slot1t[:, ic:ic + 1], float(CAP), ALU.is_lt),
            reads=["slot1t"], writes=["capok1"])
        kb.op("dve", lambda e, ic=ic: e.scalar_tensor_tensor(
            sgf[:, 0:1], e0t[:, ic:ic + 1], float(CAP), slot0t[:, ic:ic + 1],
            ALU.mult, ALU.add),
            reads=["e0t", "slot0t"], writes=["sgf"])
        kb.op("dve", lambda e, ic=ic: e.scalar_tensor_tensor(
            sgf[:, 1:2], e1t[:, ic:ic + 1], float(CAP), slot1t[:, ic:ic + 1],
            ALU.mult, ALU.add),
            reads=["e1t", "slot1t"], writes=["sgf"])
        for col, cap_t, cap_r in ((0, capok0, "capok0"), (1, capok1, "capok1")):
            kb.op("dve", lambda e, ic=ic, col=col, cap_t=cap_t: e.scalar_tensor_tensor(
                sgf[:, col:col + 1], cap_t[:, ic:ic + 1], 1e7, sgf[:, col:col + 1],
                ALU.mult, ALU.subtract),
                reads=[cap_r, "sgf"], writes=["sgf"])
            kb.op("dve", lambda e, col=col: e.tensor_scalar(
                sgf[:, col:col + 1], sgf[:, col:col + 1], -1.0, 1e7, ALU.mult, ALU.add),
                reads=["sgf"], writes=["sgf"])
        kb.op("dve", lambda e, ic=ic: e.tensor_copy(sgi[:, :], sgf[:, :]),
              reads=["sgf"], writes=["sgi"])
        kb.op("dve", lambda e, ic=ic: e.tensor_copy(slotg0i[:, ic:ic + 1], sgi[:, 0:1]),
              reads=["sgi"], writes=["slotg0i"])
        kb.op("dve", lambda e, ic=ic: e.tensor_copy(slotg1i[:, ic:ic + 1], sgi[:, 1:2]),
              reads=["sgi"], writes=["slotg1i"])
        kb.op("dve", lambda e, ic=ic: e.tensor_scalar_add(
            tokf[:, :], iota_col[:, :], float(ic * 128)),
            reads=["c:k_iota_col"], writes=["tokf"])
        kb.op("dve", lambda e: e.tensor_copy(toki[:, :], tokf[:, :]),
              reads=["tokf"], writes=["toki"])
        for kk, (sgcol, sgrg) in enumerate(((slotg0i, "slotg0i"), (slotg1i, "slotg1i"))):
            kb.op("dve", lambda e, sgcol=sgcol, ic=ic: e.tensor_single_scalar(
                isc1[:, 0:1], sgcol[:, ic:ic + 1], 127, ALU.bitwise_and),
                reads=[sgrg], writes=["isc1"])
            kb.op("dve", lambda e, sgcol=sgcol, ic=ic: e.tensor_single_scalar(
                isc2[:, 0:1], sgcol[:, ic:ic + 1], 7, ALU.arith_shift_right),
                reads=[sgrg], writes=["isc2"])
            kb.op("dve", lambda e: e.tensor_scalar(
                isc1[:, 0:1], isc1[:, 0:1], SLOTS // 128, None, ALU.mult),
                reads=["isc1"], writes=["isc1"])
            kb.op("dve", lambda e: e.tensor_add(isc1[:, 0:1], isc1[:, 0:1], isc2[:, 0:1]),
                  reads=["isc1", "isc2"], writes=["isc1"])
            kb.idma(s_sc(ic + kk), bass.AP(hbm_tok, 0, [[1, SLOTS], [1, 1]]),
                    IndirectOffsetOnAxis(isc1[:, 0:1], 0), toki[:, :], None,
                    bounds_check=bc_slots, oob_is_err=False,
                    reads=["isc1", "toki", "h:tok:z"], writes=[f"h:tok:{ic}:{kk}"])

    xg = [sb3b(f"xg{i}", (128, D), BF16) for i in range(2)]
    s_gx = kb.stream("qgx", "pool", 2)
    s_sx = kb.stream("qsx", "sp", 2)
    def emit_p5_head():
        kb.dma(s_sc(0), tokt[:, :],
               bass.AP(hbm_tok, 0, [[SLOTS // 128, 128], [1, SLOTS // 128]]),
               reads=["h:tok:z"] + [f"h:tok:{ic}:{kk}" for ic in range(NC) for kk in range(2)],
               writes=["tokt"])

    def emit_p5_blk(blk):
        bb = blk % 2
        kb.idma(s_gx(blk), xg[bb][:, :], None, hbm_xn[:, :],
                IndirectOffsetOnAxis(tokt[:, blk:blk + 1], 0),
                bounds_check=bc_tok, oob_is_err=False,
                reads=["tokt", "h:xn"], writes=[f"xg{bb}"])
        kb.dma(s_sx(blk), hbm_xe[blk * 128:(blk + 1) * 128, :], xg[bb][:, :],
               reads=[f"xg{bb}"], writes=[f"h:xe:{blk}"])

    npF_l = [sb3b(f"npF{i}", (128, 8 * ST), BF16) for i in range(2)]
    mt = sb3b("mt", (128, NI * ST), BF16)
    sgl_l = [sb3b(f"sgl{i}", (128, 512), F32) for i in range(2)]
    sbuf_s_l = [sb3b(f"sbuf_s{i}", (128, D), BF16) for i in range(2)]
    x2_tm_one = sb3b("x2_tm", (128, D), BF16)
    x2_tm = [x2_tm_one, x2_tm_one]
    s_ldnp = kb.stream("qldnp", "sp", 2)
    s_ldx2 = kb.stream("qldx2", "sp", 2)
    s_sts = kb.stream("qsts", "act", 2)
    def emit_p3b_ld(st):
        t0 = st * ST
        npF = npF_l[st % 2]
        npk = f"npF{st % 2}"
        np_r = [f"h:np:{j}" for j in range(st * CST, (st + 1) * CST)]
        for ci in range(8):
            kb.dma(s_ldnp(st), npF[:, ci * ST:(ci + 1) * ST],
                   hbm_npre[t0:t0 + ST, ci * 128:(ci + 1) * 128],
                   reads=np_r, writes=[npk], transpose=True)

    def emit_p3b_st(st):
        t0 = st * ST
        npF = npF_l[st % 2]
        npk = f"npF{st % 2}"
        for ii in range(NI):
            if ii == 2 and st + 1 < NST:
                emit_p3b_ld(st + 1)
            pg, kg, pu_, ku_ = (pA, "pA", pB, "pB") if ii % 2 == 0 else (pC, "pC", pD, "pD")
            sg_, ksg = sgl_l[ii % 2], f"sgl{ii % 2}"
            kb.mmg([(pg[:, 0:ST], wga_sb[:, ci * IMLP + ii * 128: ci * IMLP + (ii + 1) * 128],
                     npF[:, ci * ST:(ci + 1) * ST]) for ci in range(8)],
                   reads=[npk] + [f"w:wga:{ci}" for ci in range(8)], writes=[kg])
            kb.mmg([(pu_[:, 0:ST], wup_sb[:, ci * IMLP + ii * 128: ci * IMLP + (ii + 1) * 128],
                     npF[:, ci * ST:(ci + 1) * ST]) for ci in range(8)],
                   reads=[npk] + [f"w:wup:{ci}" for ci in range(8)], writes=[ku_])
            kb.op("act", lambda e, pg=pg, sg_=sg_: e.activation(
                sg_[:, 0:ST], pg[:, 0:ST], AF.Silu),
                reads=[kg], writes=[ksg])
            kb.op("dve", lambda e, ii=ii, sg_=sg_, pu_=pu_: e.tensor_mul(
                mt[:, ii * ST:(ii + 1) * ST], sg_[:, 0:ST], pu_[:, 0:ST]),
                reads=[ksg, ku_], writes=["mt"])
            yield
        for c in range(CST):
            x2t, kx2t = x2_tm_one, "x2_tm"
            sbuf_s, ksbs = sbuf_s_l[c % 2], f"sbuf_s{c % 2}"
            tok = slice(t0 + c * C, t0 + (c + 1) * C)
            kb.dma(s_ldx2(st * CST + c), x2t[:, :], hbm_x2[tok, :],
                   reads=[f"h:x2:{st * CST + c}"], writes=[kx2t])
            pd0, pd1 = (pA, pB) if c % 2 == 0 else (pC, pD)
            kpd0, kpd1 = ("pA", "pB") if c % 2 == 0 else ("pC", "pD")
            mml = []
            for half, ps in ((0, pd0), (1, pd1)):
                for ii in range(NI):
                    mml.append((ps[:, 0:512],
                                mt[:, ii * ST + c * C: ii * ST + (c + 1) * C],
                                wdn_sb[:, ii * D + half * 512: ii * D + half * 512 + 512]))
            kb.mmg(mml, reads=["mt"] + [f"w:wdn:{ci}" for ci in range(NI)],
                   writes=[kpd0, kpd1])
            for half, ps, pk in ((0, pd0, kpd0), (1, pd1, kpd1)):
                kb.op("dve", lambda e, ps=ps, half=half, x2t=x2t, sbuf_s=sbuf_s: e.scalar_tensor_tensor(
                    sbuf_s[:, half * 512:(half + 1) * 512], ps[:, 0:512], 1.0,
                    x2t[:, half * 512:(half + 1) * 512], ALU.mult, ALU.add),
                    reads=[pk, kx2t], writes=[ksbs])
            kb.dma(s_sts(st * CST + c), hbm_s[tok, :], sbuf_s[:, :],
                   reads=[ksbs], writes=[f"h:s:{st * CST + c}"])
            yield

    def gen_p4b5():
        for ic in range(NC):
            emit_p4b_chunk(ic)
            yield
        emit_p5_head()
        for blk in range(SLOTS // 128):
            emit_p5_blk(blk)
            if blk % 2:
                yield

    def gen_p3b_all():
        emit_p3b_ld(0)
        for st in range(NST):
            yield from emit_p3b_st(st)

    gA_, gB_ = gen_p4b5(), gen_p3b_all()
    live = [gA_, gB_]
    turn = 0
    while live:
        turn += 1
        for g in list(live):
            if g is gA_ and turn % 3 != 0:
                continue
            try:
                next(g)
            except StopIteration:
                live.remove(g)

    kb.barrier()
    es3b.close()

    kb.dma(s_cc(0), ccout_sb[:, :], hbm_ccout[:, :], reads=["h:ccout"], writes=["ccout_sb"])
    kb.mmg([(pA[0:1, 0:8], core_pm[:, :], ccout_sb[:, 0:8])],
           reads=["ccout_sb", "c:k_core_pm"], writes=["pA"])
    kb.op("dve", lambda e: e.tensor_copy(csc[:, 0:8], pA[0:1, 0:8]),
          reads=["pA"], writes=["csc"])
    kb.mmg([(pA[0:1, 0:8], ones8col[:, :], ccout_sb[:, 0:8]),
            (pA[0:1, 0:8], core_pm[:, :], ccout_sb[:, 8:16])],
           reads=["ccout_sb", "c:k_ones8col", "c:k_core_pm"], writes=["pA"])
    kb.op("dve", lambda e: e.tensor_copy(csc[:, 8:16], pA[0:1, 0:8]),
          reads=["pA"], writes=["csc"])
    kb.mmg([(pA[:, 0:8], ones1x128[:, :], csc[:, 0:8])],
           reads=["csc", "c:k_ones1x128"], writes=["pA"])
    kb.op("dve", lambda e: e.tensor_copy(gb0[:, :], pA[:, 0:8]),
          reads=["pA"], writes=["gb0"])
    kb.mmg([(pA[:, 0:8], ones1x128[:, :], csc[:, 8:16])],
           reads=["csc", "c:k_ones1x128"], writes=["pA"])
    kb.op("dve", lambda e: e.tensor_copy(gb1[:, :], pA[:, 0:8]),
          reads=["pA"], writes=["gb1"])
    for ic in range(NC):
        o8 = slice(ic * 8, (ic + 1) * 8)
        kb.op("dve", lambda e, o8=o8: e.scalar_tensor_tensor(
            jt8[:, :], oh0s[:, o8], 1.0, gb0[:, :], ALU.mult, ALU.mult,
            accum_out=rc0[:, :]),
            reads=["oh0s", "gb0"], writes=["jt8", "rc0"])
        kb.op("dve", lambda e, ic=ic: e.tensor_add(rc0[:, :], rc0[:, :],
                                                   pre0t[:, ic:ic + 1]),
              reads=["rc0", "pre0t"], writes=["rc0"])
        kb.op("dve", lambda e: e.tensor_single_scalar(rc0[:, :], rc0[:, :],
                                                      float(GCAP), ALU.is_lt),
              reads=["rc0"], writes=["rc0"])
        kb.op("dve", lambda e, ic=ic: e.tensor_mul(rc0[:, :], rc0[:, :],
                                                   capok0[:, ic:ic + 1]),
              reads=["rc0", "capok0"], writes=["rc0"])
        kb.op("dve", lambda e, ic=ic: e.tensor_mul(g0vt[:, ic:ic + 1],
                                                   rc0[:, :], g0t[:, ic:ic + 1]),
              reads=["rc0", "g0t"], writes=["g0vt"])
        kb.op("dve", lambda e, o8=o8: e.scalar_tensor_tensor(
            jt8[:, :], oh1s[:, o8], 1.0, gb1[:, :], ALU.mult, ALU.mult,
            accum_out=rc1[:, :]),
            reads=["oh1s", "gb1"], writes=["jt8", "rc1"])
        kb.op("dve", lambda e, ic=ic: e.tensor_add(rc1[:, :], rc1[:, :],
                                                   pre1t[:, ic:ic + 1]),
              reads=["rc1", "pre1t"], writes=["rc1"])
        kb.op("dve", lambda e: e.tensor_single_scalar(rc1[:, :], rc1[:, :],
                                                      float(GCAP), ALU.is_lt),
              reads=["rc1"], writes=["rc1"])
        kb.op("dve", lambda e, ic=ic: e.tensor_mul(rc1[:, :], rc1[:, :],
                                                   capok1[:, ic:ic + 1]),
              reads=["rc1", "capok1"], writes=["rc1"])
        kb.op("dve", lambda e, ic=ic: e.tensor_mul(g1vt[:, ic:ic + 1],
                                                   rc1[:, :], g1t[:, ic:ic + 1]),
              reads=["rc1", "g1t"], writes=["g1vt"])


    es6, sb6 = mk_scope()
    NHC = HMOE // 128
    BLKS = []
    _r = 0
    while _r < CAP:
        BLKS.append((len(BLKS), min(NBLK, CAP - _r)))
        _r += NBLK
    NB6 = len(BLKS)
    assert CAP % 128 == 0 and NBLK % 128 == 0
    w1_sb = sb6("w1_sb", (128, 8 * HMOE), BF16)
    w2_sb = sb6("w2_sb", (128, NHC * D), BF16)
    b1_sb = sb6("b1_sb", (128, NHC), F32)
    b2r = sb6("b2r", (1, D), BF16)
    ones1b = sb6("ones1b", (1, 128), BF16)
    kb.op("dve", lambda e: e.memset(ones1b[:, :], 1.0), writes=["c:ones1b"])
    XeFb = [sb6(f"XeF{i}", (128, 8 * NBLK), BF16) for i in range(2)]
    h1_sb = sb6("h1_sb", (128, NHC * NBLK), BF16)
    ye_sb = [sb6("ye_sb0", (128, D), BF16)]
    s_ldw1 = kb.stream("qldw1", "pool", 2)
    s_ldw2 = kb.stream("qldw2", "pool", 2)
    s_ldxe = kb.stream("qldxe", "sp", 2)
    s_stye = kb.stream("qstye", "act", 2)

    def ld_w1(e_):
        kb.dma(s_ldw1(e_), w1_sb[:, :],
               bass.AP(w1_h, e_ * D * HMOE,
                       [[HMOE, 128], [128 * HMOE, 8], [1, HMOE]]),
               writes=["w:w1"])
        kb.dma(s_ldw1(e_), b1_sb[:, :], b1_h[e_, :, :], writes=["w:b1"])

    def ld_w2(e_):
        kb.dma(s_ldw2(e_), w2_sb[:, :],
               bass.AP(w2_h, e_ * HMOE * D,
                       [[D, 128], [128 * D, NHC], [1, D]]),
               writes=["w:w2"])
        kb.dma(s_ldw2(e_), b2r[:, :], b2_h[e_:e_ + 1, :], writes=["w:b2"])

    def ld_xe(e_, blk, bs):
        r0 = e_ * CAP + blk * NBLK
        gb = e_ * NB6 + blk
        xe_r = [f"h:xe:{bb}" for bb in range(r0 // 128, (r0 + bs) // 128)]
        Xe = XeFb[gb % 2]
        for ci in range(8):
            kb.dma(s_ldxe(gb), Xe[:, ci * NBLK: ci * NBLK + bs],
                   hbm_xe[r0:r0 + bs, ci * 128:(ci + 1) * 128],
                   reads=xe_r, writes=[f"XeF{gb % 2}"], transpose=True)

    ld_w1(0)
    ld_w2(0)
    ld_xe(0, 0, BLKS[0][1])
    for e_ in range(E):
        for blk, bs in BLKS:
            gb = e_ * NB6 + blk
            if blk + 1 < NB6:
                ld_xe(e_, blk + 1, BLKS[blk + 1][1])
            elif e_ + 1 < E:
                ld_xe(e_ + 1, 0, BLKS[0][1])
            Xe = XeFb[gb % 2]
            xrg = f"XeF{gb % 2}"
            for hc in range(NHC):
                ps, pk = (pA, "pA") if hc % 2 == 0 else (pB, "pB")
                kb.mmg([(ps[:, 0:bs],
                         w1_sb[:, ci * HMOE + hc * 128: ci * HMOE + (hc + 1) * 128],
                         Xe[:, ci * NBLK: ci * NBLK + bs]) for ci in range(8)],
                       reads=[xrg, "w:w1"], writes=[pk])
                kb.op("act", lambda e, ps=ps, hc=hc, bs=bs: e.activation(
                    h1_sb[:, hc * NBLK: hc * NBLK + bs], ps[:, 0:bs],
                    AF.Relu, bias=b1_sb[:, hc:hc + 1]),
                    reads=[pk, "w:b1"], writes=[f"h1:{hc}"])
            if blk == NB6 - 1 and e_ + 1 < E:
                ld_w1(e_ + 1)
            h1r = [f"h1:{hc}" for hc in range(NHC)]
            r0 = e_ * CAP + blk * NBLK
            for g in range(bs // 128):
                pcs = (pC, pD, "pC", "pD") if g % 2 == 0 else (pA, pB, "pA", "pB")
                psl, psh, pkl, pkh = pcs
                mml = [(psl[:, 0:512], ones1b[:, :], b2r[:, 0:512]),
                       (psh[:, 0:512], ones1b[:, :], b2r[:, 512:1024])]
                for hc in range(NHC):
                    lh = h1_sb[:, hc * NBLK + g * 128: hc * NBLK + (g + 1) * 128]
                    mml.append((psl[:, 0:512], lh, w2_sb[:, hc * D: hc * D + 512]))
                    mml.append((psh[:, 0:512], lh, w2_sb[:, hc * D + 512: hc * D + 1024]))
                kb.mmg(mml, reads=h1r + ["w:w2", "w:b2", "c:ones1b"],
                       writes=[pkl, pkh])
                yb = ye_sb[0]
                for half, ps, pk in ((0, psl, pkl), (1, psh, pkh)):
                    kb.op("dve", lambda e, ps=ps, half=half, yb=yb: e.tensor_copy(
                        yb[:, half * 512:(half + 1) * 512], ps[:, 0:512]),
                        reads=[pk], writes=["ye_sb0"])
                rr = r0 + g * 128
                kb.dma(s_stye(g), hbm_ye[rr:rr + 128, :], yb[:, :],
                       reads=["ye_sb0"], writes=[f"h:ye:{rr}"])
        if e_ + 1 < E:
            ld_w2(e_ + 1)

    kb.barrier()
    es6.close()

    NP7 = 4
    r0b = [sb(f"r0b{i}", (128, D), BF16) for i in range(NP7)]
    r1b = [sb(f"r1b{i}", (128, D), BF16) for i in range(NP7)]
    yv = [sb(f"yv{i}", (128, D), F32) for i in range(NP7)]
    gv = [sb(f"gv{i}", (128, D), F32) for i in range(NP7)]
    s_tm = [sb(f"s_tm{i}", (128, D), BF16) for i in range(NP7)]
    x_tm2 = [sb(f"x_tm2{i}", (128, D), F32) for i in range(NP7)]
    outb = [sb(f"outb{i}", (128, D), F32) for i in range(NP7)]
    s_gy0 = kb.stream("qgy0", "pool", NP7)
    s_lds = kb.stream("qlds", "sp", NP7)
    s_ldxf = kb.stream("qldxf", "act", NP7)
    s_out = kb.stream("qout", "act", 2)
    for ic in range(NC):
        cp = ic % NP7
        R0, R1, YV, GV, SS, XX, OB = (r0b[cp], r1b[cp], yv[cp], gv[cp],
                                      s_tm[cp], x_tm2[cp], outb[cp])
        kR0, kR1, kYV, kGV, kSS, kXX, kOB = (f"r0b{cp}", f"r1b{cp}", f"yv{cp}",
                                             f"gv{cp}", f"s_tm{cp}", f"x_tm2{cp}",
                                             f"outb{cp}")
        tok = slice(ic * 128, (ic + 1) * 128)
        kb.idma(s_gy0(ic), R0[:, :], None, hbm_ye[:, :],
                IndirectOffsetOnAxis(slotg0i[:, ic:ic + 1], 0),
                bounds_check=bc_slots, oob_is_err=False,
                reads=["slotg0i"], writes=[kR0])
        kb.idma(s_gy0(ic), R1[:, :], None, hbm_ye[:, :],
                IndirectOffsetOnAxis(slotg1i[:, ic:ic + 1], 0),
                bounds_check=bc_slots, oob_is_err=False,
                reads=["slotg1i"], writes=[kR1])
        kb.dma(s_lds(ic), SS[:, :], hbm_s[tok, :], reads=[f"h:s:{ic}"], writes=[kSS])
        kb.dma(s_ldxf(ic), XX[:, :], x_h[tok, :], writes=[kXX])
        kb.op("dve", lambda e, ic=ic, R0=R0, YV=YV: e.tensor_scalar_mul(
            YV[:, :], R0[:, :], g0vt[:, ic:ic + 1]),
            reads=[kR0, "g0vt"], writes=[kYV])
        kb.op("dve", lambda e, ic=ic, R1=R1, YV=YV: e.scalar_tensor_tensor(
            YV[:, :], R1[:, :], g1vt[:, ic:ic + 1], YV[:, :], ALU.mult, ALU.add),
            reads=[kR1, "g1vt", kYV], writes=[kYV])
        kb.op("act", lambda e, YV=YV, GV=GV: e.activation(GV[:, :], YV[:, :], AF.Silu),
              reads=[kYV], writes=[kGV])
        kb.op("dve", lambda e, GV=GV, SS=SS: e.tensor_mul(GV[:, :], GV[:, :], SS[:, :]),
              reads=[kGV, kSS], writes=[kGV])
        kb.op("dve", lambda e, GV=GV, XX=XX, OB=OB: e.tensor_add(OB[:, :], GV[:, :], XX[:, :]),
              reads=[kGV, kXX], writes=[kOB])
        kb.dma(s_out(ic), out_h[tok, :], OB[:, :], reads=[kOB], writes=["h:out"])

    kb.finish()
    es.close()
    return nc



def _consts(T, ST):
    bf = ml_dtypes.bfloat16
    c = {}
    c["ident"] = np.eye(128, dtype=bf)
    s = np.arange(128)[:, None]; t = np.arange(128)[None, :]
    c["maskA"] = (s <= t).astype(bf)
    c["tril128s"] = (s < t).astype(np.float32)
    c["ones1x128"] = np.ones((1, 128), np.float32)
    c["onescol"] = np.ones((128, 1), np.float32)
    c["iotaE"] = np.tile(np.arange(8, dtype=np.float32)[None, :], (128, 1))
    c["iota_col"] = np.arange(128, dtype=np.float32)[:, None]
    c["ones8"] = np.ones((128, 8), np.float32)
    sm = np.ones((128, 4 * ST), ml_dtypes.bfloat16)
    sm[:, ::C] = 0.0
    c["scanmask"] = sm
    c["ones8col"] = np.ones((8, 1), np.float32)
    NCc = T // C
    rm = np.ones((1, 8 * NCc), ml_dtypes.bfloat16)
    rm[0, ::NCc] = 0.0
    c["rowmask"] = rm
    c["identF"] = np.eye(8, dtype=np.float32)
    fc = np.zeros((128, 8), np.float32)
    fc[:, 0] = 1e-5; fc[:, 1] = 1e-6; fc[:, 2] = 1.0
    c["fconst"] = fc
    return c


def _core_inputs(inputs, core, T, ST):
    i = inputs
    g = lambda k: np.ascontiguousarray(np.asarray(i[k]))
    d = dict(
        x=g("L0_attn_hidden_states")[core],
        wq=g("L0_attn_Wq"), wk=g("L0_attn_Wk"), wv=g("L0_attn_Wv"),
        wgo=g("L0_attn_Wg"), wgk1=g("L0_attn_Wgk1"), wgk2=g("L0_attn_Wgk2"),
        wo=g("L0_attn_Wo"), wgate=g("L0_attn_W_gate"), wup=g("L0_attn_W_up"),
        wdown=g("L0_attn_W_down"), mwg=g("L0_ffn_wg"),
        w1=g("L0_ffn_w1"), w2=g("L0_ffn_w2"), b2=g("L0_ffn_b2"),
        anw_l=np.ascontiguousarray(g("L0_attn_attn_norm_w").reshape(8, 128).T),
        mnw_l=np.ascontiguousarray(g("L0_attn_mlp_norm_w").reshape(8, 128).T),
        gnw_l=np.ascontiguousarray(np.tile(g("L0_attn_gnorm_w"), NH).reshape(8, 128).T),
        bgk2_l=np.ascontiguousarray(g("L0_attn_bgk2").reshape(4, 128).T),
        b1_l=np.ascontiguousarray(g("L0_ffn_b1").reshape(E, HMOE // 128, 128).transpose(0, 2, 1)),
        core_oh=np.eye(8, dtype=np.float32)[core][None, :],
        core_pm=(np.arange(8) < core).astype(np.float32)[:, None],
    )
    d.update(_consts(T, ST))
    return d


def kernel(**inputs):
    T = 8192
    nc = build_nc(T=T)
    in_maps = [_core_inputs(inputs, b, T, 512) for b in range(NCORE)]
    res = run_bass_kernel_spmd(nc, in_maps, core_ids=list(range(NCORE)))
    out = np.stack([r["out"] for r in res.results], axis=0)
    return out
